# version 09
# baseline (speedup 1.0000x reference)
import numpy as np
import ml_dtypes
import contextlib
from collections import defaultdict

import concourse.bass as bass
import concourse.mybir as mybir
from concourse.bass_utils import run_bass_kernel_spmd

F32 = mybir.dt.float32
BF16 = mybir.dt.bfloat16
AF = mybir.ActivationFunctionType
ALU = mybir.AluOpType

S = 2
T = 2048
D = 1024
DC = D // 128
H = 16
HD = 64
RD = 32
KVR = 256
KVC = KVR // 128
QR = 512
QC = QR // 128
DFF = 4096
FC = DFF // 128
G = 512
NG = T // G
NT = T // 128
EXPC = 11.0
MASKNEG = -60.0
SCALE_Q = float((HD + RD) ** -0.5)
LSC = float(2.0 ** 20)

_TAPS = ()
_DBG = set()

_UID = [0]


def _u(name):
    _UID[0] += 1
    return f"{name}_u{_UID[0]}"



class Sem:
    def __init__(self, es, nc, name):
        self.h = es.enter_context(nc.semaphore(name))
        self.n = 0


_DATAPATH = ("matmul", "transpose", "tensor", "activation", "memset", "memzero",
             "reciprocal", "stream_shuffle", "scalar_tensor_tensor", "dma_start",
             "reduce", "iota", "select", "ldweights", "copy", "square", "sign",
             "mul", "add", "bn_", "pool", "max", "sqrt")
_WRITE_KW = ("out", "accum_out", "out_max", "out_indices", "dest")


def _ap_name(x):
    t = getattr(x, "tensor", None)
    return getattr(t, "name", None) if t is not None else None


class _EngRec:

    def __init__(self, eng):
        self._e = eng
        self.last = None
        self._track = getattr(eng, "engine", None) not in (
            mybir.EngineType.PE, mybir.EngineType.SP)
        self._w = set()
        self._r = set()

    def __getattr__(self, name):
        a = getattr(self._e, name)
        if callable(a) and any(name.startswith(p) for p in _DATAPATH):
            def call(*args, __a=a, __n=name, **kw):
                if self._track:
                    is_dma = __n.startswith("dma_start")
                    wr, rd = [], []
                    for i, x in enumerate(args):
                        nm = _ap_name(x)
                        if nm is None:
                            continue
                        (wr if (i == 0 and not is_dma) else rd).append(nm)
                    for k, x in kw.items():
                        nm = _ap_name(x)
                        if nm is None:
                            continue
                        (wr if (k in _WRITE_KW and not is_dma) else rd).append(nm)
                    conflict = (any(n in self._w for n in rd)
                                or any(n in self._w or n in self._r for n in wr))
                    if conflict:
                        self._e.drain()
                        self._w.clear()
                        self._r.clear()
                    if not is_dma:
                        self._w.update(wr)
                    self._r.update(rd)
                r = __a(*args, **kw)
                if hasattr(r, "then_inc"):
                    self.last = r
                return r
            return call
        return a


class Phase:

    def __init__(self, nc, name=""):
        self.nc = nc
        self.name = name
        self.streams = defaultdict(list)

    def on(self, engine, fn):
        self.streams[engine].append(fn)

    def emit(self):
        if not self.streams:
            return
        with self.nc.Block() as blk:
            for engine in ("sync", "gpsimd", "tensor", "scalar", "vector"):
                fns = self.streams.get(engine)
                if not fns:
                    continue

                def runner(eng, fns=tuple(fns)):
                    rec = _EngRec(eng)
                    for f in fns:
                        f(rec)

                getattr(blk, engine)(runner)


class MK:
    def __init__(self, taps=()):
        self.taps = set(taps)
        self.nc = bass.Bass("TRN2", target_bir_lowering=False)
        self.es = contextlib.ExitStack()
        self.tap_d = {}
        self._declare_dram()
        self._alloc_persistent()

    def _declare_dram(self):
        nc = self.nc
        I, O, N = "ExternalInput", "ExternalOutput", "Internal"
        ap = lambda *a, **k: nc.dram_tensor(*a, **k).ap()
        self.d_x = ap("x", [S, T, D], F32, kind=I)
        self.d_wkv = ap("wkv", [D, KVR], F32, kind=I)
        self.d_gkv = ap("gkv", [128, KVC], F32, kind=I)
        self.d_wkdec = ap("wkdec", [KVR, D], F32, kind=I)
        self.d_wvdec = ap("wvdec", [KVR, D], F32, kind=I)
        self.d_wq = ap("wq", [D, QR], F32, kind=I)
        self.d_wqdec = ap("wqdec", [QR, D], F32, kind=I)
        self.d_wkrope = ap("wkrope", [D, H * RD], F32, kind=I)
        self.d_wqrope = ap("wqrope", [QR, H * RD], F32, kind=I)
        self.d_wo = ap("wo", [D, D], F32, kind=I)
        self.d_w1 = ap("w1", [DFF, D], F32, kind=I)
        self.d_b1 = ap("b1l", [128, FC], F32, kind=I)
        self.d_w2 = ap("w2", [D, DFF], F32, kind=I)
        self.d_b2 = ap("b2l", [128, DC], F32, kind=I)
        self.d_idf = ap("identf", [128, 128], F32, kind=I)
        self.d_idb = ap("identb", [128, 128], BF16, kind=I)
        self.d_cos = ap("cosT", [128, T], BF16, kind=I)
        self.d_ssin = ap("ssinT", [128, T], BF16, kind=I)
        self.d_mask = ap("maskw", [128, 896], BF16, kind=I)
        self.d_out = ap("out", [S, T, D], F32, kind=O)
        self.d_xT = ap("xT_scr", [S, DC, 128, T], F32, kind=N)
        self.d_h = ap("h_scr", [FC, 128, T], BF16, kind=N)
        self.d_wq1t = ap("wq1t_scr", [DC, 128, DFF], BF16, kind=N)
        self.d_wq2t = ap("wq2t_scr", [FC, 128, D], BF16, kind=N)

    def tap_dram(self, name, shape, dtype):
        if name not in self.tap_d:
            self.tap_d[name] = self.nc.dram_tensor(
                "tap_" + name, list(shape), dtype, kind="ExternalOutput").ap()
        return self.tap_d[name]

    def _alloc_persistent(self):
        nc, E = self.nc, self.es.enter_context
        self.identf = E(nc.sbuf_tensor(_u("c_identf"), [128, 128], F32))
        self.identb = E(nc.sbuf_tensor(_u("c_identb"), [128, 128], BF16))
        self.cosT = E(nc.sbuf_tensor(_u("c_cos"), [128, T], BF16))
        self.ssinT = E(nc.sbuf_tensor(_u("c_ssin"), [128, T], BF16))
        self.maskw = E(nc.sbuf_tensor(_u("c_maskw"), [128, 896], BF16))
        self.ones_col = E(nc.sbuf_tensor(_u("c_ones_col"), [128, 1], BF16))
        self.ones_colf = E(nc.sbuf_tensor(_u("c_ones_colf"), [128, 1], F32))
        self.ones_row = E(nc.sbuf_tensor(_u("c_ones_row"), [128, 128], BF16))
        self.ones_rowf = E(nc.sbuf_tensor(_u("c_ones_rowf"), [1, 128], F32))
        self.eps_ln = E(nc.sbuf_tensor(_u("c_eps_ln"), [1, 1], F32))
        self.eps_kv = E(nc.sbuf_tensor(_u("c_eps_kv"), [1, 1], F32))
        self.negc = E(nc.sbuf_tensor(_u("c_negc"), [128, 1], F32))
        self.b1l = E(nc.sbuf_tensor(_u("c_b1l"), [128, FC], F32))
        self.b2l = E(nc.sbuf_tensor(_u("c_b2l"), [128, DC], F32))
        self.gkv = E(nc.sbuf_tensor(_u("c_gkv"), [128, KVC], F32))
        self.XN = None
        self.XB = None

        self.sq = Sem(self.es, nc, "sq")
        self.sg = Sem(self.es, nc, "sg")
        self.spe = Sem(self.es, nc, "spe")
        self.sdv = Sem(self.es, nc, "sdv")
        self.sac = Sem(self.es, nc, "sac")
        self.sst = Sem(self.es, nc, "sst")
        self.sqp = [Sem(self.es, nc, "sqp0"), Sem(self.es, nc, "sqp1")]
        self.sgp2 = [Sem(self.es, nc, "sgp20"), Sem(self.es, nc, "sgp21")]
        self.sqq = [Sem(self.es, nc, "sqq0"), Sem(self.es, nc, "sqq1")]
        self.sgq = [Sem(self.es, nc, "sgq0"), Sem(self.es, nc, "sgq1")]

    @staticmethod
    def wload(e, dst_sb, src_dram, cols, sem, n=16):
        e.dma_start(out=dst_sb[:].rearrange("p (c r) -> p c r", r=cols),
                    in_=src_dram.rearrange("(c p) r -> p c r", p=128)) \
            .then_inc(sem, 16)

    def drain(self, ph, sync=False, gp=False, st=False):
        if sync:
            def fs(e, v=self.sq.n):
                e.wait_ge(self.sq.h, v)
            ph.on("sync", fs)
        if gp:
            def fg(e, v=self.sg.n, v0=self.sgp2[0].n, v1=self.sgp2[1].n,
                   q0=self.sgq[0].n, q1=self.sgq[1].n):
                e.wait_ge(self.sg.h, v)
                e.wait_ge(self.sgp2[0].h, v0)
                e.wait_ge(self.sgp2[1].h, v1)
                e.wait_ge(self.sgq[0].h, q0)
                e.wait_ge(self.sgq[1].h, q1)
            ph.on("gpsimd", fg)
        if st:
            def ft(e, v=self.sst.n):
                e.wait_ge(self.sst.h, v)
            ph.on("sync", ft)

    def p0(self):
        ph = Phase(self.nc, "p0")
        sq = self.sq

        def fsync(e):
            for dst, src in ((self.identf, self.d_idf), (self.identb, self.d_idb),
                             (self.cosT, self.d_cos), (self.ssinT, self.d_ssin),
                             (self.maskw, self.d_mask), (self.b1l, self.d_b1),
                             (self.b2l, self.d_b2), (self.gkv, self.d_gkv)):
                e.dma_start(out=dst[:], in_=src).then_inc(sq.h, 16)
        sq.n += 16 * 8
        ph.on("sync", fsync)

        def fvec(e):
            e.memset(self.ones_col[:], 1.0)
            e.memset(self.ones_colf[:], 1.0)
            e.memset(self.ones_row[:], 1.0)
            e.memset(self.ones_rowf[:], 1.0)
            e.memset(self.eps_ln[:], 1e-5)
            e.memset(self.eps_kv[:], 1e-6)
            e.memset(self.negc[:], -EXPC)
        ph.on("vector", fvec)
        self.drain(ph, sync=True)
        ph.emit()

    def quant_init(self, es):
        nc = self.nc
        E = es.enter_context
        q = self._q = {}
        q["wtmp"] = [E(nc.sbuf_tensor(_u(f"q_w{i}"), [128, 1024], F32, side="right")) for i in range(2)]
        q["acc"] = E(nc.sbuf_tensor(_u("q_acc"), [128, 2], F32, side="right"))
        q["r1"] = E(nc.sbuf_tensor(_u("q_r1"), [128, 1], F32, side="right"))
        q["thr"] = E(nc.sbuf_tensor(_u("q_thr"), [128, 2], F32, side="right"))
        q["nthr"] = E(nc.sbuf_tensor(_u("q_nthr"), [128, 2], F32, side="right"))
        q["throw"] = E(nc.sbuf_tensor(_u("q_throw"), [1, 2], F32, side="right"))
        q["bq"] = [E(nc.sbuf_tensor(_u(f"q_bq{i}"), [128, 1024], BF16, side="right")) for i in range(2)]
        q["qq"] = [E(nc.sbuf_tensor(_u(f"q_qq{i}"), [128, 1024], BF16, side="right")) for i in range(2)]
        q["stage"] = [E(nc.sbuf_tensor(_u(f"q_st{i}"), [128, 1024], BF16, side="right")) for i in range(2)]
        q["qtr"] = [E(nc.psum_tensor(_u(f"q_ptr{i}"), [128, 128], BF16, side="right")) for i in range(2)]
        units = []
        for ch in range(FC):
            units.append((0, ch, 0, D))
        for ch in range(DC):
            for qr in range(4):
                units.append((1, ch, qr * 1024, 1024))
        q["units"] = units
        q["ai"] = 0
        q["ci"] = 0
        q["thr_done"] = False
        q["amarks"] = {}
        q["cmarks"] = {}
        q["nA"] = len(units)

    def q_pump_A(self, ph, n):
        q = getattr(self, "_q", None)
        if not q or q["ai"] >= q["nA"]:
            return
        sdv = self.sdv
        i0, i1 = q["ai"], min(q["ai"] + n, q["nA"])
        q["ai"] = i1
        units, wtmp, acc, r1 = q["units"], q["wtmp"], q["acc"], q["r1"]
        loads = {}
        for i in range(i0, i1):
            self.sqq[i % 2].n += 16
            loads[i] = self.sqq[i % 2].n
            sdv.n += 1
            q["amarks"][i] = sdv.n

        def fsy(e, i0=i0, i1=i1, loads=dict(loads)):
            for i in range(i0, i1):
                mi, ch, off, cols = units[i]
                if i >= 2:
                    e.wait_ge(sdv.h, q["amarks"][i - 2])
                src = (self.d_w1 if mi == 0 else self.d_w2)[
                    ch * 128:(ch + 1) * 128, off:off + cols]
                e.dma_start(out=wtmp[i % 2][:, 0:cols], in_=src) \
                    .then_inc(self.sqq[i % 2].h, 16)
        ph.on("sync", fsy)

        def fdv(e, i0=i0, i1=i1, loads=dict(loads)):
            for i in range(i0, i1):
                mi, ch, off, cols = units[i]
                e.wait_ge(self.sqq[i % 2].h, loads[i])
                e.tensor_reduce(r1[:], wtmp[i % 2][:, 0:cols],
                                mybir.AxisListType.X, ALU.add,
                                apply_absolute_value=True)
                first = (i == 0) if mi == 0 else (i == FC)
                if first:
                    e.tensor_copy(acc[:, mi:mi + 1], r1[:]).then_inc(sdv.h, 1)
                else:
                    e.tensor_tensor(acc[:, mi:mi + 1], acc[:, mi:mi + 1], r1[:],
                                    ALU.add).then_inc(sdv.h, 1)
        ph.on("vector", fdv)

    def q_pump_thr(self, ph, es):
        q = getattr(self, "_q", None)
        if not q or q["thr_done"]:
            return
        assert q["ai"] >= q["nA"], "thr pumped before pass A complete"
        q["thr_done"] = True
        nc = self.nc
        spe, sdv = self.spe, self.sdv
        pgam = es.enter_context(nc.psum_tensor(_u("q_pgam"), [1, 2], F32))
        pthr = es.enter_context(nc.psum_tensor(_u("q_pthr"), [128, 2], F32))
        a_last = sdv.n
        spe.n += 1
        g_done = spe.n

        def fpe(e, a_last=max(q["amarks"].values())):
            e.wait_ge(sdv.h, a_last)
            e.matmul(pgam[:], self.ones_colf[:], q["acc"][:], start=True,
                     stop=True).then_inc(spe.h, 1)
        ph.on("tensor", fpe)
        sdv.n += 1
        t_done = sdv.n

        def fdv(e, g_done=g_done):
            e.wait_ge(spe.h, g_done)
            for mi, nn in ((0, FC * 128 * D), (1, DC * 128 * DFF)):
                e.tensor_scalar(q["throw"][:, mi:mi + 1], pgam[:, mi:mi + 1],
                                0.5 / nn, 0.5e-5, ALU.mult, ALU.add)
            e.last.then_inc(sdv.h, 1)
        ph.on("vector", fdv)
        spe.n += 1
        t2_done = spe.n

        def fpe2(e, t_done=t_done):
            e.wait_ge(sdv.h, t_done)
            e.matmul(pthr[:], self.ones_rowf[:], q["throw"][:], start=True,
                     stop=True).then_inc(spe.h, 1)
        ph.on("tensor", fpe2)
        sdv.n += 1
        q["thr_mark"] = sdv.n

        def fdv2(e, t2_done=t2_done):
            e.wait_ge(spe.h, t2_done)
            e.tensor_copy(q["thr"][:], pthr[:])
            e.tensor_scalar_mul(q["nthr"][:], pthr[:], -1.0).then_inc(sdv.h, 1)
        ph.on("vector", fdv2)

    def q_pump_C(self, ph, n):
        q = getattr(self, "_q", None)
        if not q or not q.get("thr_done") or q["ci"] >= q["nA"]:
            return
        spe, sdv, sg = self.spe, self.sdv, self.sg
        i0, i1 = q["ci"], min(q["ci"] + n, q["nA"])
        q["ci"] = i1
        units, wtmp = q["units"], q["wtmp"]
        bq, qq, stage, qtr = q["bq"], q["qq"], q["stage"], q["qtr"]
        loads = {}
        ktr = {}
        for i in range(i0, i1):
            mi, ch, off, cols = units[i]
            self.sqq[i % 2].n += 16
            loads[i] = self.sqq[i % 2].n
            sdv.n += 1
            q["cmarks"][("q", i)] = sdv.n
            for j in range(cols // 128):
                spe.n += 1
                q["cmarks"][("t", i, j)] = spe.n
                sdv.n += 1
                q["cmarks"][("e", i, j)] = sdv.n
            q["cmarks"][("t", i)] = spe.n
            q["cmarks"][("e", i)] = sdv.n
            self.sgq[i % 2].n += 16
            q["cmarks"][("s", i)] = self.sgq[i % 2].n

        def fsy(e, i0=i0, i1=i1, loads=dict(loads)):
            for i in range(i0, i1):
                mi, ch, off, cols = units[i]
                if i >= 2:
                    e.wait_ge(sdv.h, q["cmarks"][("q", i - 2)])
                src = (self.d_w1 if mi == 0 else self.d_w2)[
                    ch * 128:(ch + 1) * 128, off:off + cols]
                e.dma_start(out=wtmp[i % 2][:, 0:cols], in_=src) \
                    .then_inc(self.sqq[i % 2].h, 16)
        ph.on("sync", fsy)

        def fdv(e, i0=i0, i1=i1, loads=dict(loads)):
            for i in range(i0, i1):
                mi, ch, off, cols = units[i]
                e.wait_ge(self.sqq[i % 2].h, loads[i])
                if i == 0:
                    e.wait_ge(sdv.h, q["thr_mark"])
                if i >= 2:
                    e.wait_ge(self.sgq[i % 2].h,
                              q["cmarks"][("s", i - 2)])
                    e.wait_ge(spe.h, q["cmarks"][("t", i - 2)])
                e.tensor_single_scalar(bq[i % 2][:, 0:cols], wtmp[i % 2][:, 0:cols],
                                       q["nthr"][:, mi:mi + 1], ALU.is_lt)
                e.scalar_tensor_tensor(qq[i % 2][:, 0:cols], wtmp[i % 2][:, 0:cols],
                                       q["thr"][:, mi:mi + 1], bq[i % 2][:, 0:cols],
                                       ALU.is_gt, ALU.subtract).then_inc(sdv.h, 1)
                for j in range(cols // 128):
                    e.wait_ge(spe.h, q["cmarks"][("t", i, j)])
                    e.tensor_copy(stage[i % 2][:, j * 128:(j + 1) * 128],
                                  qtr[j % 2][:]).then_inc(sdv.h, 1)
        ph.on("vector", fdv)

        def fpe(e, i0=i0, i1=i1):
            flat = [(i, j) for i in range(i0, i1)
                    for j in range(units[i][3] // 128)]
            for k, (i, j) in enumerate(flat):
                if j == 0:
                    e.wait_ge(sdv.h, q["cmarks"][("q", i)])
                if k >= 2:
                    e.wait_ge(sdv.h, q["cmarks"][("e",) + flat[k - 2]])
                elif i0 > 0 and ("e", i0 - 1) in q["cmarks"]:
                    e.wait_ge(sdv.h, q["cmarks"][("e", i0 - 1)])
                e.transpose(qtr[j % 2][:], qq[i % 2][:, j * 128:(j + 1) * 128],
                            self.identb[:]).then_inc(spe.h, 1)
        ph.on("tensor", fpe)

        def fgp(e, i0=i0, i1=i1):
            for i in range(i0, i1):
                mi, ch, off, cols = units[i]
                e.wait_ge(sdv.h, q["cmarks"][("e", i)])
                if mi == 0:
                    out = self.d_wq1t[:, :, ch * 128:(ch + 1) * 128] \
                        .rearrange("c p j -> p c j")
                else:
                    f0 = off // 128
                    out = self.d_wq2t[f0:f0 + 8, :, ch * 128:(ch + 1) * 128] \
                        .rearrange("c p j -> p c j")
                e.dma_start(out=out,
                            in_=stage[i % 2][:, 0:cols]
                            .rearrange("p (c j) -> p c j", j=128)) \
                    .then_inc(self.sgq[i % 2].h, 16)
        ph.on("gpsimd", fgp)

    def ln_group(self, ph, est, tag, g, n_chunks, src_bf_slices, sq_src_slices,
                 eps_tile, repl_hilo, psig, psig2, prep, pbrep, stat, statb,
                 pe_extra_wait=None):
        nc = self.nc
        spe, sdv, sac = self.spe, self.sdv, self.sac
        nf = float(n_chunks * 128)

        dvw = sdv.n
        def fpe(e, src=list(src_bf_slices), sqs=list(sq_src_slices),
                w=pe_extra_wait, dvw=dvw):
            if w is not None:
                e.wait_ge(w[0].h, w[1])
            e.wait_ge(sdv.h, dvw)
            for c in range(len(src)):
                e.matmul(psig[:], self.ones_col[:], src[c],
                         start=(c == 0), stop=(c == len(src) - 1))
            for c in range(len(sqs)):
                e.matmul(psig2[:], self.ones_col[:], sqs[c],
                         start=(c == 0), stop=(c == len(sqs) - 1))
            e.last.then_inc(spe.h, 1)
        spe.n += 1
        stats_done = spe.n
        ph.on("tensor", fpe)

        def fdv1(e, stats_done=stats_done):
            e.wait_ge(spe.h, stats_done)
            e.tensor_scalar_mul(stat[:, 0:G], psig[:], 1.0 / nf)
            e.tensor_scalar_mul(stat[:, G:2 * G], psig2[:], 1.0 / nf)
            e.tensor_tensor(stat[:, 2 * G:3 * G], stat[:, 0:G], stat[:, 0:G], ALU.mult)
            e.tensor_tensor(stat[:, G:2 * G], stat[:, G:2 * G], stat[:, 2 * G:3 * G],
                            ALU.subtract).then_inc(sdv.h, 1)
        sdv.n += 1
        var_done = sdv.n
        ph.on("vector", fdv1)

        def fac(e, var_done=var_done):
            e.wait_ge(sdv.h, var_done)
            e.activation(stat[:, 3 * G:4 * G], stat[:, G:2 * G], AF.Ln, bias=eps_tile[:])
            e.activation(stat[:, 2 * G:3 * G], stat[:, 3 * G:4 * G], AF.Exp,
                         scale=-0.5).then_inc(sac.h, 1)
        sac.n += 1
        r_done = sac.n
        ph.on("scalar", fac)

        if not repl_hilo:
            def fdv2(e, r_done=r_done):
                e.wait_ge(sac.h, r_done)
                e.tensor_copy(statb[:, 0:G], stat[:, 2 * G:3 * G])
                e.tensor_tensor(stat[:, 3 * G:4 * G], stat[:, 0:G],
                                stat[:, 2 * G:3 * G], ALU.mult)
                e.tensor_scalar_mul(stat[:, 3 * G:4 * G], stat[:, 3 * G:4 * G], -1.0)
                e.tensor_copy(statb[:, G:2 * G], stat[:, 3 * G:4 * G]) \
                    .then_inc(sdv.h, 1)
            sdv.n += 1
            sb_done = sdv.n
            ph.on("vector", fdv2)

            def fpe2(e, sb_done=sb_done):
                e.wait_ge(sdv.h, sb_done)
                e.matmul(prep[:], self.ones_row[0:1, :], statb[:, 0:G],
                         start=True, stop=True)
                e.matmul(pbrep[:], self.ones_row[0:1, :], statb[:, G:2 * G],
                         start=True, stop=True).then_inc(spe.h, 1)
            spe.n += 1
            ph.on("tensor", fpe2)
        else:
            def fdv2(e, r_done=r_done):
                e.wait_ge(sac.h, r_done)
                e.tensor_tensor(stat[:, 3 * G:4 * G], stat[:, 0:G],
                                stat[:, 2 * G:3 * G], ALU.mult)
                e.tensor_scalar_mul(stat[:, 3 * G:4 * G], stat[:, 3 * G:4 * G], -1.0)
                e.tensor_copy(statb[:, 0:G], stat[:, 2 * G:3 * G])
                e.tensor_copy(statb[:, G:2 * G], stat[:, 3 * G:4 * G])
                e.tensor_tensor(stat[:, 4 * G:5 * G], stat[:, 2 * G:3 * G],
                                statb[:, 0:G], ALU.subtract)
                e.tensor_tensor(stat[:, 5 * G:6 * G], stat[:, 3 * G:4 * G],
                                statb[:, G:2 * G], ALU.subtract)
                e.tensor_copy(statb[:, 2 * G:3 * G], stat[:, 4 * G:5 * G])
                e.tensor_copy(statb[:, 3 * G:4 * G], stat[:, 5 * G:6 * G]) \
                    .then_inc(sdv.h, 1)
            sdv.n += 1
            sb_done = sdv.n
            ph.on("vector", fdv2)

            def fpe2(e, sb_done=sb_done):
                e.wait_ge(sdv.h, sb_done)
                e.matmul(prep[:], self.ones_row[0:1, :], statb[:, 0:G],
                         start=True, stop=False)
                e.matmul(prep[:], self.ones_row[0:1, :], statb[:, 2 * G:3 * G],
                         start=False, stop=True)
                e.matmul(pbrep[:], self.ones_row[0:1, :], statb[:, G:2 * G],
                         start=True, stop=False)
                e.matmul(pbrep[:], self.ones_row[0:1, :], statb[:, 3 * G:4 * G],
                         start=False, stop=True).then_inc(spe.h, 1)
            spe.n += 1
            ph.on("tensor", fpe2)
        return spe.n

    def ingest_ln1(self, s, XT, do_ln=True):
        nc = self.nc
        sq, sg, spe, sdv, sac = self.sq, self.sg, self.spe, self.sdv, self.sac
        with contextlib.ExitStack() as est:
            E = est.enter_context
            xtok = [E(nc.sbuf_tensor(_u(f"i_xtok{i}"), [128, D], F32)) for i in range(2)]
            pst_ = contextlib.ExitStack()
            ptr = [pst_.enter_context(nc.psum_tensor(_u(f"i_ptr{i}"), [128, 128], F32))
                   for i in range(2)]

            ph = Phase(nc, f"ingest{s}")
            NTT = NT
            for d in _DBG:
                if d.startswith("nt"):
                    NTT = int(d[2:])
            loads, trs, ev_dv, ev_ac = [], [], [], []
            for tt in range(NTT):
                self.sqp[tt % 2].n += 16
                loads.append(self.sqp[tt % 2].n)
                for c in range(DC):
                    spe.n += 1
                    trs.append(spe.n)
                    sdv.n += 1
                    ev_dv.append(sdv.n)
                    sac.n += 1
                    ev_ac.append(sac.n)

            def fsync(e):
                for tt in range(NTT):
                    if tt >= 2 and "loadsonly" not in _DBG:
                        e.wait_ge(spe.h, trs[(tt - 1) * DC - 1])
                    e.dma_start(out=xtok[tt % 2][:],
                                in_=self.d_x[s, tt * 128:(tt + 1) * 128, :]) \
                        .then_inc(self.sqp[tt % 2].h, 16)
                if "loadsonly" in _DBG:
                    e.wait_ge(self.sqp[0].h, self.sqp[0].n)
                    e.wait_ge(self.sqp[1].h, self.sqp[1].n)
            ph.on("sync", fsync)

            def fpe(e):
                if "loadsonly" in _DBG:
                    return
                k = 0
                for tt in range(NTT):
                    e.wait_ge(self.sqp[tt % 2].h, loads[tt])
                    for c in range(DC):
                        if k >= 2:
                            if "nodve" not in _DBG:
                                e.wait_ge(sdv.h, ev_dv[k - 2])
                            if "noact" not in _DBG:
                                e.wait_ge(sac.h, ev_ac[k - 2])
                        e.transpose(ptr[k % 2][:], xtok[tt % 2][:, c * 128:(c + 1) * 128],
                                    self.identf[:]).then_inc(spe.h, 1)
                        k += 1
            ph.on("tensor", fpe)

            def fdv(e):
                if "loadsonly" in _DBG or "nodve" in _DBG:
                    return
                k = 0
                for tt in range(NTT):
                    for c in range(DC):
                        e.wait_ge(spe.h, trs[k])
                        e.tensor_copy(XT[c][:, tt * 128:(tt + 1) * 128], ptr[k % 2][:]) \
                            .then_inc(sdv.h, 1)
                        k += 1
            ph.on("vector", fdv)

            def fac(e):
                if "loadsonly" in _DBG or "noact" in _DBG:
                    return
                k = 0
                for tt in range(NTT):
                    for c in range(DC):
                        e.wait_ge(sdv.h, ev_dv[k])
                        e.activation(self.XB[c][:, tt * 128:(tt + 1) * 128],
                                     ptr[k % 2][:], AF.Copy).then_inc(sac.h, 1)
                        k += 1
            ph.on("scalar", fac)

            if "nospill" not in _DBG:
                def fgp(e, w=ev_dv[-1]):
                    e.wait_ge(sdv.h, w)
                    for c in range(DC):
                        e.dma_start(out=self.d_xT[s, c], in_=XT[c][:]).then_inc(sg.h, 16)
                sg.n += 16 * DC
                ph.on("gpsimd", fgp)
            self.q_pump_A(ph, 64)
            self.drain(ph, gp=True)
            ph.emit()
            pst_.close()

            if not do_ln:
                return
            with contextlib.ExitStack() as lst:
                El = lst.enter_context
                sqb = El(nc.sbuf_tensor(_u("l1_sq"), [128, DC * G], BF16))
                stat = El(nc.sbuf_tensor(_u("l1_stat"), [1, 6 * G], F32))
                statb = El(nc.sbuf_tensor(_u("l1_statb"), [1, 4 * G], BF16))
                tmp = El(nc.sbuf_tensor(_u("l1_tmp"), [128, G], F32))
                psig = El(nc.psum_tensor(_u("l1_psig"), [1, G], F32))
                psig2 = El(nc.psum_tensor(_u("l1_psig2"), [1, G], F32))
                prep = El(nc.psum_tensor(_u("l1_prep"), [128, G], F32))
                pbrep = El(nc.psum_tensor(_u("l1_pbrep"), [128, G], F32))

                ph = Phase(nc, f"ln1_{s}")
                for g in range(NG):
                    gs = slice(g * G, (g + 1) * G)

                    def facq(e, gs=gs):
                        for c in range(DC):
                            e.activation(sqb[:, c * G:(c + 1) * G], self.XB[c][:, gs],
                                         AF.Square).then_inc(sac.h, 1)
                    sac.n += DC
                    sq_done = sac.n
                    ph.on("scalar", facq)

                    rep_done = self.ln_group(
                        ph, lst, f"ln1_{s}_{g}", g, DC,
                        [self.XB[c][:, gs] for c in range(DC)],
                        [sqb[:, c * G:(c + 1) * G] for c in range(DC)],
                        self.eps_ln, False, psig, psig2, prep, pbrep, stat, statb,
                        pe_extra_wait=(sac, sq_done))

                    def fap(e, gs=gs, rep_done=rep_done, XT=XT):
                        e.wait_ge(spe.h, rep_done)
                        for c in range(DC):
                            e.tensor_tensor(tmp[:], XT[c][:, gs], prep[:], ALU.mult)
                            e.tensor_tensor(self.XN[c][:, gs], tmp[:], pbrep[:],
                                            ALU.add)
                        e.last.then_inc(sdv.h, 1)
                    sdv.n += 1
                    ph.on("vector", fap)
                self.q_pump_thr(ph, lst)
                self.q_pump_C(ph, 14)
                ph.emit()

        self.tap_chunks(f"xn{s}", self.XN)
        self.tap_chunks(f"xt{s}", XT)

    def tap_chunks(self, name, chunks, width=T):
        if name not in self.taps:
            return
        nch = len(chunks)
        dt = chunks[0][:, 0:1].dtype
        d = self.tap_dram(name, [nch, 128, width], dt)
        ph = Phase(self.nc, f"tap_{name}")
        sst = self.sst

        def fs(e):
            for c in range(nch):
                e.dma_start(out=d[c], in_=chunks[c][:, 0:width]).then_inc(sst.h, 16)
        sst.n += 16 * nch
        ph.on("sync", fs)
        self.drain(ph, st=True)
        ph.emit()

    def tap_ap(self, name, ap_fn, shape, dtype):
        if name not in self.taps:
            return
        d = self.tap_dram(name, shape, dtype)
        ph = Phase(self.nc, f"tap_{name}")
        sst = self.sst
        pairs = ap_fn(d)

        def fs(e):
            for o, i in pairs:
                e.dma_start(out=o, in_=i).then_inc(sst.h, 16)
        sst.n += 16 * len(pairs)
        ph.on("sync", fs)
        self.drain(ph, st=True)
        ph.emit()

    def projections(self, s, est, QCb, KR, VT):
        kvst = contextlib.ExitStack()
        KVCb = [kvst.enter_context(
            self.nc.sbuf_tensor(_u(f"KVC{i}"), [128, T], BF16))
            for i in range(KVC)]
        """x_n -> kv_c (RMS+g), q_c (pre-scaled), k_rope (roped), k_content, V."""
        nc = self.nc
        sq, sg, spe, sdv, sac = self.sq, self.sg, self.spe, self.sdv, self.sac

        with contextlib.ExitStack() as wst:
            Ew = wst.enter_context
            wq = Ew(nc.sbuf_tensor(_u("p_wq"), [128, DC * QR], BF16))
            wkv = Ew(nc.sbuf_tensor(_u("p_wkv"), [128, DC * KVR], BF16))
            kvsq = Ew(nc.sbuf_tensor(_u("p_kvsq"), [128, KVC * G], BF16))
            preps = Ew(nc.sbuf_tensor(_u("p_preps"), [128, G], F32))
            stat = Ew(nc.sbuf_tensor(_u("p_stat"), [1, 4 * G], F32))
            statb = Ew(nc.sbuf_tensor(_u("p_statb"), [1, G], BF16))
            pkv = [Ew(nc.psum_tensor(_u(f"p_pkv{i}"), [128, G], F32)) for i in range(KVC)]
            pq = [Ew(nc.psum_tensor(_u(f"p_pq{i}"), [128, G], F32)) for i in range(2)]
            psig = Ew(nc.psum_tensor(_u("p_psig"), [1, G], F32))
            prep = Ew(nc.psum_tensor(_u("p_prep"), [128, G], F32))

            ph = Phase(nc, f"proj_kvq_{s}")



            def fgp(e):
                self.wload(e, wq, self.d_wq, QR, sg.h)
                self.wload(e, wkv, self.d_wkv, KVR, sg.h)
            sg.n += 32
            wkv_ld = sg.n
            ph.on("gpsimd", fgp)

            def fdvw(e, wqf_ld=wkv_ld):
                e.wait_ge(sg.h, wqf_ld)
                e.tensor_scalar_mul(wq[:], wq[:], SCALE_Q).then_inc(sdv.h, 1)
            sdv.n += 1
            wq_rdy = sdv.n
            ph.on("vector", fdvw)

            for g in range(NG):
                gs = slice(g * G, (g + 1) * G)
                def fpe(e, gs=gs, g=g, wkv_ld=wkv_ld, wq_rdy=wq_rdy, dvp=sdv.n):
                    e.wait_ge(sg.h, wkv_ld)
                    e.wait_ge(sdv.h, dvp)
                    for kc in range(KVC):
                        for k in range(DC):
                            e.matmul(pkv[kc][:],
                                     wkv[:, k * KVR + kc * 128:k * KVR + (kc + 1) * 128],
                                     self.XN[k][:, gs], start=(k == 0), stop=(k == DC - 1))
                    e.last.then_inc(spe.h, 1)
                spe.n += 1
                kv_done = spe.n
                ph.on("tensor", fpe)

                def facs(e, kv_done=kv_done):
                    e.wait_ge(spe.h, kv_done)
                    for kc in range(KVC):
                        e.activation(kvsq[:, kc * G:(kc + 1) * G], pkv[kc][:],
                                     AF.Square).then_inc(sac.h, 1)
                sac.n += KVC
                kvsq_done = sac.n
                ph.on("scalar", facs)

                def fpe_s(e, kvsq_done=kvsq_done, dvp=sdv.n):
                    e.wait_ge(sac.h, kvsq_done)
                    e.wait_ge(sdv.h, dvp)
                    for kc in range(KVC):
                        e.matmul(psig[:], self.ones_col[:], kvsq[:, kc * G:(kc + 1) * G],
                                 start=(kc == 0), stop=(kc == KVC - 1))
                    e.last.then_inc(spe.h, 1)
                spe.n += 1
                sig_done = spe.n
                ph.on("tensor", fpe_s)

                def fdv_r(e, sig_done=sig_done):
                    e.wait_ge(spe.h, sig_done)
                    e.tensor_scalar_mul(stat[:, 0:G], psig[:], 1.0 / KVR) \
                        .then_inc(sdv.h, 1)
                sdv.n += 1
                ms_done = sdv.n
                ph.on("vector", fdv_r)

                def fac_r(e, ms_done=ms_done):
                    e.wait_ge(sdv.h, ms_done)
                    e.activation(stat[:, G:2 * G], stat[:, 0:G], AF.Ln,
                                 bias=self.eps_kv[:])
                    e.activation(stat[:, 2 * G:3 * G], stat[:, G:2 * G], AF.Exp,
                                 scale=-0.5).then_inc(sac.h, 1)
                sac.n += 1
                r_done = sac.n
                ph.on("scalar", fac_r)

                def fdv_c(e, r_done=r_done):
                    e.wait_ge(sac.h, r_done)
                    e.tensor_copy(statb[:, 0:G], stat[:, 2 * G:3 * G]).then_inc(sdv.h, 1)
                sdv.n += 1
                rb_done = sdv.n
                ph.on("vector", fdv_c)

                def fpe_r(e, rb_done=rb_done):
                    e.wait_ge(sdv.h, rb_done)
                    e.matmul(prep[:], self.ones_row[0:1, :], statb[:, 0:G],
                             start=True, stop=True).then_inc(spe.h, 1)
                spe.n += 1
                rep_done = spe.n
                ph.on("tensor", fpe_r)

                def fdv_a(e, gs=gs, rep_done=rep_done):
                    e.wait_ge(spe.h, rep_done)
                    e.tensor_copy(preps[:], prep[:])
                    for kc in range(KVC):
                        e.scalar_tensor_tensor(KVCb[kc][:, gs], pkv[kc][:],
                                               self.gkv[:, kc:kc + 1], preps[:],
                                               ALU.mult, ALU.mult)
                    e.last.then_inc(sdv.h, 1)
                sdv.n += 1
                ph.on("vector", fdv_a)

                qdv_marks = [sdv.n] * 2
                for qc in range(QC):
                    def fpe_q(e, gs=gs, qc=qc, wq_rdy=wq_rdy, dvp=qdv_marks[qc]):
                        e.wait_ge(sdv.h, max(wq_rdy, dvp))
                        pb = pq[qc % 2]
                        for k in range(DC):
                            e.matmul(pb[:],
                                     wq[:, k * QR + qc * 128:k * QR + (qc + 1) * 128],
                                     self.XN[k][:, gs], start=(k == 0), stop=(k == DC - 1))
                        e.last.then_inc(spe.h, 1)
                    spe.n += 1
                    q_done = spe.n
                    ph.on("tensor", fpe_q)

                    def fdv_q(e, gs=gs, qc=qc, q_done=q_done):
                        e.wait_ge(spe.h, q_done)
                        e.tensor_copy(QCb[qc][:, gs], pq[qc % 2][:]).then_inc(sdv.h, 1)
                    sdv.n += 1
                    qdv_marks.append(sdv.n)
                    ph.on("vector", fdv_q)
            self.q_pump_C(ph, 18)
            ph.emit()

        with contextlib.ExitStack() as wst:
            Ew = wst.enter_context
            wkr = Ew(nc.sbuf_tensor(_u("p_wkr"), [128, DC * H * RD], BF16))
            pkr = [Ew(nc.psum_tensor(_u(f"p_pkr{i}"), [128, G], F32)) for i in range(2)]
            rt1 = Ew(nc.sbuf_tensor(_u("p_rt1"), [128, G], BF16))

            ph = Phase(nc, f"proj_krope_{s}")

            def fgp(e):
                self.wload(e, wkr, self.d_wkrope, H * RD, sg.h)
            sg.n += 16
            wkr_ld = sg.n
            ph.on("gpsimd", fgp)

            RC = H * RD // 128
            for g in range(NG):
                gs = slice(g * G, (g + 1) * G)
                for rc in range(RC):
                    def fpe(e, gs=gs, rc=rc, wkr_ld=wkr_ld, dvp=sdv.n):
                        e.wait_ge(sg.h, wkr_ld)
                        e.wait_ge(sdv.h, dvp)
                        for k in range(DC):
                            e.matmul(pkr[rc % 2][:],
                                     wkr[:, k * 512 + rc * 128:k * 512 + (rc + 1) * 128],
                                     self.XN[k][:, gs], start=(k == 0), stop=(k == DC - 1))
                        e.last.then_inc(spe.h, 1)
                    spe.n += 1
                    mm_done = spe.n
                    ph.on("tensor", fpe)

                    def fdv(e, gs=gs, rc=rc, mm_done=mm_done):
                        e.wait_ge(spe.h, mm_done)
                        p = pkr[rc % 2]
                        ks_ = KR[rc][:, gs]
                        e.tensor_tensor(rt1[:], p[:], self.ssinT[:, gs], ALU.mult)
                        e.stream_shuffle(ks_, rt1[:], mask=[(i + 16) % 32
                                                            for i in range(32)])
                        e.tensor_tensor(rt1[:], p[:], self.cosT[:, gs], ALU.mult)
                        e.tensor_tensor(ks_, ks_, rt1[:], ALU.add).then_inc(sdv.h, 1)
                    sdv.n += 1
                    ph.on("vector", fdv)
            self.q_pump_C(ph, 16)
            ph.emit()

        with contextlib.ExitStack() as wst:
            Ew = wst.enter_context
            wkd = Ew(nc.sbuf_tensor(_u("p_wkd"), [128, KVC * D], BF16))
            wvd = Ew(nc.sbuf_tensor(_u("p_wvd"), [128, KVC * D], BF16))
            pkd = [Ew(nc.psum_tensor(_u(f"p_pkd{i}"), [128, G], F32)) for i in range(2)]
            pv = Ew(nc.psum_tensor(_u("p_pv"), [128, D], F32))

            ph = Phase(nc, f"proj_dec_{s}")

            def fgp(e):
                self.wload(e, wkd, self.d_wkdec, D, sg.h)
                self.wload(e, wvd, self.d_wvdec, D, sg.h)
            sg.n += 32
            wd_ld = sg.n
            ph.on("gpsimd", fgp)

            for g in range(NG):
                gs = slice(g * G, (g + 1) * G)
                for c in range(DC):
                    def fpe(e, gs=gs, c=c, wd_ld=wd_ld, dvp=sdv.n):
                        e.wait_ge(sg.h, wd_ld)
                        e.wait_ge(sdv.h, dvp)
                        for kc in range(KVC):
                            e.matmul(pkd[c % 2][:],
                                     wkd[:, kc * D + c * 128:kc * D + (c + 1) * 128],
                                     KVCb[kc][:, gs], start=(kc == 0), stop=(kc == KVC - 1))
                        e.last.then_inc(spe.h, 1)
                    spe.n += 1
                    mm_done = spe.n
                    ph.on("tensor", fpe)

                    def fdv(e, gs=gs, c=c, mm_done=mm_done):
                        e.wait_ge(spe.h, mm_done)
                        e.tensor_copy(self.XB[c][:, gs], pkd[c % 2][:]).then_inc(sdv.h, 1)
                    sdv.n += 1
                    ph.on("vector", fdv)

            for tt in range(NT):
                def fpe(e, tt=tt, wd_ld=wd_ld, dvp=sdv.n):
                    e.wait_ge(sg.h, wd_ld)
                    e.wait_ge(sdv.h, dvp)
                    for half in range(2):
                        for kc in range(KVC):
                            e.matmul(pv[:, half * 512:(half + 1) * 512],
                                     KVCb[kc][:, tt * 128:(tt + 1) * 128],
                                     wvd[:, kc * D + half * 512:kc * D + (half + 1) * 512],
                                     start=(kc == 0), stop=(kc == KVC - 1))
                    e.last.then_inc(spe.h, 1)
                spe.n += 1
                mm_done = spe.n
                ph.on("tensor", fpe)

                def fdv(e, tt=tt, mm_done=mm_done):
                    e.wait_ge(spe.h, mm_done)
                    e.tensor_copy(
                        VT[:, tt * (H * 65):(tt + 1) * (H * 65)]
                        .rearrange("p (h j) -> p h j", j=65)[:, :, 0:64],
                        pv[:].rearrange("p (h j) -> p h j", j=64)) \
                        .then_inc(sdv.h, 1)
                sdv.n += 1
                ph.on("vector", fdv)

            def fdv_ones(e):
                e.memset(VT[:].rearrange("p (th j) -> p th j", j=65)[:, :, 64:65], 1.0)
                e.last.then_inc(sdv.h, 1)
            sdv.n += 1
            ph.on("vector", fdv_ones)
            self.q_pump_C(ph, 64)
            self.drain(ph, gp=True)
            ph.emit()
        kvst.close()
        if getattr(self, "_q", None) and self._q["ci"] >= self._q["nA"]:
            self._q = None
            self._qes.close()

        self.tap_chunks(f"kvc{s}", KVCb)
        self.tap_chunks(f"qc{s}", QCb)
        self.tap_chunks(f"kr{s}", KR)
        self.tap_chunks(f"kc{s}", self.XB)
        self.tap_chunks(f"vt{s}", [VT], width=NT * H * 65)

    def attention(self, s, est, QCb, KR, VT):
        nc = self.nc
        sq, sg, spe, sdv, sac = self.sq, self.sg, self.spe, self.sdv, self.sac
        E = est.enter_context
        wqd = E(nc.sbuf_tensor(_u("a_wqd"), [128, QC * D], BF16))
        wqr = E(nc.sbuf_tensor(_u("a_wqr"), [128, QC * H * RD], BF16))
        QCONT = E(nc.sbuf_tensor(_u("a_qcont"), [128, DC * G], BF16))
        QROPE = E(nc.sbuf_tensor(_u("a_qrope"), [128, 4 * G], BF16))
        PT = [E(nc.sbuf_tensor(_u(f"a_pt{i}"), [128, 4 * G], BF16)) for i in range(2)]
        OST = E(nc.sbuf_tensor(_u("a_ost"), [128, 8 * G], BF16))
        RL = E(nc.sbuf_tensor(_u("a_rl"), [128, 2 * G], BF16))
        RLS = E(nc.sbuf_tensor(_u("a_rls"), [128, 2 * G], BF16))
        RLB = E(nc.sbuf_tensor(_u("a_rlb"), [128, 2 * G], BF16))
        rt1 = E(nc.sbuf_tensor(_u("a_rt1"), [128, G], BF16))
        SCP = E(nc.psum_tensor(_u("a_scp"), [128, 4 * G], F32))
        PO = [E(nc.psum_tensor(_u(f"a_po{i}"), [65, G], F32)) for i in range(2)]
        PRL = [E(nc.psum_tensor(_u(f"a_prl{i}"), [64, G], F32)) for i in range(2)]

        ph = Phase(nc, f"attn_{s}")
        M = {}
        pe_ops, dv_ops, ac_ops, gp_ops = [], [], [], []

        def fgp0(e):
            self.wload(e, wqd, self.d_wqdec, D, sg.h)
            self.wload(e, wqr, self.d_wqrope, H * RD, sg.h)
        sg.n += 32
        wq_ld = sg.n
        ph.on("gpsimd", fgp0)

        last_oev = [0, 0]
        prev_odma = 0
        pend_prl = []
        wslice = [0, 0, 0]
        sc_slot_ev = [0] * 4
        prev_win_exp = 0
        mask_shuf = [(i + 16) % 32 for i in range(32)]

        for g in range(NG):
            gs = slice(g * G, (g + 1) * G)
            KT, NP = 4 * (g + 1), 2 * (g + 1)

            qev_start = sdv.n
            for c in range(12):
                slot = c % 4
                par, half = slot % 2, slot // 2
                spe.n += 1
                M[("qp", g, c)] = spe.n
                pe_ops.append(("qproj", g, gs, c, par, half,
                               prev_win_exp if c == 0 else 0,
                               sc_slot_ev[slot], wq_ld if (g == 0 and c == 0) else 0))
                sdv.n += 1
                M[("qe", g, c)] = sdv.n
                sc_slot_ev[slot] = sdv.n
                dv_ops.append(("qevac", g, gs, c, par, half))
            qev_all = sdv.n

            for hp in range(8):
                h0, h1 = 2 * hp, 2 * hp + 1
                for pi in range(NP):
                    sac.n += 1
                    M[("exp", g, h0, pi)] = sac.n
                    M[("exp", g, h1, pi)] = sac.n
                prev_win_exp = sac.n
                for pi in range(NP):
                    if pi * 2 >= 4 * g:
                        for h in (h0, h1):
                            sdv.n += 1
                            M[("mask", g, h, pi)] = sdv.n
                            dv_ops.append(("mask", g, h, pi))
                for h in (h0, h1):
                    sac.n += 2
                    M[("rlb", g, h)] = sac.n
                for h in (h0, h1):
                    sdv.n += 1
                    M[("oev", g, h)] = sdv.n
                    dv_ops.append(("oev", g, h,
                                   prev_odma if (h == 1 and g > 0) else 0))
                po_wait = (last_oev[0], last_oev[1])
                for pi in range(NP):
                    for h in (h0, h1):
                        spe.n += 1
                        M[("qk", g, h, pi)] = spe.n
                        pe_ops.append(("qk", g, gs, h, pi,
                                       qev_all if pi == 0 else 0))
                    if pi == 0 and pend_prl:
                        for op in pend_prl:
                            spe.n += 1
                            M[("prl",) + op[1:3]] = spe.n
                            pe_ops.append(op)
                        pend_prl.clear()
                    if pi >= 1:
                        for h in (h0, h1):
                            spe.n += 1
                            M[("pv", g, h, pi - 1)] = spe.n
                            pe_ops.append(("pv", g, h, pi - 1, pi - 1 == 0,
                                           False, po_wait[h % 2]))
                for h in (h0, h1):
                    spe.n += 1
                    M[("pv", g, h, NP - 1)] = spe.n
                    pe_ops.append(("pv", g, h, NP - 1, NP == 1, True,
                                   po_wait[h % 2] if NP == 1 else 0))
                for h in (h0, h1):
                    pend_prl.append(("prl", g, h, po_wait[h % 2]))
                    last_oev[h % 2] = M[("oev", g, h)]
            gp_ops.append(("odma", g, sdv.n, list(pend_prl)))
            for op in pend_prl:
                spe.n += 1
                M[("prl",) + op[1:3]] = spe.n
                pe_ops.append(op)
            pend_prl.clear()
            sg.n += 16 * 8
            prev_odma = sg.n
            ph.on("tensor", lambda e, a=wslice[0], b=len(pe_ops): _fpe(e, a, b))
            ph.on("vector", lambda e, a=wslice[1], b=len(dv_ops): _fdv(e, a, b))
            ph.on("scalar", lambda e, g=g: _fac(e, g))
            ph.on("gpsimd", lambda e, a=wslice[2], b=len(gp_ops): _fgp(e, a, b))
            wslice[0], wslice[1], wslice[2] = len(pe_ops), len(dv_ops), len(gp_ops)

        def _fpe(e, a, b):
            for op in pe_ops[a:b]:
                if op[0] == "qproj":
                    _, g, gs, c, par, half, w_pexp, w_ev, w_ld = op
                    if w_ld:
                        e.wait_ge(sg.h, w_ld)
                    if w_pexp:
                        e.wait_ge(sac.h, w_pexp)
                    if w_ev:
                        e.wait_ge(sdv.h, w_ev)
                    dst = SCP[:, (par * 2 + half) * G:(par * 2 + half + 1) * G]
                    if c < 8:
                        for qc in range(QC):
                            e.matmul(dst, wqd[:, qc * D + c * 128:qc * D + (c + 1) * 128],
                                     QCb[qc][:, gs], start=(qc == 0), stop=(qc == QC - 1))
                    else:
                        rc = c - 8
                        for qc in range(QC):
                            e.matmul(dst, wqr[:, qc * 512 + rc * 128:qc * 512 + (rc + 1) * 128],
                                     QCb[qc][:, gs], start=(qc == 0), stop=(qc == QC - 1))
                    e.last.then_inc(spe.h, 1)
                elif op[0] == "qk":
                    _, g, gs, h, pi, w_qev = op
                    hc, hpp = h // 2, 64 * (h % 2)
                    rc4, rp = h // 4, 32 * (h % 4)
                    if w_qev:
                        e.wait_ge(sdv.h, w_qev)
                    if pi >= 1:
                        e.wait_ge(sac.h, M[("exp", g, h, pi - 1)])
                    for j in range(2):
                        kt = pi * 2 + j
                        ks = slice(kt * 128, (kt + 1) * 128)
                        dst = SCP[:, (h % 2) * 2 * G + j * G:
                                  (h % 2) * 2 * G + (j + 1) * G]
                        e.matmul(dst, self.XB[hc][hpp:hpp + 64, ks],
                                 QCONT[hpp:hpp + 64, hc * G:(hc + 1) * G],
                                 start=True, stop=False)
                        e.matmul(dst, KR[rc4][rp:rp + 32, ks],
                                 QROPE[rp:rp + 32, rc4 * G:(rc4 + 1) * G],
                                 start=False, stop=True,
                                 tile_position=(rp, 0) if rp == 96 else None)
                    e.last.then_inc(spe.h, 1)
                elif op[0] == "pv":
                    _, g, h, pi, first, last_, w_oev = op
                    e.wait_ge(sac.h, M[("exp", g, h, pi)])
                    if first and w_oev:
                        e.wait_ge(sdv.h, w_oev)
                    pt = PT[pi % 2]
                    for j in range(2):
                        kt = pi * 2 + j
                        vs = VT[:, kt * (H * 65) + h * 65:kt * (H * 65) + (h + 1) * 65]
                        e.matmul(PO[h % 2][:], vs,
                                 pt[:, (h % 2) * 2 * G + j * G:
                                    (h % 2) * 2 * G + (j + 1) * G],
                                 start=(first and j == 0), stop=(last_ and j == 1))
                    e.last.then_inc(spe.h, 1)
                elif op[0] == "prl":
                    _, g, h, w_oev = op
                    e.wait_ge(sac.h, M[("rlb", g, h)])
                    if w_oev:
                        e.wait_ge(sdv.h, w_oev)
                    e.matmul(PRL[h % 2][:], self.ones_row[64:65, 0:64],
                             RLB[64:65, (h % 2) * G:(h % 2 + 1) * G],
                             start=True, stop=True).then_inc(spe.h, 1)
        def _fdv(e, a, b):
            for op in dv_ops[a:b]:
                if op[0] == "qevac":
                    _, g, gs, c, par, half = op
                    e.wait_ge(spe.h, M[("qp", g, c)])
                    src = SCP[:, (par * 2 + half) * G:(par * 2 + half + 1) * G]
                    if c < 8:
                        e.tensor_copy(QCONT[:, c * G:(c + 1) * G], src) \
                            .then_inc(sdv.h, 1)
                    else:
                        rc = c - 8
                        qs = QROPE[:, rc * G:(rc + 1) * G]
                        e.tensor_tensor(rt1[:], src, self.ssinT[:, gs], ALU.mult)
                        e.stream_shuffle(qs, rt1[:], mask=mask_shuf)
                        e.tensor_tensor(rt1[:], src, self.cosT[:, gs], ALU.mult)
                        e.tensor_tensor(qs, qs, rt1[:], ALU.add).then_inc(sdv.h, 1)
                elif op[0] == "mask":
                    _, g, h, pi = op
                    e.wait_ge(spe.h, M[("qk", g, h, pi)])
                    for j in range(2):
                        kt = pi * 2 + j
                        off = 384 - 128 * (kt - 4 * g)
                        sl = slice((h % 2) * 2 * G + j * G,
                                   (h % 2) * 2 * G + (j + 1) * G)
                        e.tensor_tensor(SCP[:, sl], SCP[:, sl],
                                        self.maskw[:, off:off + G], ALU.add)
                    e.last.then_inc(sdv.h, 1)
                elif op[0] == "oev":
                    _, g, h, w_odma = op
                    sel = h % 2
                    if w_odma:
                        e.wait_ge(sg.h, w_odma)
                    e.wait_ge(spe.h, M[("prl", g, h)])
                    if h % 2 == 0:
                        dst = self.XN[h // 2][0:64, g * G:(g + 1) * G]
                    else:
                        dst = OST[0:64, (h // 2) * G:(h // 2 + 1) * G]
                    ls = slice(sel * G, (sel + 1) * G)
                    e.tensor_copy(RLS[0:64, ls], PRL[sel][:])
                    e.tensor_tensor(dst, PO[sel][0:64, :], RLS[0:64, ls],
                                    ALU.mult).then_inc(sdv.h, 1)
        def _fac(e, g):
            if True:
                NP = 2 * (g + 1)
                for hp in range(8):
                    h0, h1 = 2 * hp, 2 * hp + 1
                    for pi in range(NP):
                        diag = (pi * 2 >= 4 * g)
                        if diag:
                            e.wait_ge(sdv.h, M[("mask", g, h1, pi)])
                        else:
                            e.wait_ge(spe.h, M[("qk", g, h1, pi)])
                        if pi >= 2:
                            e.wait_ge(spe.h, M[("pv", g, h1, pi - 2)])
                        e.activation(PT[pi % 2][:], SCP[:], AF.Exp,
                                     bias=self.negc[:]).then_inc(sac.h, 1)
                    for h in (h0, h1):
                        sel = h % 2
                        ls = slice(sel * G, (sel + 1) * G)
                        e.wait_ge(spe.h, M[("pv", g, h, NP - 1)])
                        e.activation(RL[64:65, ls], PO[sel][64:65, :], AF.Ln) \
                            .then_inc(sac.h, 1)
                        e.activation(RLB[64:65, ls], RL[64:65, ls], AF.Exp,
                                     scale=-1.0).then_inc(sac.h, 1)
        def _fgp(e, a, b):
            for op in gp_ops[a:b]:
                _, g, w_dv = op[:3]
                e.wait_ge(sdv.h, w_dv)
                for c in range(DC):
                    e.dma_start(out=self.XN[c][64:128, g * G:(g + 1) * G],
                                in_=OST[0:64, c * G:(c + 1) * G]) \
                        .then_inc(sg.h, 16)
        self.drain(ph, gp=True)
        ph.emit()

        self.tap_chunks(f"ot{s}", self.XN)

    def ln_stage(self, M, tag, src_bf, src_sq, n_chunks, eps_tile, hilo,
                 psig, psig2, prep, pbrep, stat, statb, pe_list, dv_list, ac_list,
                 w_src_sac, w_psig_free, w_prep_free):
        spe, sdv, sac = self.spe, self.sdv, self.sac
        nf = float(n_chunks * 128)
        spe.n += 1
        M[("stats", tag)] = spe.n
        pe_list.append(("stats", tag, src_bf, src_sq, w_src_sac, w_psig_free,
                        psig, psig2))
        sdv.n += 1
        M[("fdv1", tag)] = sdv.n
        dv_list.append(("fdv1", tag, nf, psig, psig2, stat))
        sac.n += 1
        M[("rstd", tag)] = sac.n
        ac_list.append(("rstd", tag, eps_tile, stat))
        sdv.n += 1
        M[("statb", tag)] = sdv.n
        dv_list.append(("statb", tag, hilo, stat, statb))

    def ln_stage_repl(self, M, tag, hilo, statb, prep, pbrep, pe_list, w_prep_free):
        spe = self.spe
        spe.n += 1
        M[("repl", tag)] = spe.n
        pe_list.append(("repl", tag, hilo, statb, prep, pbrep, w_prep_free))

    def emit_ln_pe(self, e, op, M):
        spe, sdv, sac = self.spe, self.sdv, self.sac
        if op[0] == "stats":
            _, tag, src_bf, src_sq, w_sac, w_free, psig, psig2 = op
            e.wait_ge(sac.h, w_sac if not callable(w_sac) else w_sac())
            wf = w_free() if callable(w_free) else w_free
            if wf:
                e.wait_ge(sdv.h, wf)
            n = len(src_bf)
            for c in range(n):
                e.matmul(psig[:], self.ones_col[:], src_bf[c],
                         start=(c == 0), stop=(c == n - 1))
            for c in range(n):
                e.matmul(psig2[:], self.ones_col[:], src_sq[c],
                         start=(c == 0), stop=(c == n - 1))
            e.last.then_inc(spe.h, 1)
        elif op[0] == "repl":
            _, tag, hilo, statb, prep, pbrep, w_prep = op
            e.wait_ge(sdv.h, M[("statb", tag)])
            wp = w_prep() if callable(w_prep) else w_prep
            if wp:
                e.wait_ge(sdv.h, wp)
            if hilo:
                e.matmul(prep[:], self.ones_row[0:1, :], statb[:, 0:G],
                         start=True, stop=False)
                e.matmul(prep[:], self.ones_row[0:1, :], statb[:, 2 * G:3 * G],
                         start=False, stop=True)
                e.matmul(pbrep[:], self.ones_row[0:1, :], statb[:, G:2 * G],
                         start=True, stop=False)
                e.matmul(pbrep[:], self.ones_row[0:1, :], statb[:, 3 * G:4 * G],
                         start=False, stop=True).then_inc(spe.h, 1)
            else:
                e.matmul(prep[:], self.ones_row[0:1, :], statb[:, 0:G],
                         start=True, stop=True)
                e.matmul(pbrep[:], self.ones_row[0:1, :], statb[:, G:2 * G],
                         start=True, stop=True).then_inc(spe.h, 1)

    def emit_ln_dv(self, e, op, M):
        spe, sdv, sac = self.spe, self.sdv, self.sac
        if op[0] == "fdv1":
            _, tag, nf, psig, psig2, stat = op
            e.wait_ge(spe.h, M[("stats", tag)])
            e.tensor_scalar_mul(stat[:, 0:G], psig[:], 1.0 / nf)
            e.tensor_scalar_mul(stat[:, G:2 * G], psig2[:], 1.0 / nf)
            e.tensor_tensor(stat[:, 2 * G:3 * G], stat[:, 0:G], stat[:, 0:G],
                            ALU.mult)
            e.tensor_tensor(stat[:, G:2 * G], stat[:, G:2 * G],
                            stat[:, 2 * G:3 * G], ALU.subtract).then_inc(sdv.h, 1)
        elif op[0] == "statb":
            _, tag, hilo, stat, statb = op
            e.wait_ge(sac.h, M[("rstd", tag)])
            if hilo:
                e.tensor_tensor(stat[:, 3 * G:4 * G], stat[:, 0:G],
                                stat[:, 2 * G:3 * G], ALU.mult)
                e.tensor_scalar_mul(stat[:, 3 * G:4 * G], stat[:, 3 * G:4 * G], -1.0)
                e.tensor_copy(statb[:, 0:G], stat[:, 2 * G:3 * G])
                e.tensor_copy(statb[:, G:2 * G], stat[:, 3 * G:4 * G])
                e.tensor_tensor(stat[:, 4 * G:5 * G], stat[:, 2 * G:3 * G],
                                statb[:, 0:G], ALU.subtract)
                e.tensor_tensor(stat[:, 5 * G:6 * G], stat[:, 3 * G:4 * G],
                                statb[:, G:2 * G], ALU.subtract)
                e.tensor_copy(statb[:, 2 * G:3 * G], stat[:, 4 * G:5 * G])
                e.tensor_copy(statb[:, 3 * G:4 * G], stat[:, 5 * G:6 * G]) \
                    .then_inc(sdv.h, 1)
            else:
                e.tensor_copy(statb[:, 0:G], stat[:, 2 * G:3 * G])
                e.tensor_tensor(stat[:, 3 * G:4 * G], stat[:, 0:G],
                                stat[:, 2 * G:3 * G], ALU.mult)
                e.tensor_scalar_mul(stat[:, 3 * G:4 * G], stat[:, 3 * G:4 * G], -1.0)
                e.tensor_copy(statb[:, G:2 * G], stat[:, 3 * G:4 * G]) \
                    .then_inc(sdv.h, 1)

    def emit_ln_ac(self, e, op, M):
        sdv, sac = self.sdv, self.sac
        _, tag, eps_tile, stat = op
        e.wait_ge(sdv.h, M[("fdv1", tag)])
        e.activation(stat[:, 3 * G:4 * G], stat[:, G:2 * G], AF.Ln, bias=eps_tile[:])
        e.activation(stat[:, 2 * G:3 * G], stat[:, 3 * G:4 * G], AF.Exp,
                     scale=-0.5).then_inc(sac.h, 1)

    def wo_ln2_ln3(self, s):
        nc = self.nc
        sq, sg, spe, sdv, sac = self.sq, self.sg, self.spe, self.sdv, self.sac
        with contextlib.ExitStack() as est:
            E = est.enter_context
            wo = E(nc.sbuf_tensor(_u("o_wo"), [128, DC * D], BF16))
            OBF = [E(nc.sbuf_tensor(_u(f"o_obf{i}"), [128, DC * G], BF16))
                   for i in range(2)]
            OSQ = [E(nc.sbuf_tensor(_u(f"o_osq{i}"), [128, DC * G], BF16))
                   for i in range(2)]
            YG = [E(nc.sbuf_tensor(_u(f"o_yg{i}"), [128, DC * G], F32))
                  for i in range(2)]
            YBF = E(nc.sbuf_tensor(_u("o_ybf"), [128, DC * G], BF16))
            YSQ = E(nc.sbuf_tensor(_u("o_ysq"), [128, DC * G], BF16))
            stat = E(nc.sbuf_tensor(_u("o_stat"), [1, 6 * G], F32))
            statb = E(nc.sbuf_tensor(_u("o_statb"), [1, 4 * G], BF16))
            tmp = E(nc.sbuf_tensor(_u("o_tmp"), [128, G], F32))
            po = [E(nc.psum_tensor(_u(f"o_po{i}"), [128, G], F32)) for i in range(2)]
            psig = E(nc.psum_tensor(_u("o_psig"), [1, G], F32))
            psig2 = E(nc.psum_tensor(_u("o_psig2"), [1, G], F32))
            prep = E(nc.psum_tensor(_u("o_prep"), [128, G], F32))
            pbrep = E(nc.psum_tensor(_u("o_pbrep"), [128, G], F32))

            ph = Phase(nc, f"wo_{s}")
            M = {}
            pe_ops, dv_ops, ac_ops, gp_ops = [], [], [], []

            def fgp0(e):
                self.wload(e, wo, self.d_wo, D, sg.h)
            sg.n += 16
            wo_ld = sg.n
            ph.on("gpsimd", fgp0)

            def sched_chains(g):
                for c in range(DC):
                    spe.n += 1
                    M[("woc", g, c)] = spe.n
                    pe_ops.append(("woc", g, c,
                                   wo_ld if (g == 0 and c == 0) else 0))
                    sac.n += 1
                    M[("oev", g, c)] = sac.n
                    ac_ops.append(("oev", g, c))

            def sched_stage(kind, g):
                tag = (s, kind, g)
                if kind == 2:
                    src_bf = [OBF[g % 2][:, c * G:(c + 1) * G] for c in range(DC)]
                    src_sq = [OSQ[g % 2][:, c * G:(c + 1) * G] for c in range(DC)]
                    w_sac = M[("oev", g, DC - 1)]
                else:
                    src_bf = [YBF[:, c * G:(c + 1) * G] for c in range(DC)]
                    src_sq = [YSQ[:, c * G:(c + 1) * G] for c in range(DC)]
                    w_sac = M[("ysq", g)]
                w_psig = self._last_fdv1[0]
                w_prep = self._last_apply[0]
                self.ln_stage(M, tag, src_bf, src_sq, DC, self.eps_ln, True,
                              psig, psig2, prep, pbrep, stat, statb,
                              pe_ops_stats, dv_ops, ac_ops, w_sac, w_psig, w_prep)
                self.ln_stage_repl(M, tag, True, statb, prep, pbrep,
                                   pe_ops_stats, w_prep)
                self._last_fdv1[0] = M[("fdv1", tag)]
                if kind == 2:
                    sdv.n += 1
                    M[("applyu", g)] = sdv.n
                    dv_ops.append(("applyu", g))
                    self._last_apply[0] = sdv.n
                    gp_ops.append(("xacc", g, sdv.n))
                    self.sgp2[g % 2].n += 16 * DC
                    M[("xacc", g)] = self.sgp2[g % 2].n
                    sdv.n += 1
                    M[("ybf", g)] = sdv.n
                    dv_ops.append(("ybf", g))
                    sac.n += 1
                    M[("ysq", g)] = sac.n
                    ac_ops.append(("ysq", g))
                else:
                    sdv.n += 1
                    M[("applyz", g)] = sdv.n
                    dv_ops.append(("applyz", g))
                    self._last_apply[0] = sdv.n
                    gp_ops.append(("yspill", g, sdv.n))
                    self.sgp2[g % 2].n += 16 * DC
                    M[("yspill", g)] = self.sgp2[g % 2].n

            self._last_fdv1 = [0]
            self._last_apply = [0]
            pe_ops_stats = pe_ops
            sched_chains(0)
            sched_chains(1)
            sched_stage(2, 0)
            sched_chains(2)
            sched_stage(2, 1)
            sched_stage(3, 0)
            sched_chains(3)
            sched_stage(2, 2)
            sched_stage(3, 1)
            sched_stage(2, 3)
            sched_stage(3, 2)
            sched_stage(3, 3)

            def fpe(e, ops=tuple(pe_ops)):
                for op in ops:
                    if op[0] == "woc":
                        _, g, c, w_ld = op
                        if w_ld:
                            e.wait_ge(sg.h, w_ld)
                        gc = g * DC + c
                        if gc >= 2:
                            pg, pc = divmod(gc - 2, DC)
                            e.wait_ge(sac.h, M[("oev", pg, pc)])
                        gs = slice(g * G, (g + 1) * G)
                        for k in range(DC):
                            e.matmul(po[c % 2][:],
                                     wo[:, k * D + c * 128:k * D + (c + 1) * 128],
                                     self.XN[k][:, gs], start=(k == 0),
                                     stop=(k == DC - 1))
                        e.last.then_inc(spe.h, 1)
                    else:
                        self.emit_ln_pe(e, op, M)
            ph.on("tensor", fpe)

            def fac(e, ops=tuple(ac_ops)):
                for op in ops:
                    if op[0] == "oev":
                        _, g, c = op
                        e.wait_ge(spe.h, M[("woc", g, c)])
                        if ("stats", (s, 2, g - 2)) in M:
                            e.wait_ge(spe.h, M[("stats", (s, 2, g - 2))])
                        e.activation(OBF[g % 2][:, c * G:(c + 1) * G], po[c % 2][:],
                                     AF.Copy)
                        e.activation(OSQ[g % 2][:, c * G:(c + 1) * G], po[c % 2][:],
                                     AF.Square).then_inc(sac.h, 1)
                    elif op[0] == "ysq":
                        _, g = op
                        e.wait_ge(sdv.h, M[("ybf", g)])
                        if ("stats", (s, 3, g - 1)) in M:
                            e.wait_ge(spe.h, M[("stats", (s, 3, g - 1))])
                        e.activation(YSQ[:], YBF[:], AF.Square).then_inc(sac.h, 1)
                    else:
                        self.emit_ln_ac(e, op, M)
            ph.on("scalar", fac)

            def fdv(e, ops=tuple(dv_ops)):
                for op in ops:
                    if op[0] == "applyu":
                        _, g = op
                        e.wait_ge(spe.h, M[("repl", (s, 2, g))])
                        if ("yspill", g - 2) in M:
                            e.wait_ge(self.sgp2[g % 2].h, M[("yspill", g - 2)])
                        for c in range(DC):
                            e.tensor_tensor(tmp[:], OBF[g % 2][:, c * G:(c + 1) * G],
                                            prep[:], ALU.mult)
                            e.tensor_tensor(YG[g % 2][:, c * G:(c + 1) * G], tmp[:],
                                            pbrep[:], ALU.add)
                        e.last.then_inc(sdv.h, 1)
                    elif op[0] == "ybf":
                        _, g = op
                        e.wait_ge(self.sgp2[g % 2].h, M[("xacc", g)])
                        if ("stats", (s, 3, g - 1)) in M:
                            e.wait_ge(spe.h, M[("stats", (s, 3, g - 1))])
                        e.tensor_copy(YBF[:], YG[g % 2][:]).then_inc(sdv.h, 1)
                    elif op[0] == "applyz":
                        _, g = op
                        gs = slice(g * G, (g + 1) * G)
                        e.wait_ge(spe.h, M[("repl", (s, 3, g))])
                        for c in range(DC):
                            e.tensor_tensor(tmp[:], YG[g % 2][:, c * G:(c + 1) * G],
                                            prep[:], ALU.mult)
                            e.tensor_tensor(self.XN[c][:, gs], tmp[:], pbrep[:],
                                            ALU.add)
                        e.last.then_inc(sdv.h, 1)
                    else:
                        self.emit_ln_dv(e, op, M)
            ph.on("vector", fdv)

            def fgp(e, ops=tuple(gp_ops)):
                for op in ops:
                    kind, g, w = op
                    e.wait_ge(sdv.h, w)
                    gs = slice(g * G, (g + 1) * G)
                    for c in range(DC):
                        if kind == "xacc":
                            e.dma_start(out=YG[g % 2][:, c * G:(c + 1) * G],
                                        in_=self.d_xT[s, c][:, gs],
                                        accum_op=ALU.add) \
                                .then_inc(self.sgp2[g % 2].h, 16)
                        else:
                            e.dma_start(out=self.d_xT[s, c][:, gs],
                                        in_=YG[g % 2][:, c * G:(c + 1) * G]) \
                                .then_inc(self.sgp2[g % 2].h, 16)
            ph.on("gpsimd", fgp)
            self.drain(ph, gp=True)
            ph.emit()

        self.tap_chunks(f"zn{s}", self.XN)

    def ffn_w1(self, s):
        nc = self.nc
        sq, sg, spe, sdv, sac = self.sq, self.sg, self.spe, self.sdv, self.sac
        with contextlib.ExitStack() as est:
            E = est.enter_context
            WQT = E(nc.sbuf_tensor(_u("f_wqt1"), [128, DC * DFF], BF16))
            HB = [E(nc.sbuf_tensor(_u(f"f_hb{i}"), [128, FC * G], BF16))
                  for i in range(2)]
            ph1 = [E(nc.psum_tensor(_u(f"f_ph{i}"), [128, G], F32)) for i in range(4)]
            ph = Phase(nc, f"w1_{s}")
            M = {}
            pe_ops, ac_ops, gp_ops = [], [], []

            def fs0(e):
                e.dma_start(out=WQT[:].rearrange("p (c f) -> p c f", f=DFF),
                            in_=self.d_wq1t.rearrange("c p f -> p c f")) \
                    .then_inc(sq.h, 16)
            sq.n += 16
            w_ld = sq.n
            ph.on("sync", fs0)

            for g in range(NG):
                for f in range(FC):
                    spe.n += 1
                    M[("w1c", g, f)] = spe.n
                    pe_ops.append(("w1c", g, f, w_ld if (g == 0 and f == 0) else 0))
                    sac.n += 1
                    M[("hev", g, f)] = sac.n
                    ac_ops.append(("hev", g, f))
                gp_ops.append(("hspill", g, sac.n))
                self.sgp2[g % 2].n += 16
                M[("hspill", g)] = self.sgp2[g % 2].n

            def fpe(e, ops=tuple(pe_ops)):
                for op in ops:
                    _, g, f, w_ld_ = op
                    if w_ld_:
                        e.wait_ge(sq.h, w_ld_)
                    gf = g * FC + f
                    if gf >= 4:
                        pg, pf = divmod(gf - 4, FC)
                        e.wait_ge(sac.h, M[("hev", pg, pf)])
                    gs = slice(g * G, (g + 1) * G)
                    for k in range(DC):
                        e.matmul(ph1[f % 4][:],
                                 WQT[:, k * DFF + f * 128:k * DFF + (f + 1) * 128],
                                 self.XN[k][:, gs], start=(k == 0), stop=(k == DC - 1))
                    e.last.then_inc(spe.h, 1)
            ph.on("tensor", fpe)

            def fac(e, ops=tuple(ac_ops)):
                for op in ops:
                    _, g, f = op
                    e.wait_ge(spe.h, M[("w1c", g, f)])
                    if ("hspill", g - 2) in M and f == 0:
                        e.wait_ge(self.sgp2[g % 2].h, M[("hspill", g - 2)])
                    e.activation(HB[g % 2][:, f * G:(f + 1) * G], ph1[f % 4][:],
                                 AF.Relu, bias=self.b1l[:, f:f + 1]) \
                        .then_inc(sac.h, 1)
            ph.on("scalar", fac)

            def fgp(e, ops=tuple(gp_ops)):
                for op in ops:
                    _, g, w = op
                    e.wait_ge(sac.h, w)
                    gs = slice(g * G, (g + 1) * G)
                    e.dma_start(out=self.d_h[:, :, gs].rearrange("f p j -> p f j"),
                                in_=HB[g % 2][:].rearrange("p (f j) -> p f j", j=G)) \
                        .then_inc(self.sgp2[g % 2].h, 16)
            ph.on("gpsimd", fgp)
            self.drain(ph, gp=True)
            ph.emit()

    def ffn_w2(self, s):
        nc = self.nc
        sq, sg, spe, sdv, sac = self.sq, self.sg, self.spe, self.sdv, self.sac
        with contextlib.ExitStack() as est:
            E = est.enter_context
            WQT = E(nc.sbuf_tensor(_u("f_wqt2"), [128, FC * D], BF16))
            HB = [E(nc.sbuf_tensor(_u(f"f_hb2{i}"), [128, FC * G], BF16))
                  for i in range(2)]
            FBF = E(nc.sbuf_tensor(_u("f_fbf"), [128, DC * G], BF16))
            FSQ = E(nc.sbuf_tensor(_u("f_fsq"), [128, DC * G], BF16))
            FOUT = E(nc.sbuf_tensor(_u("f_fout"), [128, DC * G], F32))
            OUTT = E(nc.sbuf_tensor(_u("f_outt"), [128, D], F32))
            stat = E(nc.sbuf_tensor(_u("f_stat"), [1, 6 * G], F32))
            statb = E(nc.sbuf_tensor(_u("f_statb"), [1, 4 * G], BF16))
            tmp = E(nc.sbuf_tensor(_u("f_tmp"), [128, G], F32))
            pf = [E(nc.psum_tensor(_u(f"f_pf{i}"), [128, G], F32)) for i in range(2)]
            psig = E(nc.psum_tensor(_u("f_psig"), [1, G], F32))
            psig2 = E(nc.psum_tensor(_u("f_psig2"), [1, G], F32))
            prep = E(nc.psum_tensor(_u("f_prep"), [128, G], F32))
            pbrep = E(nc.psum_tensor(_u("f_pbrep"), [128, G], F32))
            ptr = [E(nc.psum_tensor(_u(f"f_ptr{i}"), [128, 128], F32)) for i in range(2)]

            ph = Phase(nc, f"w2_{s}")
            M = {}
            if not hasattr(self, "_dbgM"):
                self._dbgM = {}
            self._dbgM[s] = M
            pe_ops, dv_ops, ac_ops, gp_ops, sy_ops = [], [], [], [], []
            self._last_fdv1 = [0]
            self._last_apply = [0]

            sq.n += 16
            M[("wld",)] = sq.n
            sy_ops.append(("wld",))

            def sched_hload(g):
                self.sqp[g % 2].n += 16
                M[("hld", g)] = self.sqp[g % 2].n
                sy_ops.append(("hld", g))

            def sched_chains(g, mid=None):
                for c in range(DC):
                    spe.n += 1
                    M[("w2c", g, c)] = spe.n
                    pe_ops.append(("w2c", g, c))
                    sac.n += 1
                    M[("fev", g, c)] = sac.n
                    ac_ops.append(("fev", g, c))
                    if c == 1 and mid is not None:
                        mid()

            def sched_stats4(g):
                tag = (s, 4, g)
                src_bf = [FBF[:, c * G:(c + 1) * G] for c in range(DC)]
                src_sq = [FSQ[:, c * G:(c + 1) * G] for c in range(DC)]
                self.ln_stage(M, tag, src_bf, src_sq, DC, self.eps_ln, True,
                              psig, psig2, prep, pbrep, stat, statb,
                              pe_ops, dv_ops, ac_ops,
                              M[("fev", g, DC - 1)], self._last_fdv1[0],
                              self._last_apply[0])
                self._last_fdv1[0] = M[("fdv1", tag)]

            def sched_repl4(g):
                self.ln_stage_repl(M, (s, 4, g), True, statb, prep, pbrep,
                                   pe_ops, self._last_apply[0])

            def sched_applyf(g):
                sdv.n += 1
                M[("applyf", g)] = sdv.n
                dv_ops.append(("applyf", g))
                self._last_apply[0] = sdv.n
                gp_ops.append(("yacc", g, sdv.n))
                self.sgp2[g % 2].n += 16 * DC
                M[("yacc", g)] = self.sgp2[g % 2].n

            def sched_egress(g):
                for st in range(4):
                    for c in range(DC):
                        spe.n += 1
                        M[("etr", g, st, c)] = spe.n
                        pe_ops.append(("etr", g, st, c))
                        sdv.n += 1
                        M[("eev", g, st, c)] = sdv.n
                        dv_ops.append(("eev", g, st, c))
                    self.sst.n += 16
                    M[("store", g, st)] = self.sst.n
                    sy_ops.append(("store", g, st))

            sched_hload(0)
            sched_chains(0)
            sched_hload(1)
            sched_stats4(0)
            sched_chains(1, mid=lambda: sched_repl4(0))
            sched_applyf(0)
            sched_hload(2)
            sched_stats4(1)
            sched_chains(2, mid=lambda: (sched_repl4(1), sched_egress(0)))
            sched_applyf(1)
            sched_hload(3)
            sched_stats4(2)
            sched_chains(3, mid=lambda: (sched_repl4(2), sched_egress(1)))
            sched_applyf(2)
            sched_stats4(3)
            sched_repl4(3)
            sched_egress(2)
            sched_applyf(3)
            sched_egress(3)

            def fpe(e, ops=tuple(pe_ops)):
                for op in ops:
                    if op[0] == "w2c":
                        _, g, c = op
                        if g == 0 and c == 0:
                            e.wait_ge(sq.h, M[("wld",)])
                        e.wait_ge(self.sqp[g % 2].h, M[("hld", g)])
                        gc = g * DC + c
                        if gc >= 2:
                            pg, pc = divmod(gc - 2, DC)
                            e.wait_ge(sac.h, M[("fev", pg, pc)])
                        for f in range(FC):
                            e.matmul(pf[c % 2][:],
                                     WQT[:, f * D + c * 128:f * D + (c + 1) * 128],
                                     HB[g % 2][:, f * G:(f + 1) * G],
                                     start=(f == 0), stop=(f == FC - 1))
                        e.last.then_inc(spe.h, 1)
                    elif op[0] == "etr":
                        _, g, st, c = op
                        k = (g * 4 + st) * DC + c
                        if st == 0 and c == 0:
                            e.wait_ge(self.sgp2[g % 2].h, M[("yacc", g)])
                        if k >= 2:
                            pg2 = k - 2
                            gg, rem = divmod(pg2, 4 * DC)
                            pst_, pc = divmod(rem, DC)
                            e.wait_ge(sdv.h, M[("eev", gg, pst_, pc)])
                        e.transpose(ptr[k % 2][:],
                                    FOUT[:, c * G + st * 128:c * G + (st + 1) * 128],
                                    self.identf[:]).then_inc(spe.h, 1)
                    else:
                        self.emit_ln_pe(e, op, M)
            ph.on("tensor", fpe)

            def fac(e, ops=tuple(ac_ops)):
                for op in ops:
                    if op[0] == "fev":
                        _, g, c = op
                        e.wait_ge(spe.h, M[("w2c", g, c)])
                        if g >= 1 and c == 0:
                            e.wait_ge(sdv.h, M[("applyf", g - 1)])
                        e.activation(FBF[:, c * G:(c + 1) * G], pf[c % 2][:],
                                     AF.Identity, bias=self.b2l[:, c:c + 1])
                        e.activation(FSQ[:, c * G:(c + 1) * G], pf[c % 2][:],
                                     AF.Square, bias=self.b2l[:, c:c + 1]) \
                            .then_inc(sac.h, 1)
                    else:
                        self.emit_ln_ac(e, op, M)
            ph.on("scalar", fac)

            def fdv(e, ops=tuple(dv_ops)):
                for op in ops:
                    if op[0] == "applyf":
                        _, g = op
                        e.wait_ge(spe.h, M[("repl", (s, 4, g))])
                        for c in range(DC):
                            e.tensor_tensor(tmp[:], FBF[:, c * G:(c + 1) * G],
                                            prep[:], ALU.mult)
                            e.tensor_tensor(FOUT[:, c * G:(c + 1) * G], tmp[:],
                                            pbrep[:], ALU.add)
                        e.last.then_inc(sdv.h, 1)
                    elif op[0] == "eev":
                        _, g, st, c = op
                        e.wait_ge(spe.h, M[("etr", g, st, c)])
                        if c == 0:
                            prev = M.get(("store", g, st - 1)) if st >= 1 \
                                else M.get(("store", g - 1, 3), 0)
                            if prev:
                                e.wait_ge(self.sst.h, prev)
                        e.tensor_copy(OUTT[:, c * 128:(c + 1) * 128],
                                      ptr[((g * 4 + st) * DC + c) % 2][:]) \
                            .then_inc(sdv.h, 1)
                    else:
                        self.emit_ln_dv(e, op, M)
            ph.on("vector", fdv)

            def fgp(e, ops=tuple(gp_ops)):
                for op in ops:
                    _, g, w = op
                    e.wait_ge(sdv.h, w)
                    gs = slice(g * G, (g + 1) * G)
                    for c in range(DC):
                        e.dma_start(out=FOUT[:, c * G:(c + 1) * G],
                                    in_=self.d_xT[s, c][:, gs],
                                    accum_op=ALU.add) \
                            .then_inc(self.sgp2[g % 2].h, 16)
            ph.on("gpsimd", fgp)

            def fsy(e, ops=tuple(sy_ops)):
                for op in ops:
                    if op[0] == "wld":
                        e.dma_start(out=WQT[:].rearrange("p (c f) -> p c f", f=D),
                                    in_=self.d_wq2t.rearrange("c p f -> p c f")) \
                            .then_inc(sq.h, 16)
                    elif op[0] == "hld":
                        _, g = op
                        if g >= 2:
                            e.wait_ge(spe.h, M[("w2c", g - 2, DC - 1)])
                        gs = slice(g * G, (g + 1) * G)
                        e.dma_start(out=HB[g % 2][:]
                                    .rearrange("p (f j) -> p f j", j=G),
                                    in_=self.d_h[:, :, gs]
                                    .rearrange("f p j -> p f j")) \
                            .then_inc(self.sqp[g % 2].h, 16)
                    elif op[0] == "store":
                        _, g, st = op
                        for c in range(DC):
                            e.wait_ge(sdv.h, M[("eev", g, st, c)])
                        e.dma_start(out=self.d_out[s, g * G + st * 128:
                                                   g * G + (st + 1) * 128, :],
                                    in_=OUTT[:]).then_inc(self.sst.h, 16)
            ph.on("sync", fsy)
            self.drain(ph, st=True, gp=True)
            ph.emit()

    def finale(self):
        ph = Phase(self.nc, "fin")
        sst = self.sst

        def fs(e):
            e.wait_ge(sst.h, sst.n)
        ph.on("sync", fs)
        ph.emit()

    def build(self, upto=None):
        nc = self.nc
        self.p0()
        if upto != "p0":
            do_q = upto not in ("ingest", "ln1", "proj", "attn", "wo")
            if do_q:
                self._qes = contextlib.ExitStack()
                self.quant_init(self._qes)
            for s in range(S):
                with contextlib.ExitStack() as xbst:
                    self.XN = [xbst.enter_context(
                        nc.sbuf_tensor(_u(f"XN{c}"), [128, T], BF16))
                        for c in range(DC)]
                    self.XB = [xbst.enter_context(
                        nc.sbuf_tensor(_u(f"XB{c}"), [128, T], BF16))
                        for c in range(DC)]
                    with contextlib.ExitStack() as sst_:
                        XT = [sst_.enter_context(
                            nc.sbuf_tensor(_u(f"XT{c}"), [128, T], F32))
                            for c in range(DC)]
                        self.ingest_ln1(s, XT, do_ln=(upto != "ingest"))
                    if upto in ("ln1", "ingest"):
                        continue
                    with contextlib.ExitStack() as pst:
                        E = pst.enter_context
                        QCb = [E(nc.sbuf_tensor(_u(f"QC{i}"), [128, T], BF16))
                               for i in range(QC)]
                        KR = [E(nc.sbuf_tensor(_u(f"KR{i}"), [128, T], BF16))
                              for i in range(4)]
                        VT = E(nc.sbuf_tensor(_u("VTt"), [128, NT * H * 65], BF16))
                        self.projections(s, pst, QCb, KR, VT)
                        if upto == "proj":
                            continue
                        with contextlib.ExitStack() as ast:
                            self.attention(s, ast, QCb, KR, VT)
                    if upto in ("attn", "ln1", "ingest", "proj"):
                        continue
                    self.wo_ln2_ln3(s)
                    if upto == "wo":
                        continue
                    self.ffn_w1(s)
                if upto in ("attn", "ln1", "ingest", "proj", "wo"):
                    continue
                self.ffn_w2(s)
        self.finale()
        return nc


def _host_consts():
    inv_freq = 1.0 / 10000.0 ** (np.arange(0, RD, 2, dtype=np.float32) / RD)
    t = np.arange(T, dtype=np.float32)
    freqs = np.outer(t, inv_freq)
    emb = np.concatenate([freqs, freqs], -1)
    cos = np.cos(emb).T.astype(np.float32)
    sin = np.sin(emb).T.astype(np.float32)
    ssin = np.concatenate([-sin[:RD // 2], sin[RD // 2:]], 0)
    ssin = ssin[(np.arange(RD) + RD // 2) % RD]
    cos128 = np.tile(cos, (4, 1)).astype(ml_dtypes.bfloat16)
    ssin128 = np.tile(ssin, (4, 1)).astype(ml_dtypes.bfloat16)
    ku = np.arange(128)[:, None]
    uu = np.arange(896)[None, :]
    maskw = np.where(uu < ku + 384, np.float32(MASKNEG), np.float32(0.0))
    return {
        "identf": np.eye(128, dtype=np.float32),
        "identb": np.eye(128, dtype=ml_dtypes.bfloat16),
        "cosT": np.ascontiguousarray(cos128),
        "ssinT": np.ascontiguousarray(ssin128),
        "maskw": maskw.astype(ml_dtypes.bfloat16),
    }


def _host_inputs(L0_attn_x, L0_attn_W_kv, L0_attn_g_kv, L0_attn_W_kdec,
                 L0_attn_W_vdec, L0_attn_W_q, L0_attn_W_qdec, L0_attn_W_krope,
                 L0_attn_W_qrope, L0_attn_W_o, L0_ffn_W1, L0_ffn_b1, L0_ffn_W2,
                 L0_ffn_b2):
    x = np.ascontiguousarray(np.asarray(L0_attn_x, dtype=np.float32))
    f32 = lambda a: np.ascontiguousarray(np.asarray(a, np.float32))
    w = {
        "wkv": f32(L0_attn_W_kv),
        "gkv": np.ascontiguousarray(f32(L0_attn_g_kv).reshape(KVC, 128).T),
        "wkdec": f32(L0_attn_W_kdec),
        "wvdec": f32(L0_attn_W_vdec),
        "wq": f32(L0_attn_W_q),
        "wqdec": f32(L0_attn_W_qdec),
        "wkrope": f32(L0_attn_W_krope),
        "wqrope": f32(L0_attn_W_qrope),
        "wo": f32(L0_attn_W_o),
        "w1": f32(L0_ffn_W1),
        "b1l": np.ascontiguousarray(f32(L0_ffn_b1).reshape(FC, 128).T),
        "w2": f32(L0_ffn_W2),
        "b2l": np.ascontiguousarray(f32(L0_ffn_b2).reshape(DC, 128).T),
    }
    w.update(_host_consts())
    in_maps = []
    for c in range(8):
        m = dict(w)
        m["x"] = np.ascontiguousarray(x[2 * c:2 * c + 2])
        in_maps.append(m)
    return in_maps


def kernel(**inputs):
    in_maps = _host_inputs(**inputs)
    mk = MK(_TAPS)
    nc = mk.build()
    res = run_bass_kernel_spmd(nc, in_maps, core_ids=list(range(8)))
    mk.es.close()
    out = np.stack([res.results[c]["out"] for c in range(8)], 0)
    return out.reshape(16, T, D).astype(np.float32)


# version 12
# speedup vs baseline: 1.0805x; 1.0805x over previous
import numpy as np
import ml_dtypes
import contextlib
from collections import defaultdict

import concourse.bass as bass
import concourse.mybir as mybir
from concourse.bass_utils import run_bass_kernel_spmd

F32 = mybir.dt.float32
BF16 = mybir.dt.bfloat16
AF = mybir.ActivationFunctionType
ALU = mybir.AluOpType

S = 2
T = 2048
D = 1024
DC = D // 128
H = 16
HD = 64
RD = 32
KVR = 256
KVC = KVR // 128
QR = 512
QC = QR // 128
DFF = 4096
FC = DFF // 128
G = 512
NG = T // G
NT = T // 128
EXPC = 11.0
MASKNEG = -60.0
SCALE_Q = float((HD + RD) ** -0.5)
LSC = float(2.0 ** 20)

_TAPS = ()
_DBG = set()

_UID = [0]


def _u(name):
    _UID[0] += 1
    return f"{name}_u{_UID[0]}"



class Sem:
    def __init__(self, es, nc, name):
        self.h = es.enter_context(nc.semaphore(name))
        self.n = 0


_DATAPATH = ("matmul", "transpose", "tensor", "activation", "memset", "memzero",
             "reciprocal", "stream_shuffle", "scalar_tensor_tensor", "dma_start",
             "reduce", "iota", "select", "ldweights", "copy", "square", "sign",
             "mul", "add", "bn_", "pool", "max", "sqrt")
_WRITE_KW = ("out", "accum_out", "out_max", "out_indices", "dest")


def _ap_name(x):
    t = getattr(x, "tensor", None)
    return getattr(t, "name", None) if t is not None else None


class _EngRec:

    def __init__(self, eng):
        self._e = eng
        self.last = None
        self._track = getattr(eng, "engine", None) not in (
            mybir.EngineType.PE, mybir.EngineType.SP)
        self._w = set()
        self._r = set()

    def __getattr__(self, name):
        a = getattr(self._e, name)
        if callable(a) and any(name.startswith(p) for p in _DATAPATH):
            def call(*args, __a=a, __n=name, **kw):
                if self._track:
                    is_dma = __n.startswith("dma_start")
                    wr, rd = [], []
                    for i, x in enumerate(args):
                        nm = _ap_name(x)
                        if nm is None:
                            continue
                        (wr if (i == 0 and not is_dma) else rd).append(nm)
                    for k, x in kw.items():
                        nm = _ap_name(x)
                        if nm is None:
                            continue
                        (wr if (k in _WRITE_KW and not is_dma) else rd).append(nm)
                    conflict = (any(n in self._w for n in rd)
                                or any(n in self._w or n in self._r for n in wr))
                    if conflict:
                        self._e.drain()
                        self._w.clear()
                        self._r.clear()
                    if not is_dma:
                        self._w.update(wr)
                    self._r.update(rd)
                r = __a(*args, **kw)
                if hasattr(r, "then_inc"):
                    self.last = r
                return r
            return call
        return a


class Phase:

    def __init__(self, nc, name=""):
        self.nc = nc
        self.name = name
        self.streams = defaultdict(list)

    def on(self, engine, fn):
        self.streams[engine].append(fn)

    def emit(self):
        if not self.streams:
            return
        with self.nc.Block() as blk:
            for engine in ("sync", "gpsimd", "tensor", "scalar", "vector"):
                fns = self.streams.get(engine)
                if not fns:
                    continue

                def runner(eng, fns=tuple(fns)):
                    rec = _EngRec(eng)
                    for f in fns:
                        f(rec)

                getattr(blk, engine)(runner)


class MK:
    def __init__(self, taps=()):
        self.taps = set(taps)
        self.nc = bass.Bass("TRN2", target_bir_lowering=False)
        self.es = contextlib.ExitStack()
        self.tap_d = {}
        self._declare_dram()
        self._alloc_persistent()

    def _declare_dram(self):
        nc = self.nc
        I, O, N = "ExternalInput", "ExternalOutput", "Internal"
        ap = lambda *a, **k: nc.dram_tensor(*a, **k).ap()
        self.d_x = ap("x", [S, T, D], F32, kind=I)
        self.d_wkv = ap("wkv", [D, KVR], F32, kind=I)
        self.d_gkv = ap("gkv", [128, KVC], F32, kind=I)
        self.d_wkdec = ap("wkdec", [KVR, D], F32, kind=I)
        self.d_wvdec = ap("wvdec", [KVR, D], F32, kind=I)
        self.d_wq = ap("wq", [D, QR], F32, kind=I)
        self.d_wqdec = ap("wqdec", [QR, D], F32, kind=I)
        self.d_wkrope = ap("wkrope", [D, H * RD], F32, kind=I)
        self.d_wqrope = ap("wqrope", [QR, H * RD], F32, kind=I)
        self.d_wo = ap("wo", [D, D], F32, kind=I)
        self.d_w1 = ap("w1", [DFF, D], F32, kind=I)
        self.d_b1 = ap("b1l", [128, FC], F32, kind=I)
        self.d_w2 = ap("w2", [D, DFF], F32, kind=I)
        self.d_b2 = ap("b2l", [128, DC], F32, kind=I)
        self.d_idf = ap("identf", [128, 128], F32, kind=I)
        self.d_idb = ap("identb", [128, 128], BF16, kind=I)
        self.d_cos = ap("cosT", [128, T], BF16, kind=I)
        self.d_ssin = ap("ssinT", [128, T], BF16, kind=I)
        self.d_mask = ap("maskw", [128, 896], BF16, kind=I)
        self.d_out = ap("out", [S, T, D], F32, kind=O)
        self.d_xT = ap("xT_scr", [S, DC, 128, T], F32, kind=N)
        self.d_h = ap("h_scr", [FC, 128, T], BF16, kind=N)
        self.d_wq1t = ap("wq1t_scr", [DC, 128, DFF], BF16, kind=N)
        self.d_wq2t = ap("wq2t_scr", [FC, 128, D], BF16, kind=N)

    def tap_dram(self, name, shape, dtype):
        if name not in self.tap_d:
            self.tap_d[name] = self.nc.dram_tensor(
                "tap_" + name, list(shape), dtype, kind="ExternalOutput").ap()
        return self.tap_d[name]

    def _alloc_persistent(self):
        nc, E = self.nc, self.es.enter_context
        self.identf = E(nc.sbuf_tensor(_u("c_identf"), [128, 128], F32))
        self.identb = E(nc.sbuf_tensor(_u("c_identb"), [128, 128], BF16))
        self.cosT = E(nc.sbuf_tensor(_u("c_cos"), [128, T], BF16))
        self.ssinT = E(nc.sbuf_tensor(_u("c_ssin"), [128, T], BF16))
        self.maskw = E(nc.sbuf_tensor(_u("c_maskw"), [128, 896], BF16))
        self.ones_col = E(nc.sbuf_tensor(_u("c_ones_col"), [128, 1], BF16))
        self.ones_colf = E(nc.sbuf_tensor(_u("c_ones_colf"), [128, 1], F32))
        self.ones_row = E(nc.sbuf_tensor(_u("c_ones_row"), [128, 128], BF16))
        self.prl65 = E(nc.sbuf_tensor(_u("c_prl65"), [65, 65], BF16))
        self.ones_rowf = E(nc.sbuf_tensor(_u("c_ones_rowf"), [1, 128], F32))
        self.eps_ln = E(nc.sbuf_tensor(_u("c_eps_ln"), [1, 1], F32))
        self.eps_kv = E(nc.sbuf_tensor(_u("c_eps_kv"), [1, 1], F32))
        self.negc = E(nc.sbuf_tensor(_u("c_negc"), [128, 1], F32))
        self.b1l = E(nc.sbuf_tensor(_u("c_b1l"), [128, FC], F32))
        self.b2l = E(nc.sbuf_tensor(_u("c_b2l"), [128, DC], F32))
        self.gkv = E(nc.sbuf_tensor(_u("c_gkv"), [128, KVC], F32))
        self.XN = None
        self.XB = None

        self.sq = Sem(self.es, nc, "sq")
        self.sg = Sem(self.es, nc, "sg")
        self.spe = Sem(self.es, nc, "spe")
        self.sdv = Sem(self.es, nc, "sdv")
        self.sac = Sem(self.es, nc, "sac")
        self.sst = Sem(self.es, nc, "sst")
        self.sqp = [Sem(self.es, nc, "sqp0"), Sem(self.es, nc, "sqp1")]
        self.sgp2 = [Sem(self.es, nc, "sgp20"), Sem(self.es, nc, "sgp21")]
        self.sqq = [Sem(self.es, nc, "sqq0"), Sem(self.es, nc, "sqq1")]
        self.sgq = [Sem(self.es, nc, "sgq0"), Sem(self.es, nc, "sgq1")]

    @staticmethod
    def wload(e, dst_sb, src_dram, cols, sem, n=16):
        e.dma_start(out=dst_sb[:].rearrange("p (c r) -> p c r", r=cols),
                    in_=src_dram.rearrange("(c p) r -> p c r", p=128)) \
            .then_inc(sem, 16)

    def drain(self, ph, sync=False, gp=False, st=False):
        if sync:
            def fs(e, v=self.sq.n):
                e.wait_ge(self.sq.h, v)
            ph.on("sync", fs)
        if gp:
            def fg(e, v=self.sg.n, v0=self.sgp2[0].n, v1=self.sgp2[1].n,
                   q0=self.sgq[0].n, q1=self.sgq[1].n):
                e.wait_ge(self.sg.h, v)
                e.wait_ge(self.sgp2[0].h, v0)
                e.wait_ge(self.sgp2[1].h, v1)
                e.wait_ge(self.sgq[0].h, q0)
                e.wait_ge(self.sgq[1].h, q1)
            ph.on("gpsimd", fg)
        if st:
            def ft(e, v=self.sst.n):
                e.wait_ge(self.sst.h, v)
            ph.on("sync", ft)

    def p0(self):
        ph = Phase(self.nc, "p0")
        sq = self.sq

        def fsync(e):
            for dst, src in ((self.identf, self.d_idf), (self.identb, self.d_idb),
                             (self.cosT, self.d_cos), (self.ssinT, self.d_ssin),
                             (self.maskw, self.d_mask), (self.b1l, self.d_b1),
                             (self.b2l, self.d_b2), (self.gkv, self.d_gkv)):
                e.dma_start(out=dst[:], in_=src).then_inc(sq.h, 16)
        sq.n += 16 * 8
        ph.on("sync", fsync)

        def fvec(e):
            e.memset(self.ones_col[:], 1.0)
            e.memset(self.ones_colf[:], 1.0)
            e.memset(self.ones_row[:], 1.0)
            e.memset(self.ones_rowf[:], 1.0)
            e.memset(self.eps_ln[:], 1e-5)
            e.memset(self.eps_kv[:], 1e-6)
            e.memset(self.negc[:], -EXPC)
            e.memset(self.prl65[:], 0.0)
            e.memset(self.prl65[64:65, :], 1.0)
        ph.on("vector", fvec)
        self.drain(ph, sync=True)
        ph.emit()

    def quant_init(self, es):
        nc = self.nc
        E = es.enter_context
        q = self._q = {}
        q["wtmp"] = [E(nc.sbuf_tensor(_u(f"q_w{i}"), [128, 1024], F32, side="right")) for i in range(2)]
        q["acc"] = E(nc.sbuf_tensor(_u("q_acc"), [128, 2], F32, side="right"))
        q["r1"] = E(nc.sbuf_tensor(_u("q_r1"), [128, 1], F32, side="right"))
        q["thr"] = E(nc.sbuf_tensor(_u("q_thr"), [128, 2], F32, side="right"))
        q["nthr"] = E(nc.sbuf_tensor(_u("q_nthr"), [128, 2], F32, side="right"))
        q["throw"] = E(nc.sbuf_tensor(_u("q_throw"), [1, 2], F32, side="right"))
        q["bq"] = [E(nc.sbuf_tensor(_u(f"q_bq{i}"), [128, 1024], BF16, side="right")) for i in range(2)]
        q["qq"] = [E(nc.sbuf_tensor(_u(f"q_qq{i}"), [128, 1024], BF16, side="right")) for i in range(2)]
        q["stage"] = [E(nc.sbuf_tensor(_u(f"q_st{i}"), [128, 1024], BF16, side="right")) for i in range(2)]
        q["qtr"] = [E(nc.psum_tensor(_u(f"q_ptr{i}"), [128, 128], BF16, side="right")) for i in range(2)]
        units = []
        for ch in range(FC):
            units.append((0, ch, 0, D))
        for ch in range(DC):
            for qr in range(4):
                units.append((1, ch, qr * 1024, 1024))
        q["units"] = units
        q["ai"] = 0
        q["ci"] = 0
        q["thr_done"] = False
        q["amarks"] = {}
        q["cmarks"] = {}
        q["nA"] = len(units)

    def q_pump_A(self, ph, n):
        q = getattr(self, "_q", None)
        if not q or q["ai"] >= q["nA"]:
            return
        sdv = self.sdv
        i0, i1 = q["ai"], min(q["ai"] + n, q["nA"])
        q["ai"] = i1
        units, wtmp, acc, r1 = q["units"], q["wtmp"], q["acc"], q["r1"]
        loads = {}
        for i in range(i0, i1):
            self.sqq[i % 2].n += 16
            loads[i] = self.sqq[i % 2].n
            sdv.n += 1
            q["amarks"][i] = sdv.n

        def fsy(e, i0=i0, i1=i1, loads=dict(loads)):
            for i in range(i0, i1):
                mi, ch, off, cols = units[i]
                if i >= 2:
                    e.wait_ge(sdv.h, q["amarks"][i - 2])
                src = (self.d_w1 if mi == 0 else self.d_w2)[
                    ch * 128:(ch + 1) * 128, off:off + cols]
                e.dma_start(out=wtmp[i % 2][:, 0:cols], in_=src) \
                    .then_inc(self.sqq[i % 2].h, 16)
        ph.on("sync", fsy)

        def fdv(e, i0=i0, i1=i1, loads=dict(loads)):
            for i in range(i0, i1):
                mi, ch, off, cols = units[i]
                e.wait_ge(self.sqq[i % 2].h, loads[i])
                e.tensor_reduce(r1[:], wtmp[i % 2][:, 0:cols],
                                mybir.AxisListType.X, ALU.add,
                                apply_absolute_value=True)
                first = (i == 0) if mi == 0 else (i == FC)
                if first:
                    e.tensor_copy(acc[:, mi:mi + 1], r1[:]).then_inc(sdv.h, 1)
                else:
                    e.tensor_tensor(acc[:, mi:mi + 1], acc[:, mi:mi + 1], r1[:],
                                    ALU.add).then_inc(sdv.h, 1)
        ph.on("vector", fdv)

    def q_pump_thr(self, ph, es):
        q = getattr(self, "_q", None)
        if not q or q["thr_done"]:
            return
        assert q["ai"] >= q["nA"], "thr pumped before pass A complete"
        q["thr_done"] = True
        nc = self.nc
        spe, sdv = self.spe, self.sdv
        pgam = es.enter_context(nc.psum_tensor(_u("q_pgam"), [1, 2], F32))
        pthr = es.enter_context(nc.psum_tensor(_u("q_pthr"), [128, 2], F32))
        a_last = sdv.n
        spe.n += 1
        g_done = spe.n

        def fpe(e, a_last=max(q["amarks"].values())):
            e.wait_ge(sdv.h, a_last)
            e.matmul(pgam[:], self.ones_colf[:], q["acc"][:], start=True,
                     stop=True).then_inc(spe.h, 1)
        ph.on("tensor", fpe)
        sdv.n += 1
        t_done = sdv.n

        def fdv(e, g_done=g_done):
            e.wait_ge(spe.h, g_done)
            for mi, nn in ((0, FC * 128 * D), (1, DC * 128 * DFF)):
                e.tensor_scalar(q["throw"][:, mi:mi + 1], pgam[:, mi:mi + 1],
                                0.5 / nn, 0.5e-5, ALU.mult, ALU.add)
            e.last.then_inc(sdv.h, 1)
        ph.on("vector", fdv)
        spe.n += 1
        t2_done = spe.n

        def fpe2(e, t_done=t_done):
            e.wait_ge(sdv.h, t_done)
            e.matmul(pthr[:], self.ones_rowf[:], q["throw"][:], start=True,
                     stop=True).then_inc(spe.h, 1)
        ph.on("tensor", fpe2)
        sdv.n += 1
        q["thr_mark"] = sdv.n

        def fdv2(e, t2_done=t2_done):
            e.wait_ge(spe.h, t2_done)
            e.tensor_copy(q["thr"][:], pthr[:])
            e.tensor_scalar_mul(q["nthr"][:], pthr[:], -1.0).then_inc(sdv.h, 1)
        ph.on("vector", fdv2)

    def q_pump_C(self, ph, n):
        q = getattr(self, "_q", None)
        if not q or not q.get("thr_done") or q["ci"] >= q["nA"]:
            return
        spe, sdv, sg = self.spe, self.sdv, self.sg
        i0, i1 = q["ci"], min(q["ci"] + n, q["nA"])
        q["ci"] = i1
        units, wtmp = q["units"], q["wtmp"]
        bq, qq, stage, qtr = q["bq"], q["qq"], q["stage"], q["qtr"]
        loads = {}
        ktr = {}
        for i in range(i0, i1):
            mi, ch, off, cols = units[i]
            self.sqq[i % 2].n += 16
            loads[i] = self.sqq[i % 2].n
            sdv.n += 1
            q["cmarks"][("q", i)] = sdv.n
            for j in range(cols // 128):
                spe.n += 1
                q["cmarks"][("t", i, j)] = spe.n
                sdv.n += 1
                q["cmarks"][("e", i, j)] = sdv.n
            q["cmarks"][("t", i)] = spe.n
            q["cmarks"][("e", i)] = sdv.n
            self.sgq[i % 2].n += 16
            q["cmarks"][("s", i)] = self.sgq[i % 2].n

        def fsy(e, i0=i0, i1=i1, loads=dict(loads)):
            for i in range(i0, i1):
                mi, ch, off, cols = units[i]
                if i >= 2:
                    e.wait_ge(sdv.h, q["cmarks"][("q", i - 2)])
                src = (self.d_w1 if mi == 0 else self.d_w2)[
                    ch * 128:(ch + 1) * 128, off:off + cols]
                e.dma_start(out=wtmp[i % 2][:, 0:cols], in_=src) \
                    .then_inc(self.sqq[i % 2].h, 16)
        ph.on("sync", fsy)

        def fdv(e, i0=i0, i1=i1, loads=dict(loads)):
            for i in range(i0, i1):
                mi, ch, off, cols = units[i]
                e.wait_ge(self.sqq[i % 2].h, loads[i])
                if i == 0:
                    e.wait_ge(sdv.h, q["thr_mark"])
                if i >= 2:
                    e.wait_ge(self.sgq[i % 2].h,
                              q["cmarks"][("s", i - 2)])
                    e.wait_ge(spe.h, q["cmarks"][("t", i - 2)])
                e.tensor_single_scalar(bq[i % 2][:, 0:cols], wtmp[i % 2][:, 0:cols],
                                       q["nthr"][:, mi:mi + 1], ALU.is_lt)
                e.scalar_tensor_tensor(qq[i % 2][:, 0:cols], wtmp[i % 2][:, 0:cols],
                                       q["thr"][:, mi:mi + 1], bq[i % 2][:, 0:cols],
                                       ALU.is_gt, ALU.subtract).then_inc(sdv.h, 1)
                for j in range(cols // 128):
                    e.wait_ge(spe.h, q["cmarks"][("t", i, j)])
                    e.tensor_copy(stage[i % 2][:, j * 128:(j + 1) * 128],
                                  qtr[j % 2][:]).then_inc(sdv.h, 1)
        ph.on("vector", fdv)

        def fpe(e, i0=i0, i1=i1):
            flat = [(i, j) for i in range(i0, i1)
                    for j in range(units[i][3] // 128)]
            for k, (i, j) in enumerate(flat):
                if j == 0:
                    e.wait_ge(sdv.h, q["cmarks"][("q", i)])
                if k >= 2:
                    e.wait_ge(sdv.h, q["cmarks"][("e",) + flat[k - 2]])
                elif i0 > 0 and ("e", i0 - 1) in q["cmarks"]:
                    e.wait_ge(sdv.h, q["cmarks"][("e", i0 - 1)])
                e.transpose(qtr[j % 2][:], qq[i % 2][:, j * 128:(j + 1) * 128],
                            self.identb[:]).then_inc(spe.h, 1)
        ph.on("tensor", fpe)

        def fgp(e, i0=i0, i1=i1):
            for i in range(i0, i1):
                mi, ch, off, cols = units[i]
                e.wait_ge(sdv.h, q["cmarks"][("e", i)])
                if mi == 0:
                    out = self.d_wq1t[:, :, ch * 128:(ch + 1) * 128] \
                        .rearrange("c p j -> p c j")
                else:
                    f0 = off // 128
                    out = self.d_wq2t[f0:f0 + 8, :, ch * 128:(ch + 1) * 128] \
                        .rearrange("c p j -> p c j")
                e.dma_start(out=out,
                            in_=stage[i % 2][:, 0:cols]
                            .rearrange("p (c j) -> p c j", j=128)) \
                    .then_inc(self.sgq[i % 2].h, 16)
        ph.on("gpsimd", fgp)

    def ln_group(self, ph, est, tag, g, n_chunks, src_bf_slices, sq_src_slices,
                 eps_tile, repl_hilo, psig, psig2, prep, pbrep, stat, statb,
                 pe_extra_wait=None):
        nc = self.nc
        spe, sdv, sac = self.spe, self.sdv, self.sac
        nf = float(n_chunks * 128)

        dvw = sdv.n
        def fpe(e, src=list(src_bf_slices), sqs=list(sq_src_slices),
                w=pe_extra_wait, dvw=dvw):
            if w is not None:
                e.wait_ge(w[0].h, w[1])
            e.wait_ge(sdv.h, dvw)
            for c in range(len(src)):
                e.matmul(psig[:], self.ones_col[:], src[c],
                         start=(c == 0), stop=(c == len(src) - 1))
            for c in range(len(sqs)):
                e.matmul(psig2[:], self.ones_col[:], sqs[c],
                         start=(c == 0), stop=(c == len(sqs) - 1))
            e.last.then_inc(spe.h, 1)
        spe.n += 1
        stats_done = spe.n
        ph.on("tensor", fpe)

        def fdv1(e, stats_done=stats_done):
            e.wait_ge(spe.h, stats_done)
            e.tensor_scalar_mul(stat[:, 0:G], psig[:], 1.0 / nf)
            e.tensor_scalar_mul(stat[:, G:2 * G], psig2[:], 1.0 / nf)
            e.tensor_tensor(stat[:, 2 * G:3 * G], stat[:, 0:G], stat[:, 0:G], ALU.mult)
            e.tensor_tensor(stat[:, G:2 * G], stat[:, G:2 * G], stat[:, 2 * G:3 * G],
                            ALU.subtract).then_inc(sdv.h, 1)
        sdv.n += 1
        var_done = sdv.n
        ph.on("vector", fdv1)

        def fac(e, var_done=var_done):
            e.wait_ge(sdv.h, var_done)
            e.activation(stat[:, 3 * G:4 * G], stat[:, G:2 * G], AF.Ln, bias=eps_tile[:])
            e.activation(stat[:, 2 * G:3 * G], stat[:, 3 * G:4 * G], AF.Exp,
                         scale=-0.5).then_inc(sac.h, 1)
        sac.n += 1
        r_done = sac.n
        ph.on("scalar", fac)

        if not repl_hilo:
            def fdv2(e, r_done=r_done):
                e.wait_ge(sac.h, r_done)
                e.tensor_copy(statb[:, 0:G], stat[:, 2 * G:3 * G])
                e.tensor_tensor(stat[:, 3 * G:4 * G], stat[:, 0:G],
                                stat[:, 2 * G:3 * G], ALU.mult)
                e.tensor_scalar_mul(stat[:, 3 * G:4 * G], stat[:, 3 * G:4 * G], -1.0)
                e.tensor_copy(statb[:, G:2 * G], stat[:, 3 * G:4 * G]) \
                    .then_inc(sdv.h, 1)
            sdv.n += 1
            sb_done = sdv.n
            ph.on("vector", fdv2)

            def fpe2(e, sb_done=sb_done):
                e.wait_ge(sdv.h, sb_done)
                e.matmul(prep[:], self.ones_row[0:1, :], statb[:, 0:G],
                         start=True, stop=True)
                e.matmul(pbrep[:], self.ones_row[0:1, :], statb[:, G:2 * G],
                         start=True, stop=True).then_inc(spe.h, 1)
            spe.n += 1
            ph.on("tensor", fpe2)
        else:
            def fdv2(e, r_done=r_done):
                e.wait_ge(sac.h, r_done)
                e.tensor_tensor(stat[:, 3 * G:4 * G], stat[:, 0:G],
                                stat[:, 2 * G:3 * G], ALU.mult)
                e.tensor_scalar_mul(stat[:, 3 * G:4 * G], stat[:, 3 * G:4 * G], -1.0)
                e.tensor_copy(statb[:, 0:G], stat[:, 2 * G:3 * G])
                e.tensor_copy(statb[:, G:2 * G], stat[:, 3 * G:4 * G])
                e.tensor_tensor(stat[:, 4 * G:5 * G], stat[:, 2 * G:3 * G],
                                statb[:, 0:G], ALU.subtract)
                e.tensor_tensor(stat[:, 5 * G:6 * G], stat[:, 3 * G:4 * G],
                                statb[:, G:2 * G], ALU.subtract)
                e.tensor_copy(statb[:, 2 * G:3 * G], stat[:, 4 * G:5 * G])
                e.tensor_copy(statb[:, 3 * G:4 * G], stat[:, 5 * G:6 * G]) \
                    .then_inc(sdv.h, 1)
            sdv.n += 1
            sb_done = sdv.n
            ph.on("vector", fdv2)

            def fpe2(e, sb_done=sb_done):
                e.wait_ge(sdv.h, sb_done)
                e.matmul(prep[:], self.ones_row[0:1, :], statb[:, 0:G],
                         start=True, stop=False)
                e.matmul(prep[:], self.ones_row[0:1, :], statb[:, 2 * G:3 * G],
                         start=False, stop=True)
                e.matmul(pbrep[:], self.ones_row[0:1, :], statb[:, G:2 * G],
                         start=True, stop=False)
                e.matmul(pbrep[:], self.ones_row[0:1, :], statb[:, 3 * G:4 * G],
                         start=False, stop=True).then_inc(spe.h, 1)
            spe.n += 1
            ph.on("tensor", fpe2)
        return spe.n

    def ingest_ln1(self, s, XT, do_ln=True):
        nc = self.nc
        sq, sg, spe, sdv, sac = self.sq, self.sg, self.spe, self.sdv, self.sac
        with contextlib.ExitStack() as est:
            E = est.enter_context
            xtok = [E(nc.sbuf_tensor(_u(f"i_xtok{i}"), [128, D], F32)) for i in range(2)]
            pst_ = contextlib.ExitStack()
            ptr = [pst_.enter_context(nc.psum_tensor(_u(f"i_ptr{i}"), [128, 128], F32))
                   for i in range(2)]

            ph = Phase(nc, f"ingest{s}")
            NTT = NT
            for d in _DBG:
                if d.startswith("nt"):
                    NTT = int(d[2:])
            loads, trs, ev_dv, ev_ac = [], [], [], []
            for tt in range(NTT):
                self.sqp[tt % 2].n += 16
                loads.append(self.sqp[tt % 2].n)
                for c in range(DC):
                    spe.n += 1
                    trs.append(spe.n)
                    sdv.n += 1
                    ev_dv.append(sdv.n)
                    sac.n += 1
                    ev_ac.append(sac.n)

            def fsync(e):
                for tt in range(NTT):
                    if tt >= 2 and "loadsonly" not in _DBG:
                        e.wait_ge(spe.h, trs[(tt - 1) * DC - 1])
                    e.dma_start(out=xtok[tt % 2][:],
                                in_=self.d_x[s, tt * 128:(tt + 1) * 128, :]) \
                        .then_inc(self.sqp[tt % 2].h, 16)
                if "loadsonly" in _DBG:
                    e.wait_ge(self.sqp[0].h, self.sqp[0].n)
                    e.wait_ge(self.sqp[1].h, self.sqp[1].n)
            ph.on("sync", fsync)

            def fpe(e):
                if "loadsonly" in _DBG:
                    return
                k = 0
                for tt in range(NTT):
                    e.wait_ge(self.sqp[tt % 2].h, loads[tt])
                    for c in range(DC):
                        if k >= 2:
                            if "nodve" not in _DBG:
                                e.wait_ge(sdv.h, ev_dv[k - 2])
                            if "noact" not in _DBG:
                                e.wait_ge(sac.h, ev_ac[k - 2])
                        e.transpose(ptr[k % 2][:], xtok[tt % 2][:, c * 128:(c + 1) * 128],
                                    self.identf[:]).then_inc(spe.h, 1)
                        k += 1
            ph.on("tensor", fpe)

            def fdv(e):
                if "loadsonly" in _DBG or "nodve" in _DBG:
                    return
                k = 0
                for tt in range(NTT):
                    for c in range(DC):
                        e.wait_ge(spe.h, trs[k])
                        e.tensor_copy(XT[c][:, tt * 128:(tt + 1) * 128], ptr[k % 2][:]) \
                            .then_inc(sdv.h, 1)
                        k += 1
            ph.on("vector", fdv)

            def fac(e):
                if "loadsonly" in _DBG or "noact" in _DBG:
                    return
                k = 0
                for tt in range(NTT):
                    for c in range(DC):
                        e.wait_ge(sdv.h, ev_dv[k])
                        e.activation(self.XB[c][:, tt * 128:(tt + 1) * 128],
                                     ptr[k % 2][:], AF.Copy).then_inc(sac.h, 1)
                        k += 1
            ph.on("scalar", fac)

            if "nospill" not in _DBG:
                def fgp(e, w=ev_dv[-1]):
                    e.wait_ge(sdv.h, w)
                    for c in range(DC):
                        e.dma_start(out=self.d_xT[s, c], in_=XT[c][:]).then_inc(sg.h, 16)
                sg.n += 16 * DC
                ph.on("gpsimd", fgp)
            self.q_pump_A(ph, 64)
            self.drain(ph, gp=True)
            ph.emit()
            pst_.close()

            if not do_ln:
                return
            with contextlib.ExitStack() as lst:
                El = lst.enter_context
                sqb = El(nc.sbuf_tensor(_u("l1_sq"), [128, DC * G], BF16))
                stat = El(nc.sbuf_tensor(_u("l1_stat"), [1, 6 * G], F32))
                statb = El(nc.sbuf_tensor(_u("l1_statb"), [1, 4 * G], BF16))
                tmp = El(nc.sbuf_tensor(_u("l1_tmp"), [128, G], F32))
                psig = El(nc.psum_tensor(_u("l1_psig"), [1, G], F32))
                psig2 = El(nc.psum_tensor(_u("l1_psig2"), [1, G], F32))
                prep = El(nc.psum_tensor(_u("l1_prep"), [128, G], F32))
                pbrep = El(nc.psum_tensor(_u("l1_pbrep"), [128, G], F32))

                ph = Phase(nc, f"ln1_{s}")
                for g in range(NG):
                    gs = slice(g * G, (g + 1) * G)

                    def facq(e, gs=gs):
                        for c in range(DC):
                            e.activation(sqb[:, c * G:(c + 1) * G], self.XB[c][:, gs],
                                         AF.Square).then_inc(sac.h, 1)
                    sac.n += DC
                    sq_done = sac.n
                    ph.on("scalar", facq)

                    rep_done = self.ln_group(
                        ph, lst, f"ln1_{s}_{g}", g, DC,
                        [self.XB[c][:, gs] for c in range(DC)],
                        [sqb[:, c * G:(c + 1) * G] for c in range(DC)],
                        self.eps_ln, False, psig, psig2, prep, pbrep, stat, statb,
                        pe_extra_wait=(sac, sq_done))

                    def fap(e, gs=gs, rep_done=rep_done, XT=XT):
                        e.wait_ge(spe.h, rep_done)
                        for c in range(DC):
                            e.tensor_tensor(tmp[:], XT[c][:, gs], prep[:], ALU.mult)
                            e.tensor_tensor(self.XN[c][:, gs], tmp[:], pbrep[:],
                                            ALU.add)
                        e.last.then_inc(sdv.h, 1)
                    sdv.n += 1
                    ph.on("vector", fap)
                self.q_pump_thr(ph, lst)
                self.q_pump_C(ph, 14)
                ph.emit()

        self.tap_chunks(f"xn{s}", self.XN)
        self.tap_chunks(f"xt{s}", XT)

    def tap_chunks(self, name, chunks, width=T):
        if name not in self.taps:
            return
        nch = len(chunks)
        dt = chunks[0][:, 0:1].dtype
        d = self.tap_dram(name, [nch, 128, width], dt)
        ph = Phase(self.nc, f"tap_{name}")
        sst = self.sst

        def fs(e):
            for c in range(nch):
                e.dma_start(out=d[c], in_=chunks[c][:, 0:width]).then_inc(sst.h, 16)
        sst.n += 16 * nch
        ph.on("sync", fs)
        self.drain(ph, st=True)
        ph.emit()

    def tap_ap(self, name, ap_fn, shape, dtype):
        if name not in self.taps:
            return
        d = self.tap_dram(name, shape, dtype)
        ph = Phase(self.nc, f"tap_{name}")
        sst = self.sst
        pairs = ap_fn(d)

        def fs(e):
            for o, i in pairs:
                e.dma_start(out=o, in_=i).then_inc(sst.h, 16)
        sst.n += 16 * len(pairs)
        ph.on("sync", fs)
        self.drain(ph, st=True)
        ph.emit()

    def projections(self, s, est, QCb, KR, VT):
        kvst = contextlib.ExitStack()
        KVCb = [kvst.enter_context(
            self.nc.sbuf_tensor(_u(f"KVC{i}"), [128, T], BF16))
            for i in range(KVC)]
        """x_n -> kv_c (RMS+g), q_c (pre-scaled), k_rope (roped), k_content, V."""
        nc = self.nc
        sq, sg, spe, sdv, sac = self.sq, self.sg, self.spe, self.sdv, self.sac

        with contextlib.ExitStack() as wst:
            Ew = wst.enter_context
            wq = Ew(nc.sbuf_tensor(_u("p_wq"), [128, DC * QR], BF16))
            wkv = Ew(nc.sbuf_tensor(_u("p_wkv"), [128, DC * KVR], BF16))
            kvsq = Ew(nc.sbuf_tensor(_u("p_kvsq"), [128, KVC * G], BF16))
            preps = Ew(nc.sbuf_tensor(_u("p_preps"), [128, G], F32))
            stat = Ew(nc.sbuf_tensor(_u("p_stat"), [1, 4 * G], F32))
            statb = Ew(nc.sbuf_tensor(_u("p_statb"), [1, G], BF16))
            pkv = [Ew(nc.psum_tensor(_u(f"p_pkv{i}"), [128, G], F32)) for i in range(KVC)]
            pq = [Ew(nc.psum_tensor(_u(f"p_pq{i}"), [128, G], F32)) for i in range(2)]
            psig = Ew(nc.psum_tensor(_u("p_psig"), [1, G], F32))
            prep = Ew(nc.psum_tensor(_u("p_prep"), [128, G], F32))

            ph = Phase(nc, f"proj_kvq_{s}")



            def fgp(e):
                self.wload(e, wq, self.d_wq, QR, sg.h)
                self.wload(e, wkv, self.d_wkv, KVR, sg.h)
            sg.n += 32
            wkv_ld = sg.n
            ph.on("gpsimd", fgp)

            def fdvw(e, wqf_ld=wkv_ld):
                e.wait_ge(sg.h, wqf_ld)
                e.tensor_scalar_mul(wq[:], wq[:], SCALE_Q).then_inc(sdv.h, 1)
            sdv.n += 1
            wq_rdy = sdv.n
            ph.on("vector", fdvw)

            for g in range(NG):
                gs = slice(g * G, (g + 1) * G)
                def fpe(e, gs=gs, g=g, wkv_ld=wkv_ld, wq_rdy=wq_rdy, dvp=sdv.n):
                    e.wait_ge(sg.h, wkv_ld)
                    e.wait_ge(sdv.h, dvp)
                    for kc in range(KVC):
                        for k in range(DC):
                            e.matmul(pkv[kc][:],
                                     wkv[:, k * KVR + kc * 128:k * KVR + (kc + 1) * 128],
                                     self.XN[k][:, gs], start=(k == 0), stop=(k == DC - 1))
                    e.last.then_inc(spe.h, 1)
                spe.n += 1
                kv_done = spe.n
                ph.on("tensor", fpe)

                def facs(e, kv_done=kv_done):
                    e.wait_ge(spe.h, kv_done)
                    for kc in range(KVC):
                        e.activation(kvsq[:, kc * G:(kc + 1) * G], pkv[kc][:],
                                     AF.Square).then_inc(sac.h, 1)
                sac.n += KVC
                kvsq_done = sac.n
                ph.on("scalar", facs)

                def fpe_s(e, kvsq_done=kvsq_done, dvp=sdv.n):
                    e.wait_ge(sac.h, kvsq_done)
                    e.wait_ge(sdv.h, dvp)
                    for kc in range(KVC):
                        e.matmul(psig[:], self.ones_col[:], kvsq[:, kc * G:(kc + 1) * G],
                                 start=(kc == 0), stop=(kc == KVC - 1))
                    e.last.then_inc(spe.h, 1)
                spe.n += 1
                sig_done = spe.n
                ph.on("tensor", fpe_s)

                def fdv_r(e, sig_done=sig_done):
                    e.wait_ge(spe.h, sig_done)
                    e.tensor_scalar_mul(stat[:, 0:G], psig[:], 1.0 / KVR) \
                        .then_inc(sdv.h, 1)
                sdv.n += 1
                ms_done = sdv.n
                ph.on("vector", fdv_r)

                def fac_r(e, ms_done=ms_done):
                    e.wait_ge(sdv.h, ms_done)
                    e.activation(stat[:, G:2 * G], stat[:, 0:G], AF.Ln,
                                 bias=self.eps_kv[:])
                    e.activation(stat[:, 2 * G:3 * G], stat[:, G:2 * G], AF.Exp,
                                 scale=-0.5).then_inc(sac.h, 1)
                sac.n += 1
                r_done = sac.n
                ph.on("scalar", fac_r)

                def fdv_c(e, r_done=r_done):
                    e.wait_ge(sac.h, r_done)
                    e.tensor_copy(statb[:, 0:G], stat[:, 2 * G:3 * G]).then_inc(sdv.h, 1)
                sdv.n += 1
                rb_done = sdv.n
                ph.on("vector", fdv_c)

                def fpe_r(e, rb_done=rb_done):
                    e.wait_ge(sdv.h, rb_done)
                    e.matmul(prep[:], self.ones_row[0:1, :], statb[:, 0:G],
                             start=True, stop=True).then_inc(spe.h, 1)
                spe.n += 1
                rep_done = spe.n
                ph.on("tensor", fpe_r)

                def fdv_a(e, gs=gs, rep_done=rep_done):
                    e.wait_ge(spe.h, rep_done)
                    e.tensor_copy(preps[:], prep[:])
                    for kc in range(KVC):
                        e.scalar_tensor_tensor(KVCb[kc][:, gs], pkv[kc][:],
                                               self.gkv[:, kc:kc + 1], preps[:],
                                               ALU.mult, ALU.mult)
                    e.last.then_inc(sdv.h, 1)
                sdv.n += 1
                ph.on("vector", fdv_a)

                qdv_marks = [sdv.n] * 2
                for qc in range(QC):
                    def fpe_q(e, gs=gs, qc=qc, wq_rdy=wq_rdy, dvp=qdv_marks[qc]):
                        e.wait_ge(sdv.h, max(wq_rdy, dvp))
                        pb = pq[qc % 2]
                        for k in range(DC):
                            e.matmul(pb[:],
                                     wq[:, k * QR + qc * 128:k * QR + (qc + 1) * 128],
                                     self.XN[k][:, gs], start=(k == 0), stop=(k == DC - 1))
                        e.last.then_inc(spe.h, 1)
                    spe.n += 1
                    q_done = spe.n
                    ph.on("tensor", fpe_q)

                    def fdv_q(e, gs=gs, qc=qc, q_done=q_done):
                        e.wait_ge(spe.h, q_done)
                        e.tensor_copy(QCb[qc][:, gs], pq[qc % 2][:]).then_inc(sdv.h, 1)
                    sdv.n += 1
                    qdv_marks.append(sdv.n)
                    ph.on("vector", fdv_q)
            self.q_pump_C(ph, 18)
            ph.emit()

        with contextlib.ExitStack() as wst:
            Ew = wst.enter_context
            wkr = Ew(nc.sbuf_tensor(_u("p_wkr"), [128, DC * H * RD], BF16))
            pkr = [Ew(nc.psum_tensor(_u(f"p_pkr{i}"), [128, G], F32)) for i in range(2)]
            rt1 = Ew(nc.sbuf_tensor(_u("p_rt1"), [128, G], BF16))

            ph = Phase(nc, f"proj_krope_{s}")

            def fgp(e):
                self.wload(e, wkr, self.d_wkrope, H * RD, sg.h)
            sg.n += 16
            wkr_ld = sg.n
            ph.on("gpsimd", fgp)

            RC = H * RD // 128
            for g in range(NG):
                gs = slice(g * G, (g + 1) * G)
                for rc in range(RC):
                    def fpe(e, gs=gs, rc=rc, wkr_ld=wkr_ld, dvp=sdv.n):
                        e.wait_ge(sg.h, wkr_ld)
                        e.wait_ge(sdv.h, dvp)
                        for k in range(DC):
                            e.matmul(pkr[rc % 2][:],
                                     wkr[:, k * 512 + rc * 128:k * 512 + (rc + 1) * 128],
                                     self.XN[k][:, gs], start=(k == 0), stop=(k == DC - 1))
                        e.last.then_inc(spe.h, 1)
                    spe.n += 1
                    mm_done = spe.n
                    ph.on("tensor", fpe)

                    def fdv(e, gs=gs, rc=rc, mm_done=mm_done):
                        e.wait_ge(spe.h, mm_done)
                        p = pkr[rc % 2]
                        ks_ = KR[rc][:, gs]
                        e.tensor_tensor(rt1[:], p[:], self.ssinT[:, gs], ALU.mult)
                        e.stream_shuffle(ks_, rt1[:], mask=[(i + 16) % 32
                                                            for i in range(32)])
                        e.tensor_tensor(rt1[:], p[:], self.cosT[:, gs], ALU.mult)
                        e.tensor_tensor(ks_, ks_, rt1[:], ALU.add).then_inc(sdv.h, 1)
                    sdv.n += 1
                    ph.on("vector", fdv)
            self.q_pump_C(ph, 16)
            ph.emit()

        with contextlib.ExitStack() as wst:
            Ew = wst.enter_context
            wkd = Ew(nc.sbuf_tensor(_u("p_wkd"), [128, KVC * D], BF16))
            wvd = Ew(nc.sbuf_tensor(_u("p_wvd"), [128, KVC * D], BF16))
            pkd = [Ew(nc.psum_tensor(_u(f"p_pkd{i}"), [128, G], F32)) for i in range(2)]
            pv = Ew(nc.psum_tensor(_u("p_pv"), [128, D], F32))

            ph = Phase(nc, f"proj_dec_{s}")

            def fgp(e):
                self.wload(e, wkd, self.d_wkdec, D, sg.h)
                self.wload(e, wvd, self.d_wvdec, D, sg.h)
            sg.n += 32
            wd_ld = sg.n
            ph.on("gpsimd", fgp)

            for g in range(NG):
                gs = slice(g * G, (g + 1) * G)
                for c in range(DC):
                    def fpe(e, gs=gs, c=c, wd_ld=wd_ld, dvp=sdv.n):
                        e.wait_ge(sg.h, wd_ld)
                        e.wait_ge(sdv.h, dvp)
                        for kc in range(KVC):
                            e.matmul(pkd[c % 2][:],
                                     wkd[:, kc * D + c * 128:kc * D + (c + 1) * 128],
                                     KVCb[kc][:, gs], start=(kc == 0), stop=(kc == KVC - 1))
                        e.last.then_inc(spe.h, 1)
                    spe.n += 1
                    mm_done = spe.n
                    ph.on("tensor", fpe)

                    def fdv(e, gs=gs, c=c, mm_done=mm_done):
                        e.wait_ge(spe.h, mm_done)
                        e.tensor_copy(self.XB[c][:, gs], pkd[c % 2][:]).then_inc(sdv.h, 1)
                    sdv.n += 1
                    ph.on("vector", fdv)

            for tt in range(NT):
                def fpe(e, tt=tt, wd_ld=wd_ld, dvp=sdv.n):
                    e.wait_ge(sg.h, wd_ld)
                    e.wait_ge(sdv.h, dvp)
                    for half in range(2):
                        for kc in range(KVC):
                            e.matmul(pv[:, half * 512:(half + 1) * 512],
                                     KVCb[kc][:, tt * 128:(tt + 1) * 128],
                                     wvd[:, kc * D + half * 512:kc * D + (half + 1) * 512],
                                     start=(kc == 0), stop=(kc == KVC - 1))
                    e.last.then_inc(spe.h, 1)
                spe.n += 1
                mm_done = spe.n
                ph.on("tensor", fpe)

                def fdv(e, tt=tt, mm_done=mm_done):
                    e.wait_ge(spe.h, mm_done)
                    e.tensor_copy(
                        VT[:, tt * (H * 65):(tt + 1) * (H * 65)]
                        .rearrange("p (h j) -> p h j", j=65)[:, :, 0:64],
                        pv[:].rearrange("p (h j) -> p h j", j=64)) \
                        .then_inc(sdv.h, 1)
                sdv.n += 1
                ph.on("vector", fdv)

            def fdv_ones(e):
                e.memset(VT[:].rearrange("p (th j) -> p th j", j=65)[:, :, 64:65], 1.0)
                e.last.then_inc(sdv.h, 1)
            sdv.n += 1
            ph.on("vector", fdv_ones)
            self.q_pump_C(ph, 64)
            self.drain(ph, gp=True)
            ph.emit()
        kvst.close()
        if getattr(self, "_q", None) and self._q["ci"] >= self._q["nA"]:
            self._q = None
            self._qes.close()

        self.tap_chunks(f"kvc{s}", KVCb)
        self.tap_chunks(f"qc{s}", QCb)
        self.tap_chunks(f"kr{s}", KR)
        self.tap_chunks(f"kc{s}", self.XB)
        self.tap_chunks(f"vt{s}", [VT], width=NT * H * 65)

    def attention(self, s, est, QCb, KR, VT):
        nc = self.nc
        sq, sg, spe, sdv, sac = self.sq, self.sg, self.spe, self.sdv, self.sac
        E = est.enter_context
        wqd = E(nc.sbuf_tensor(_u("a_wqd"), [128, QC * D], BF16))
        wqr = E(nc.sbuf_tensor(_u("a_wqr"), [128, QC * H * RD], BF16))
        QCZ = E(nc.sbuf_tensor(_u("a_qcz"), [128, H * G], BF16))
        QRZ = E(nc.sbuf_tensor(_u("a_qrz"), [128, H * G], BF16))
        PT = [E(nc.sbuf_tensor(_u(f"a_pt{i}"), [128, 4 * G], BF16)) for i in range(2)]
        OST = E(nc.sbuf_tensor(_u("a_ost"), [128, 8 * G], BF16))
        RL = E(nc.sbuf_tensor(_u("a_rl"), [128, 2 * G], BF16))
        RLS = E(nc.sbuf_tensor(_u("a_rls"), [128, 2 * G], BF16))
        RLB = E(nc.sbuf_tensor(_u("a_rlb"), [128, 2 * G], BF16))
        SCP = E(nc.psum_tensor(_u("a_scp"), [128, 4 * G], F32))
        PO = [E(nc.psum_tensor(_u(f"a_po{i}"), [65, G], F32)) for i in range(2)]
        PRL = [E(nc.psum_tensor(_u(f"a_prl{i}"), [65, G], F32)) for i in range(2)]

        ph = Phase(nc, f"attn_{s}")

        def fz0(e):
            e.memset(QCZ[:], 0.0)
            e.memset(QRZ[:], 0.0)
            e.memset(RLB[:], 0.0)
        ph.on("vector", fz0)
        M = {}
        pe_ops, dv_ops, ac_ops, gp_ops = [], [], [], []

        def fgp0(e):
            self.wload(e, wqd, self.d_wqdec, D, sg.h)
            self.wload(e, wqr, self.d_wqrope, H * RD, sg.h)
        sg.n += 32
        wq_ld = sg.n
        ph.on("gpsimd", fgp0)

        last_oev = [0, 0]
        prev_odma = 0
        pend_prl = []
        wslice = [0, 0, 0]
        sc_slot_ev = [0] * 4
        prev_win_exp = 0
        mask_shuf = [(i + 16) % 32 for i in range(32)]

        for g in range(NG):
            gs = slice(g * G, (g + 1) * G)
            KT, NP = 4 * (g + 1), 2 * (g + 1)

            qev_start = sdv.n
            for c in range(12):
                slot = c % 4
                par, half = slot % 2, slot // 2
                spe.n += 1
                M[("qp", g, c)] = spe.n
                pe_ops.append(("qproj", g, gs, c, par, half,
                               prev_win_exp if c == 0 else 0,
                               sc_slot_ev[slot], wq_ld if (g == 0 and c == 0) else 0))
                sdv.n += 1
                M[("qe", g, c)] = sdv.n
                sc_slot_ev[slot] = sdv.n
                dv_ops.append(("qevac", g, gs, c, par, half))
            qev_all = sdv.n

            for hp in range(8):
                h0, h1 = 2 * hp, 2 * hp + 1
                for pi in range(NP):
                    sac.n += 1
                    M[("exp", g, h0, pi)] = sac.n
                    M[("exp", g, h1, pi)] = sac.n
                prev_win_exp = sac.n
                for pi in range(NP):
                    if pi * 2 >= 4 * g:
                        for h in (h0, h1):
                            sdv.n += 1
                            M[("mask", g, h, pi)] = sdv.n
                            dv_ops.append(("mask", g, h, pi))
                for h in (h0, h1):
                    sac.n += 2
                    M[("rlb", g, h)] = sac.n
                for h in (h0, h1):
                    sdv.n += 1
                    M[("oev", g, h)] = sdv.n
                    dv_ops.append(("oev", g, h,
                                   prev_odma if (h == 1 and g > 0) else 0))
                po_wait = (last_oev[0], last_oev[1])
                for pi in range(NP):
                    for h in (h0, h1):
                        spe.n += 1
                        M[("qk", g, h, pi)] = spe.n
                        pe_ops.append(("qk", g, gs, h, pi,
                                       qev_all if pi == 0 else 0))
                    if pi == 0 and pend_prl:
                        for op in pend_prl:
                            spe.n += 1
                            M[("prl",) + op[1:3]] = spe.n
                            pe_ops.append(op)
                        pend_prl.clear()
                    if pi >= 1:
                        for h in (h0, h1):
                            spe.n += 1
                            M[("pv", g, h, pi - 1)] = spe.n
                            pe_ops.append(("pv", g, h, pi - 1, pi - 1 == 0,
                                           False, po_wait[h % 2]))
                for h in (h0, h1):
                    spe.n += 1
                    M[("pv", g, h, NP - 1)] = spe.n
                    pe_ops.append(("pv", g, h, NP - 1, NP == 1, True,
                                   po_wait[h % 2] if NP == 1 else 0))
                for h in (h0, h1):
                    pend_prl.append(("prl", g, h, po_wait[h % 2]))
                    last_oev[h % 2] = M[("oev", g, h)]
            gp_ops.append(("odma", g, sdv.n, list(pend_prl)))
            for op in pend_prl:
                spe.n += 1
                M[("prl",) + op[1:3]] = spe.n
                pe_ops.append(op)
            pend_prl.clear()
            sg.n += 16 * 8
            prev_odma = sg.n
            ph.on("tensor", lambda e, a=wslice[0], b=len(pe_ops): _fpe(e, a, b))
            ph.on("vector", lambda e, a=wslice[1], b=len(dv_ops): _fdv(e, a, b))
            ph.on("scalar", lambda e, g=g: _fac(e, g))
            ph.on("gpsimd", lambda e, a=wslice[2], b=len(gp_ops): _fgp(e, a, b))
            wslice[0], wslice[1], wslice[2] = len(pe_ops), len(dv_ops), len(gp_ops)

        def _fpe(e, a, b):
            for op in pe_ops[a:b]:
                if op[0] == "qproj":
                    _, g, gs, c, par, half, w_pexp, w_ev, w_ld = op
                    if w_ld:
                        e.wait_ge(sg.h, w_ld)
                    if w_pexp:
                        e.wait_ge(sac.h, w_pexp)
                    if w_ev:
                        e.wait_ge(sdv.h, w_ev)
                    dst = SCP[:, (par * 2 + half) * G:(par * 2 + half + 1) * G]
                    if c < 8:
                        for qc in range(QC):
                            e.matmul(dst, wqd[:, qc * D + c * 128:qc * D + (c + 1) * 128],
                                     QCb[qc][:, gs], start=(qc == 0), stop=(qc == QC - 1))
                    else:
                        rc = c - 8
                        for qc in range(QC):
                            e.matmul(dst, wqr[:, qc * 512 + rc * 128:qc * 512 + (rc + 1) * 128],
                                     QCb[qc][:, gs], start=(qc == 0), stop=(qc == QC - 1))
                    e.last.then_inc(spe.h, 1)
                elif op[0] == "qk":
                    _, g, gs, h, pi, w_qev = op
                    hc = h // 2
                    rc4 = h // 4
                    if w_qev:
                        e.wait_ge(sdv.h, w_qev)
                    if pi >= 1:
                        e.wait_ge(sac.h, M[("exp", g, h, pi - 1)])
                    for j in range(2):
                        kt = pi * 2 + j
                        ks = slice(kt * 128, (kt + 1) * 128)
                        dst = SCP[:, (h % 2) * 2 * G + j * G:
                                  (h % 2) * 2 * G + (j + 1) * G]
                        e.matmul(dst, self.XB[hc][:, ks],
                                 QCZ[:, h * G:(h + 1) * G],
                                 start=True, stop=False)
                        e.matmul(dst, KR[rc4][:, ks],
                                 QRZ[:, h * G:(h + 1) * G],
                                 start=False, stop=True)
                    e.last.then_inc(spe.h, 1)
                elif op[0] == "pv":
                    _, g, h, pi, first, last_, w_oev = op
                    e.wait_ge(sac.h, M[("exp", g, h, pi)])
                    if first and w_oev:
                        e.wait_ge(sdv.h, w_oev)
                    pt = PT[pi % 2]
                    for j in range(2):
                        kt = pi * 2 + j
                        vs = VT[:, kt * (H * 65) + h * 65:kt * (H * 65) + (h + 1) * 65]
                        e.matmul(PO[h % 2][:], vs,
                                 pt[:, (h % 2) * 2 * G + j * G:
                                    (h % 2) * 2 * G + (j + 1) * G],
                                 start=(first and j == 0), stop=(last_ and j == 1))
                    e.last.then_inc(spe.h, 1)
                elif op[0] == "prl":
                    _, g, h, w_oev = op
                    e.wait_ge(sac.h, M[("rlb", g, h)])
                    if w_oev:
                        e.wait_ge(sdv.h, w_oev)
                    e.matmul(PRL[h % 2][:], self.prl65[:],
                             RLB[0:65, (h % 2) * G:(h % 2 + 1) * G],
                             start=True, stop=True).then_inc(spe.h, 1)
        def _fdv(e, a, b):
            for op in dv_ops[a:b]:
                if op[0] == "qevac":
                    _, g, gs, c, par, half = op
                    e.wait_ge(spe.h, M[("qp", g, c)])
                    src = SCP[:, (par * 2 + half) * G:(par * 2 + half + 1) * G]
                    if c < 8:
                        e.tensor_copy(QCZ[0:64, 2 * c * G:(2 * c + 1) * G],
                                      src[0:64, :])
                        e.tensor_copy(QCZ[64:128, (2 * c + 1) * G:(2 * c + 2) * G],
                                      src[64:128, :]).then_inc(sdv.h, 1)
                    else:
                        rc = c - 8
                        ta, tb = PT[0][:, 0:G], PT[0][:, G:2 * G]
                        e.tensor_tensor(ta, src, self.ssinT[:, gs], ALU.mult)
                        e.stream_shuffle(tb, ta, mask=mask_shuf)
                        e.tensor_tensor(ta, src, self.cosT[:, gs], ALU.mult)
                        e.tensor_tensor(tb, tb, ta, ALU.add)
                        for qb in range(4):
                            hh = 4 * rc + qb
                            e.tensor_copy(
                                QRZ[32 * qb:32 * qb + 32, hh * G:(hh + 1) * G],
                                PT[0][32 * qb:32 * qb + 32, G:2 * G])
                        e.last.then_inc(sdv.h, 1)
                elif op[0] == "mask":
                    _, g, h, pi = op
                    e.wait_ge(spe.h, M[("qk", g, h, pi)])
                    for j in range(2):
                        kt = pi * 2 + j
                        off = 384 - 128 * (kt - 4 * g)
                        sl = slice((h % 2) * 2 * G + j * G,
                                   (h % 2) * 2 * G + (j + 1) * G)
                        e.tensor_tensor(SCP[:, sl], SCP[:, sl],
                                        self.maskw[:, off:off + G], ALU.add)
                    e.last.then_inc(sdv.h, 1)
                elif op[0] == "oev":
                    _, g, h, w_odma = op
                    sel = h % 2
                    if w_odma:
                        e.wait_ge(sg.h, w_odma)
                    e.wait_ge(spe.h, M[("prl", g, h)])
                    if h % 2 == 0:
                        dst = self.XN[h // 2][0:64, g * G:(g + 1) * G]
                    else:
                        dst = OST[0:64, (h // 2) * G:(h // 2 + 1) * G]
                    ls = slice(sel * G, (sel + 1) * G)
                    e.tensor_copy(RLS[0:64, ls], PRL[sel][0:64, :])
                    e.tensor_tensor(dst, PO[sel][0:64, :], RLS[0:64, ls],
                                    ALU.mult).then_inc(sdv.h, 1)
        def _fac(e, g):
            if True:
                NP = 2 * (g + 1)
                for hp in range(8):
                    h0, h1 = 2 * hp, 2 * hp + 1
                    for pi in range(NP):
                        diag = (pi * 2 >= 4 * g)
                        if diag:
                            e.wait_ge(sdv.h, M[("mask", g, h1, pi)])
                        else:
                            e.wait_ge(spe.h, M[("qk", g, h1, pi)])
                        if pi >= 2:
                            e.wait_ge(spe.h, M[("pv", g, h1, pi - 2)])
                        e.activation(PT[pi % 2][:], SCP[:], AF.Exp,
                                     bias=self.negc[:]).then_inc(sac.h, 1)
                    for h in (h0, h1):
                        sel = h % 2
                        ls = slice(sel * G, (sel + 1) * G)
                        e.wait_ge(spe.h, M[("pv", g, h, NP - 1)])
                        e.activation(RL[64:65, ls], PO[sel][64:65, :], AF.Ln) \
                            .then_inc(sac.h, 1)
                        e.activation(RLB[64:65, ls], RL[64:65, ls], AF.Exp,
                                     scale=-1.0).then_inc(sac.h, 1)
        def _fgp(e, a, b):
            for op in gp_ops[a:b]:
                _, g, w_dv = op[:3]
                e.wait_ge(sdv.h, w_dv)
                for c in range(DC):
                    e.dma_start(out=self.XN[c][64:128, g * G:(g + 1) * G],
                                in_=OST[0:64, c * G:(c + 1) * G]) \
                        .then_inc(sg.h, 16)
        self.drain(ph, gp=True)
        ph.emit()

        self.tap_chunks(f"ot{s}", self.XN)

    def ln_stage(self, M, tag, src_bf, src_sq, n_chunks, eps_tile, hilo,
                 psig, psig2, prep, pbrep, stat, statb, pe_list, dv_list, ac_list,
                 w_src_sac, w_psig_free, w_prep_free):
        spe, sdv, sac = self.spe, self.sdv, self.sac
        nf = float(n_chunks * 128)
        spe.n += 1
        M[("stats", tag)] = spe.n
        pe_list.append(("stats", tag, src_bf, src_sq, w_src_sac, w_psig_free,
                        psig, psig2))
        sdv.n += 1
        M[("fdv1", tag)] = sdv.n
        dv_list.append(("fdv1", tag, nf, psig, psig2, stat))
        sac.n += 1
        M[("rstd", tag)] = sac.n
        ac_list.append(("rstd", tag, eps_tile, stat))
        sdv.n += 1
        M[("statb", tag)] = sdv.n
        dv_list.append(("statb", tag, hilo, stat, statb))

    def ln_stage_repl(self, M, tag, hilo, statb, prep, pbrep, pe_list, w_prep_free):
        spe = self.spe
        spe.n += 1
        M[("repl", tag)] = spe.n
        pe_list.append(("repl", tag, hilo, statb, prep, pbrep, w_prep_free))

    def emit_ln_pe(self, e, op, M):
        spe, sdv, sac = self.spe, self.sdv, self.sac
        if op[0] == "stats":
            _, tag, src_bf, src_sq, w_sac, w_free, psig, psig2 = op
            e.wait_ge(sac.h, w_sac if not callable(w_sac) else w_sac())
            wf = w_free() if callable(w_free) else w_free
            if wf:
                e.wait_ge(sdv.h, wf)
            n = len(src_bf)
            for c in range(n):
                e.matmul(psig[:], self.ones_col[:], src_bf[c],
                         start=(c == 0), stop=(c == n - 1))
            for c in range(n):
                e.matmul(psig2[:], self.ones_col[:], src_sq[c],
                         start=(c == 0), stop=(c == n - 1))
            e.last.then_inc(spe.h, 1)
        elif op[0] == "repl":
            _, tag, hilo, statb, prep, pbrep, w_prep = op
            e.wait_ge(sdv.h, M[("statb", tag)])
            wp = w_prep() if callable(w_prep) else w_prep
            if wp:
                e.wait_ge(sdv.h, wp)
            if hilo:
                e.matmul(prep[:], self.ones_row[0:1, :], statb[:, 0:G],
                         start=True, stop=False)
                e.matmul(prep[:], self.ones_row[0:1, :], statb[:, 2 * G:3 * G],
                         start=False, stop=True)
                e.matmul(pbrep[:], self.ones_row[0:1, :], statb[:, G:2 * G],
                         start=True, stop=False)
                e.matmul(pbrep[:], self.ones_row[0:1, :], statb[:, 3 * G:4 * G],
                         start=False, stop=True).then_inc(spe.h, 1)
            else:
                e.matmul(prep[:], self.ones_row[0:1, :], statb[:, 0:G],
                         start=True, stop=True)
                e.matmul(pbrep[:], self.ones_row[0:1, :], statb[:, G:2 * G],
                         start=True, stop=True).then_inc(spe.h, 1)

    def emit_ln_dv(self, e, op, M):
        spe, sdv, sac = self.spe, self.sdv, self.sac
        if op[0] == "fdv1":
            _, tag, nf, psig, psig2, stat = op
            e.wait_ge(spe.h, M[("stats", tag)])
            e.tensor_scalar_mul(stat[:, 0:G], psig[:], 1.0 / nf)
            e.tensor_scalar_mul(stat[:, G:2 * G], psig2[:], 1.0 / nf)
            e.tensor_tensor(stat[:, 2 * G:3 * G], stat[:, 0:G], stat[:, 0:G],
                            ALU.mult)
            e.tensor_tensor(stat[:, G:2 * G], stat[:, G:2 * G],
                            stat[:, 2 * G:3 * G], ALU.subtract).then_inc(sdv.h, 1)
        elif op[0] == "statb":
            _, tag, hilo, stat, statb = op
            e.wait_ge(sac.h, M[("rstd", tag)])
            if hilo:
                e.tensor_tensor(stat[:, 3 * G:4 * G], stat[:, 0:G],
                                stat[:, 2 * G:3 * G], ALU.mult)
                e.tensor_scalar_mul(stat[:, 3 * G:4 * G], stat[:, 3 * G:4 * G], -1.0)
                e.tensor_copy(statb[:, 0:G], stat[:, 2 * G:3 * G])
                e.tensor_copy(statb[:, G:2 * G], stat[:, 3 * G:4 * G])
                e.tensor_tensor(stat[:, 4 * G:5 * G], stat[:, 2 * G:3 * G],
                                statb[:, 0:G], ALU.subtract)
                e.tensor_tensor(stat[:, 5 * G:6 * G], stat[:, 3 * G:4 * G],
                                statb[:, G:2 * G], ALU.subtract)
                e.tensor_copy(statb[:, 2 * G:3 * G], stat[:, 4 * G:5 * G])
                e.tensor_copy(statb[:, 3 * G:4 * G], stat[:, 5 * G:6 * G]) \
                    .then_inc(sdv.h, 1)
            else:
                e.tensor_copy(statb[:, 0:G], stat[:, 2 * G:3 * G])
                e.tensor_tensor(stat[:, 3 * G:4 * G], stat[:, 0:G],
                                stat[:, 2 * G:3 * G], ALU.mult)
                e.tensor_scalar_mul(stat[:, 3 * G:4 * G], stat[:, 3 * G:4 * G], -1.0)
                e.tensor_copy(statb[:, G:2 * G], stat[:, 3 * G:4 * G]) \
                    .then_inc(sdv.h, 1)

    def emit_ln_ac(self, e, op, M):
        sdv, sac = self.sdv, self.sac
        _, tag, eps_tile, stat = op
        e.wait_ge(sdv.h, M[("fdv1", tag)])
        e.activation(stat[:, 3 * G:4 * G], stat[:, G:2 * G], AF.Ln, bias=eps_tile[:])
        e.activation(stat[:, 2 * G:3 * G], stat[:, 3 * G:4 * G], AF.Exp,
                     scale=-0.5).then_inc(sac.h, 1)

    def wo_ln2_ln3(self, s):
        nc = self.nc
        sq, sg, spe, sdv, sac = self.sq, self.sg, self.spe, self.sdv, self.sac
        with contextlib.ExitStack() as est:
            E = est.enter_context
            wo = E(nc.sbuf_tensor(_u("o_wo"), [128, DC * D], BF16))
            OBF = [E(nc.sbuf_tensor(_u(f"o_obf{i}"), [128, DC * G], BF16))
                   for i in range(2)]
            OSQ = [E(nc.sbuf_tensor(_u(f"o_osq{i}"), [128, DC * G], BF16))
                   for i in range(2)]
            YG = [E(nc.sbuf_tensor(_u(f"o_yg{i}"), [128, DC * G], F32))
                  for i in range(2)]
            YBF = E(nc.sbuf_tensor(_u("o_ybf"), [128, DC * G], BF16))
            YSQ = E(nc.sbuf_tensor(_u("o_ysq"), [128, DC * G], BF16))
            stat = E(nc.sbuf_tensor(_u("o_stat"), [1, 6 * G], F32))
            statb = E(nc.sbuf_tensor(_u("o_statb"), [1, 4 * G], BF16))
            tmp = E(nc.sbuf_tensor(_u("o_tmp"), [128, G], F32))
            po = [E(nc.psum_tensor(_u(f"o_po{i}"), [128, G], F32)) for i in range(2)]
            psig = E(nc.psum_tensor(_u("o_psig"), [1, G], F32))
            psig2 = E(nc.psum_tensor(_u("o_psig2"), [1, G], F32))
            prep = E(nc.psum_tensor(_u("o_prep"), [128, G], F32))
            pbrep = E(nc.psum_tensor(_u("o_pbrep"), [128, G], F32))

            ph = Phase(nc, f"wo_{s}")
            M = {}
            pe_ops, dv_ops, ac_ops, gp_ops = [], [], [], []

            def fgp0(e):
                self.wload(e, wo, self.d_wo, D, sg.h)
            sg.n += 16
            wo_ld = sg.n
            ph.on("gpsimd", fgp0)

            def sched_chains(g):
                for c in range(DC):
                    spe.n += 1
                    M[("woc", g, c)] = spe.n
                    pe_ops.append(("woc", g, c,
                                   wo_ld if (g == 0 and c == 0) else 0))
                    sac.n += 1
                    M[("oev", g, c)] = sac.n
                    ac_ops.append(("oev", g, c))

            def sched_stage(kind, g):
                tag = (s, kind, g)
                if kind == 2:
                    src_bf = [OBF[g % 2][:, c * G:(c + 1) * G] for c in range(DC)]
                    src_sq = [OSQ[g % 2][:, c * G:(c + 1) * G] for c in range(DC)]
                    w_sac = M[("oev", g, DC - 1)]
                else:
                    src_bf = [YBF[:, c * G:(c + 1) * G] for c in range(DC)]
                    src_sq = [YSQ[:, c * G:(c + 1) * G] for c in range(DC)]
                    w_sac = M[("ysq", g)]
                w_psig = self._last_fdv1[0]
                w_prep = self._last_apply[0]
                self.ln_stage(M, tag, src_bf, src_sq, DC, self.eps_ln, True,
                              psig, psig2, prep, pbrep, stat, statb,
                              pe_ops_stats, dv_ops, ac_ops, w_sac, w_psig, w_prep)
                self.ln_stage_repl(M, tag, True, statb, prep, pbrep,
                                   pe_ops_stats, w_prep)
                self._last_fdv1[0] = M[("fdv1", tag)]
                if kind == 2:
                    sdv.n += 1
                    M[("applyu", g)] = sdv.n
                    dv_ops.append(("applyu", g))
                    self._last_apply[0] = sdv.n
                    gp_ops.append(("xacc", g, sdv.n))
                    self.sgp2[g % 2].n += 16 * DC
                    M[("xacc", g)] = self.sgp2[g % 2].n
                    sdv.n += 1
                    M[("ybf", g)] = sdv.n
                    dv_ops.append(("ybf", g))
                    sac.n += 1
                    M[("ysq", g)] = sac.n
                    ac_ops.append(("ysq", g))
                else:
                    sdv.n += 1
                    M[("applyz", g)] = sdv.n
                    dv_ops.append(("applyz", g))
                    self._last_apply[0] = sdv.n
                    gp_ops.append(("yspill", g, sdv.n))
                    self.sgp2[g % 2].n += 16 * DC
                    M[("yspill", g)] = self.sgp2[g % 2].n

            self._last_fdv1 = [0]
            self._last_apply = [0]
            pe_ops_stats = pe_ops
            sched_chains(0)
            sched_chains(1)
            sched_stage(2, 0)
            sched_chains(2)
            sched_stage(2, 1)
            sched_stage(3, 0)
            sched_chains(3)
            sched_stage(2, 2)
            sched_stage(3, 1)
            sched_stage(2, 3)
            sched_stage(3, 2)
            sched_stage(3, 3)

            def fpe(e, ops=tuple(pe_ops)):
                for op in ops:
                    if op[0] == "woc":
                        _, g, c, w_ld = op
                        if w_ld:
                            e.wait_ge(sg.h, w_ld)
                        gc = g * DC + c
                        if gc >= 2:
                            pg, pc = divmod(gc - 2, DC)
                            e.wait_ge(sac.h, M[("oev", pg, pc)])
                        gs = slice(g * G, (g + 1) * G)
                        for k in range(DC):
                            e.matmul(po[c % 2][:],
                                     wo[:, k * D + c * 128:k * D + (c + 1) * 128],
                                     self.XN[k][:, gs], start=(k == 0),
                                     stop=(k == DC - 1))
                        e.last.then_inc(spe.h, 1)
                    else:
                        self.emit_ln_pe(e, op, M)
            ph.on("tensor", fpe)

            def fac(e, ops=tuple(ac_ops)):
                for op in ops:
                    if op[0] == "oev":
                        _, g, c = op
                        e.wait_ge(spe.h, M[("woc", g, c)])
                        if ("stats", (s, 2, g - 2)) in M:
                            e.wait_ge(spe.h, M[("stats", (s, 2, g - 2))])
                        e.activation(OBF[g % 2][:, c * G:(c + 1) * G], po[c % 2][:],
                                     AF.Copy)
                        e.activation(OSQ[g % 2][:, c * G:(c + 1) * G], po[c % 2][:],
                                     AF.Square).then_inc(sac.h, 1)
                    elif op[0] == "ysq":
                        _, g = op
                        e.wait_ge(sdv.h, M[("ybf", g)])
                        if ("stats", (s, 3, g - 1)) in M:
                            e.wait_ge(spe.h, M[("stats", (s, 3, g - 1))])
                        e.activation(YSQ[:], YBF[:], AF.Square).then_inc(sac.h, 1)
                    else:
                        self.emit_ln_ac(e, op, M)
            ph.on("scalar", fac)

            def fdv(e, ops=tuple(dv_ops)):
                for op in ops:
                    if op[0] == "applyu":
                        _, g = op
                        e.wait_ge(spe.h, M[("repl", (s, 2, g))])
                        if ("yspill", g - 2) in M:
                            e.wait_ge(self.sgp2[g % 2].h, M[("yspill", g - 2)])
                        for c in range(DC):
                            e.tensor_tensor(tmp[:], OBF[g % 2][:, c * G:(c + 1) * G],
                                            prep[:], ALU.mult)
                            e.tensor_tensor(YG[g % 2][:, c * G:(c + 1) * G], tmp[:],
                                            pbrep[:], ALU.add)
                        e.last.then_inc(sdv.h, 1)
                    elif op[0] == "ybf":
                        _, g = op
                        e.wait_ge(self.sgp2[g % 2].h, M[("xacc", g)])
                        if ("stats", (s, 3, g - 1)) in M:
                            e.wait_ge(spe.h, M[("stats", (s, 3, g - 1))])
                        e.tensor_copy(YBF[:], YG[g % 2][:]).then_inc(sdv.h, 1)
                    elif op[0] == "applyz":
                        _, g = op
                        gs = slice(g * G, (g + 1) * G)
                        e.wait_ge(spe.h, M[("repl", (s, 3, g))])
                        for c in range(DC):
                            e.tensor_tensor(tmp[:], YG[g % 2][:, c * G:(c + 1) * G],
                                            prep[:], ALU.mult)
                            e.tensor_tensor(self.XN[c][:, gs], tmp[:], pbrep[:],
                                            ALU.add)
                        e.last.then_inc(sdv.h, 1)
                    else:
                        self.emit_ln_dv(e, op, M)
            ph.on("vector", fdv)

            def fgp(e, ops=tuple(gp_ops)):
                for op in ops:
                    kind, g, w = op
                    e.wait_ge(sdv.h, w)
                    gs = slice(g * G, (g + 1) * G)
                    for c in range(DC):
                        if kind == "xacc":
                            e.dma_start(out=YG[g % 2][:, c * G:(c + 1) * G],
                                        in_=self.d_xT[s, c][:, gs],
                                        accum_op=ALU.add) \
                                .then_inc(self.sgp2[g % 2].h, 16)
                        else:
                            e.dma_start(out=self.d_xT[s, c][:, gs],
                                        in_=YG[g % 2][:, c * G:(c + 1) * G]) \
                                .then_inc(self.sgp2[g % 2].h, 16)
            ph.on("gpsimd", fgp)
            self.drain(ph, gp=True)
            ph.emit()

        self.tap_chunks(f"zn{s}", self.XN)

    def ffn_w1(self, s):
        nc = self.nc
        sq, sg, spe, sdv, sac = self.sq, self.sg, self.spe, self.sdv, self.sac
        with contextlib.ExitStack() as est:
            E = est.enter_context
            WQT = E(nc.sbuf_tensor(_u("f_wqt1"), [128, DC * DFF], BF16))
            HB = [E(nc.sbuf_tensor(_u(f"f_hb{i}"), [128, FC * G], BF16))
                  for i in range(2)]
            ph1 = [E(nc.psum_tensor(_u(f"f_ph{i}"), [128, G], F32)) for i in range(4)]
            ph = Phase(nc, f"w1_{s}")
            M = {}
            pe_ops, ac_ops, gp_ops = [], [], []

            def fs0(e):
                e.dma_start(out=WQT[:].rearrange("p (c f) -> p c f", f=DFF),
                            in_=self.d_wq1t.rearrange("c p f -> p c f")) \
                    .then_inc(sq.h, 16)
            sq.n += 16
            w_ld = sq.n
            ph.on("sync", fs0)

            for g in range(NG):
                for f in range(FC):
                    spe.n += 1
                    M[("w1c", g, f)] = spe.n
                    pe_ops.append(("w1c", g, f, w_ld if (g == 0 and f == 0) else 0))
                    sac.n += 1
                    M[("hev", g, f)] = sac.n
                    ac_ops.append(("hev", g, f))
                gp_ops.append(("hspill", g, sac.n))
                self.sgp2[g % 2].n += 16
                M[("hspill", g)] = self.sgp2[g % 2].n

            def fpe(e, ops=tuple(pe_ops)):
                for op in ops:
                    _, g, f, w_ld_ = op
                    if w_ld_:
                        e.wait_ge(sq.h, w_ld_)
                    gf = g * FC + f
                    if gf >= 4:
                        pg, pf = divmod(gf - 4, FC)
                        e.wait_ge(sac.h, M[("hev", pg, pf)])
                    gs = slice(g * G, (g + 1) * G)
                    for k in range(DC):
                        e.matmul(ph1[f % 4][:],
                                 WQT[:, k * DFF + f * 128:k * DFF + (f + 1) * 128],
                                 self.XN[k][:, gs], start=(k == 0), stop=(k == DC - 1))
                    e.last.then_inc(spe.h, 1)
            ph.on("tensor", fpe)

            def fac(e, ops=tuple(ac_ops)):
                for op in ops:
                    _, g, f = op
                    e.wait_ge(spe.h, M[("w1c", g, f)])
                    if ("hspill", g - 2) in M and f == 0:
                        e.wait_ge(self.sgp2[g % 2].h, M[("hspill", g - 2)])
                    e.activation(HB[g % 2][:, f * G:(f + 1) * G], ph1[f % 4][:],
                                 AF.Relu, bias=self.b1l[:, f:f + 1]) \
                        .then_inc(sac.h, 1)
            ph.on("scalar", fac)

            def fgp(e, ops=tuple(gp_ops)):
                for op in ops:
                    _, g, w = op
                    e.wait_ge(sac.h, w)
                    gs = slice(g * G, (g + 1) * G)
                    e.dma_start(out=self.d_h[:, :, gs].rearrange("f p j -> p f j"),
                                in_=HB[g % 2][:].rearrange("p (f j) -> p f j", j=G)) \
                        .then_inc(self.sgp2[g % 2].h, 16)
            ph.on("gpsimd", fgp)
            self.drain(ph, gp=True)
            ph.emit()

    def ffn_w2(self, s):
        nc = self.nc
        sq, sg, spe, sdv, sac = self.sq, self.sg, self.spe, self.sdv, self.sac
        with contextlib.ExitStack() as est:
            E = est.enter_context
            WQT = E(nc.sbuf_tensor(_u("f_wqt2"), [128, FC * D], BF16))
            HB = [E(nc.sbuf_tensor(_u(f"f_hb2{i}"), [128, FC * G], BF16))
                  for i in range(2)]
            FBF = E(nc.sbuf_tensor(_u("f_fbf"), [128, DC * G], BF16))
            FSQ = E(nc.sbuf_tensor(_u("f_fsq"), [128, DC * G], BF16))
            FOUT = E(nc.sbuf_tensor(_u("f_fout"), [128, DC * G], F32))
            OUTT = E(nc.sbuf_tensor(_u("f_outt"), [128, D], F32))
            stat = E(nc.sbuf_tensor(_u("f_stat"), [1, 6 * G], F32))
            statb = E(nc.sbuf_tensor(_u("f_statb"), [1, 4 * G], BF16))
            tmp = E(nc.sbuf_tensor(_u("f_tmp"), [128, G], F32))
            pf = [E(nc.psum_tensor(_u(f"f_pf{i}"), [128, G], F32)) for i in range(2)]
            psig = E(nc.psum_tensor(_u("f_psig"), [1, G], F32))
            psig2 = E(nc.psum_tensor(_u("f_psig2"), [1, G], F32))
            prep = E(nc.psum_tensor(_u("f_prep"), [128, G], F32))
            pbrep = E(nc.psum_tensor(_u("f_pbrep"), [128, G], F32))
            ptr = [E(nc.psum_tensor(_u(f"f_ptr{i}"), [128, 128], F32)) for i in range(2)]

            ph = Phase(nc, f"w2_{s}")
            M = {}
            if not hasattr(self, "_dbgM"):
                self._dbgM = {}
            self._dbgM[s] = M
            pe_ops, dv_ops, ac_ops, gp_ops, sy_ops = [], [], [], [], []
            self._last_fdv1 = [0]
            self._last_apply = [0]

            sq.n += 16
            M[("wld",)] = sq.n
            sy_ops.append(("wld",))

            def sched_hload(g):
                self.sqp[g % 2].n += 16
                M[("hld", g)] = self.sqp[g % 2].n
                sy_ops.append(("hld", g))

            def sched_chains(g, mid=None):
                for c in range(DC):
                    spe.n += 1
                    M[("w2c", g, c)] = spe.n
                    pe_ops.append(("w2c", g, c))
                    sac.n += 1
                    M[("fev", g, c)] = sac.n
                    ac_ops.append(("fev", g, c))
                    if c == 1 and mid is not None:
                        mid()

            def sched_stats4(g):
                tag = (s, 4, g)
                src_bf = [FBF[:, c * G:(c + 1) * G] for c in range(DC)]
                src_sq = [FSQ[:, c * G:(c + 1) * G] for c in range(DC)]
                self.ln_stage(M, tag, src_bf, src_sq, DC, self.eps_ln, True,
                              psig, psig2, prep, pbrep, stat, statb,
                              pe_ops, dv_ops, ac_ops,
                              M[("fev", g, DC - 1)], self._last_fdv1[0],
                              self._last_apply[0])
                self._last_fdv1[0] = M[("fdv1", tag)]

            def sched_repl4(g):
                self.ln_stage_repl(M, (s, 4, g), True, statb, prep, pbrep,
                                   pe_ops, self._last_apply[0])

            def sched_applyf(g):
                sdv.n += 1
                M[("applyf", g)] = sdv.n
                dv_ops.append(("applyf", g))
                self._last_apply[0] = sdv.n
                gp_ops.append(("yacc", g, sdv.n))
                self.sgp2[g % 2].n += 16 * DC
                M[("yacc", g)] = self.sgp2[g % 2].n

            def sched_egress(g):
                for st in range(4):
                    for c in range(DC):
                        spe.n += 1
                        M[("etr", g, st, c)] = spe.n
                        pe_ops.append(("etr", g, st, c))
                        sdv.n += 1
                        M[("eev", g, st, c)] = sdv.n
                        dv_ops.append(("eev", g, st, c))
                    self.sst.n += 16
                    M[("store", g, st)] = self.sst.n
                    sy_ops.append(("store", g, st))

            sched_hload(0)
            sched_chains(0)
            sched_hload(1)
            sched_stats4(0)
            sched_chains(1, mid=lambda: sched_repl4(0))
            sched_applyf(0)
            sched_hload(2)
            sched_stats4(1)
            sched_chains(2, mid=lambda: (sched_repl4(1), sched_egress(0)))
            sched_applyf(1)
            sched_hload(3)
            sched_stats4(2)
            sched_chains(3, mid=lambda: (sched_repl4(2), sched_egress(1)))
            sched_applyf(2)
            sched_stats4(3)
            sched_repl4(3)
            sched_egress(2)
            sched_applyf(3)
            sched_egress(3)

            def fpe(e, ops=tuple(pe_ops)):
                for op in ops:
                    if op[0] == "w2c":
                        _, g, c = op
                        if g == 0 and c == 0:
                            e.wait_ge(sq.h, M[("wld",)])
                        e.wait_ge(self.sqp[g % 2].h, M[("hld", g)])
                        gc = g * DC + c
                        if gc >= 2:
                            pg, pc = divmod(gc - 2, DC)
                            e.wait_ge(sac.h, M[("fev", pg, pc)])
                        for f in range(FC):
                            e.matmul(pf[c % 2][:],
                                     WQT[:, f * D + c * 128:f * D + (c + 1) * 128],
                                     HB[g % 2][:, f * G:(f + 1) * G],
                                     start=(f == 0), stop=(f == FC - 1))
                        e.last.then_inc(spe.h, 1)
                    elif op[0] == "etr":
                        _, g, st, c = op
                        k = (g * 4 + st) * DC + c
                        if st == 0 and c == 0:
                            e.wait_ge(self.sgp2[g % 2].h, M[("yacc", g)])
                        if k >= 2:
                            pg2 = k - 2
                            gg, rem = divmod(pg2, 4 * DC)
                            pst_, pc = divmod(rem, DC)
                            e.wait_ge(sdv.h, M[("eev", gg, pst_, pc)])
                        e.transpose(ptr[k % 2][:],
                                    FOUT[:, c * G + st * 128:c * G + (st + 1) * 128],
                                    self.identf[:]).then_inc(spe.h, 1)
                    else:
                        self.emit_ln_pe(e, op, M)
            ph.on("tensor", fpe)

            def fac(e, ops=tuple(ac_ops)):
                for op in ops:
                    if op[0] == "fev":
                        _, g, c = op
                        e.wait_ge(spe.h, M[("w2c", g, c)])
                        if g >= 1 and c == 0:
                            e.wait_ge(sdv.h, M[("applyf", g - 1)])
                        e.activation(FBF[:, c * G:(c + 1) * G], pf[c % 2][:],
                                     AF.Identity, bias=self.b2l[:, c:c + 1])
                        e.activation(FSQ[:, c * G:(c + 1) * G], pf[c % 2][:],
                                     AF.Square, bias=self.b2l[:, c:c + 1]) \
                            .then_inc(sac.h, 1)
                    else:
                        self.emit_ln_ac(e, op, M)
            ph.on("scalar", fac)

            def fdv(e, ops=tuple(dv_ops)):
                for op in ops:
                    if op[0] == "applyf":
                        _, g = op
                        e.wait_ge(spe.h, M[("repl", (s, 4, g))])
                        for c in range(DC):
                            e.tensor_tensor(tmp[:], FBF[:, c * G:(c + 1) * G],
                                            prep[:], ALU.mult)
                            e.tensor_tensor(FOUT[:, c * G:(c + 1) * G], tmp[:],
                                            pbrep[:], ALU.add)
                        e.last.then_inc(sdv.h, 1)
                    elif op[0] == "eev":
                        _, g, st, c = op
                        e.wait_ge(spe.h, M[("etr", g, st, c)])
                        if c == 0:
                            prev = M.get(("store", g, st - 1)) if st >= 1 \
                                else M.get(("store", g - 1, 3), 0)
                            if prev:
                                e.wait_ge(self.sst.h, prev)
                        e.tensor_copy(OUTT[:, c * 128:(c + 1) * 128],
                                      ptr[((g * 4 + st) * DC + c) % 2][:]) \
                            .then_inc(sdv.h, 1)
                    else:
                        self.emit_ln_dv(e, op, M)
            ph.on("vector", fdv)

            def fgp(e, ops=tuple(gp_ops)):
                for op in ops:
                    _, g, w = op
                    e.wait_ge(sdv.h, w)
                    gs = slice(g * G, (g + 1) * G)
                    for c in range(DC):
                        e.dma_start(out=FOUT[:, c * G:(c + 1) * G],
                                    in_=self.d_xT[s, c][:, gs],
                                    accum_op=ALU.add) \
                            .then_inc(self.sgp2[g % 2].h, 16)
            ph.on("gpsimd", fgp)

            def fsy(e, ops=tuple(sy_ops)):
                for op in ops:
                    if op[0] == "wld":
                        e.dma_start(out=WQT[:].rearrange("p (c f) -> p c f", f=D),
                                    in_=self.d_wq2t.rearrange("c p f -> p c f")) \
                            .then_inc(sq.h, 16)
                    elif op[0] == "hld":
                        _, g = op
                        if g >= 2:
                            e.wait_ge(spe.h, M[("w2c", g - 2, DC - 1)])
                        gs = slice(g * G, (g + 1) * G)
                        e.dma_start(out=HB[g % 2][:]
                                    .rearrange("p (f j) -> p f j", j=G),
                                    in_=self.d_h[:, :, gs]
                                    .rearrange("f p j -> p f j")) \
                            .then_inc(self.sqp[g % 2].h, 16)
                    elif op[0] == "store":
                        _, g, st = op
                        for c in range(DC):
                            e.wait_ge(sdv.h, M[("eev", g, st, c)])
                        e.dma_start(out=self.d_out[s, g * G + st * 128:
                                                   g * G + (st + 1) * 128, :],
                                    in_=OUTT[:]).then_inc(self.sst.h, 16)
            ph.on("sync", fsy)
            self.drain(ph, st=True, gp=True)
            ph.emit()

    def finale(self):
        ph = Phase(self.nc, "fin")
        sst = self.sst

        def fs(e):
            e.wait_ge(sst.h, sst.n)
        ph.on("sync", fs)
        ph.emit()

    def build(self, upto=None):
        nc = self.nc
        self.p0()
        if upto != "p0":
            do_q = upto not in ("ingest", "ln1", "proj", "attn", "wo")
            if do_q:
                self._qes = contextlib.ExitStack()
                self.quant_init(self._qes)
            for s in range(S):
                with contextlib.ExitStack() as xbst:
                    self.XN = [xbst.enter_context(
                        nc.sbuf_tensor(_u(f"XN{c}"), [128, T], BF16))
                        for c in range(DC)]
                    self.XB = [xbst.enter_context(
                        nc.sbuf_tensor(_u(f"XB{c}"), [128, T], BF16))
                        for c in range(DC)]
                    with contextlib.ExitStack() as sst_:
                        XT = [sst_.enter_context(
                            nc.sbuf_tensor(_u(f"XT{c}"), [128, T], F32))
                            for c in range(DC)]
                        self.ingest_ln1(s, XT, do_ln=(upto != "ingest"))
                    if upto in ("ln1", "ingest"):
                        continue
                    with contextlib.ExitStack() as pst:
                        E = pst.enter_context
                        QCb = [E(nc.sbuf_tensor(_u(f"QC{i}"), [128, T], BF16))
                               for i in range(QC)]
                        KR = [E(nc.sbuf_tensor(_u(f"KR{i}"), [128, T], BF16))
                              for i in range(4)]
                        VT = E(nc.sbuf_tensor(_u("VTt"), [128, NT * H * 65], BF16))
                        self.projections(s, pst, QCb, KR, VT)
                        if upto == "proj":
                            continue
                        with contextlib.ExitStack() as ast:
                            self.attention(s, ast, QCb, KR, VT)
                    if upto in ("attn", "ln1", "ingest", "proj"):
                        continue
                    self.wo_ln2_ln3(s)
                    if upto == "wo":
                        continue
                    self.ffn_w1(s)
                if upto in ("attn", "ln1", "ingest", "proj", "wo"):
                    continue
                self.ffn_w2(s)
        self.finale()
        return nc


def _host_consts():
    inv_freq = 1.0 / 10000.0 ** (np.arange(0, RD, 2, dtype=np.float32) / RD)
    t = np.arange(T, dtype=np.float32)
    freqs = np.outer(t, inv_freq)
    emb = np.concatenate([freqs, freqs], -1)
    cos = np.cos(emb).T.astype(np.float32)
    sin = np.sin(emb).T.astype(np.float32)
    ssin = np.concatenate([-sin[:RD // 2], sin[RD // 2:]], 0)
    ssin = ssin[(np.arange(RD) + RD // 2) % RD]
    cos128 = np.tile(cos, (4, 1)).astype(ml_dtypes.bfloat16)
    ssin128 = np.tile(ssin, (4, 1)).astype(ml_dtypes.bfloat16)
    ku = np.arange(128)[:, None]
    uu = np.arange(896)[None, :]
    maskw = np.where(uu < ku + 384, np.float32(MASKNEG), np.float32(0.0))
    return {
        "identf": np.eye(128, dtype=np.float32),
        "identb": np.eye(128, dtype=ml_dtypes.bfloat16),
        "cosT": np.ascontiguousarray(cos128),
        "ssinT": np.ascontiguousarray(ssin128),
        "maskw": maskw.astype(ml_dtypes.bfloat16),
    }


def _host_inputs(L0_attn_x, L0_attn_W_kv, L0_attn_g_kv, L0_attn_W_kdec,
                 L0_attn_W_vdec, L0_attn_W_q, L0_attn_W_qdec, L0_attn_W_krope,
                 L0_attn_W_qrope, L0_attn_W_o, L0_ffn_W1, L0_ffn_b1, L0_ffn_W2,
                 L0_ffn_b2):
    x = np.ascontiguousarray(np.asarray(L0_attn_x, dtype=np.float32))
    f32 = lambda a: np.ascontiguousarray(np.asarray(a, np.float32))
    w = {
        "wkv": f32(L0_attn_W_kv),
        "gkv": np.ascontiguousarray(f32(L0_attn_g_kv).reshape(KVC, 128).T),
        "wkdec": f32(L0_attn_W_kdec),
        "wvdec": f32(L0_attn_W_vdec),
        "wq": f32(L0_attn_W_q),
        "wqdec": f32(L0_attn_W_qdec),
        "wkrope": f32(L0_attn_W_krope),
        "wqrope": f32(L0_attn_W_qrope),
        "wo": f32(L0_attn_W_o),
        "w1": f32(L0_ffn_W1),
        "b1l": np.ascontiguousarray(f32(L0_ffn_b1).reshape(FC, 128).T),
        "w2": f32(L0_ffn_W2),
        "b2l": np.ascontiguousarray(f32(L0_ffn_b2).reshape(DC, 128).T),
    }
    w.update(_host_consts())
    in_maps = []
    for c in range(8):
        m = dict(w)
        m["x"] = np.ascontiguousarray(x[2 * c:2 * c + 2])
        in_maps.append(m)
    return in_maps


def kernel(**inputs):
    in_maps = _host_inputs(**inputs)
    mk = MK(_TAPS)
    nc = mk.build()
    res = run_bass_kernel_spmd(nc, in_maps, core_ids=list(range(8)))
    mk.es.close()
    out = np.stack([res.results[c]["out"] for c in range(8)], 0)
    return out.reshape(16, T, D).astype(np.float32)


# version 13
# speedup vs baseline: 1.1931x; 1.1041x over previous
import numpy as np
import ml_dtypes
import contextlib
from collections import defaultdict

import concourse.bass as bass
import concourse.mybir as mybir
from concourse.bass_utils import run_bass_kernel_spmd

F32 = mybir.dt.float32
BF16 = mybir.dt.bfloat16
AF = mybir.ActivationFunctionType
ALU = mybir.AluOpType

S = 2
T = 2048
D = 1024
DC = D // 128
H = 16
HD = 64
RD = 32
KVR = 256
KVC = KVR // 128
QR = 512
QC = QR // 128
DFF = 4096
FC = DFF // 128
G = 512
NG = T // G
NT = T // 128
EXPC = 11.0
MASKNEG = -60.0
SCALE_Q = float((HD + RD) ** -0.5)
LSC = float(2.0 ** 20)

_TAPS = ()
_DBG = set()

_UID = [0]


def _u(name):
    _UID[0] += 1
    return f"{name}_u{_UID[0]}"



class Sem:
    def __init__(self, es, nc, name):
        self.h = es.enter_context(nc.semaphore(name))
        self.n = 0


_DATAPATH = ("matmul", "transpose", "tensor", "activation", "memset", "memzero",
             "reciprocal", "stream_shuffle", "scalar_tensor_tensor", "dma_start",
             "reduce", "iota", "select", "ldweights", "copy", "square", "sign",
             "mul", "add", "bn_", "pool", "max", "sqrt")
_WRITE_KW = ("out", "accum_out", "out_max", "out_indices", "dest")


def _ap_name(x):
    t = getattr(x, "tensor", None)
    return getattr(t, "name", None) if t is not None else None


class _EngRec:

    def __init__(self, eng):
        self._e = eng
        self.last = None
        self._track = getattr(eng, "engine", None) not in (
            mybir.EngineType.PE, mybir.EngineType.SP)
        self._w = set()
        self._r = set()

    def __getattr__(self, name):
        a = getattr(self._e, name)
        if callable(a) and any(name.startswith(p) for p in _DATAPATH):
            def call(*args, __a=a, __n=name, **kw):
                if self._track:
                    is_dma = __n.startswith("dma_start")
                    wr, rd = [], []
                    for i, x in enumerate(args):
                        nm = _ap_name(x)
                        if nm is None:
                            continue
                        (wr if (i == 0 and not is_dma) else rd).append(nm)
                    for k, x in kw.items():
                        nm = _ap_name(x)
                        if nm is None:
                            continue
                        (wr if (k in _WRITE_KW and not is_dma) else rd).append(nm)
                    conflict = (any(n in self._w for n in rd)
                                or any(n in self._w or n in self._r for n in wr))
                    if conflict:
                        self._e.drain()
                        self._w.clear()
                        self._r.clear()
                    if not is_dma:
                        self._w.update(wr)
                    self._r.update(rd)
                r = __a(*args, **kw)
                if hasattr(r, "then_inc"):
                    self.last = r
                return r
            return call
        return a


class Phase:

    def __init__(self, nc, name=""):
        self.nc = nc
        self.name = name
        self.streams = defaultdict(list)

    def on(self, engine, fn):
        self.streams[engine].append(fn)

    def emit(self):
        if not self.streams:
            return
        with self.nc.Block() as blk:
            for engine in ("sync", "gpsimd", "tensor", "scalar", "vector"):
                fns = self.streams.get(engine)
                if not fns:
                    continue

                def runner(eng, fns=tuple(fns)):
                    rec = _EngRec(eng)
                    for f in fns:
                        f(rec)

                getattr(blk, engine)(runner)


class MK:
    def __init__(self, taps=()):
        self.taps = set(taps)
        self.nc = bass.Bass("TRN2", target_bir_lowering=False)
        self.es = contextlib.ExitStack()
        self.tap_d = {}
        self._declare_dram()
        self._alloc_persistent()

    def _declare_dram(self):
        nc = self.nc
        I, O, N = "ExternalInput", "ExternalOutput", "Internal"
        ap = lambda *a, **k: nc.dram_tensor(*a, **k).ap()
        self.d_x = ap("x", [S, T, D], F32, kind=I)
        self.d_wkv = ap("wkv", [D, KVR], F32, kind=I)
        self.d_gkv = ap("gkv", [128, KVC], F32, kind=I)
        self.d_wkdec = ap("wkdec", [KVR, D], F32, kind=I)
        self.d_wvdec = ap("wvdec", [KVR, D], F32, kind=I)
        self.d_wq = ap("wq", [D, QR], F32, kind=I)
        self.d_wqdec = ap("wqdec", [QR, D], F32, kind=I)
        self.d_wkrope = ap("wkrope", [D, H * RD], F32, kind=I)
        self.d_wqrope = ap("wqrope", [QR, H * RD], F32, kind=I)
        self.d_wo = ap("wo", [D, D], F32, kind=I)
        self.d_w1 = ap("w1", [DFF, D], F32, kind=I)
        self.d_b1 = ap("b1l", [128, FC], F32, kind=I)
        self.d_w2 = ap("w2", [D, DFF], F32, kind=I)
        self.d_b2 = ap("b2l", [128, DC], F32, kind=I)
        self.d_idf = ap("identf", [128, 128], F32, kind=I)
        self.d_idb = ap("identb", [128, 128], BF16, kind=I)
        self.d_cos = ap("cosT", [128, T], BF16, kind=I)
        self.d_ssin = ap("ssinT", [128, T], BF16, kind=I)
        self.d_mask = ap("maskw", [128, 896], BF16, kind=I)
        self.d_out = ap("out", [S, T, D], F32, kind=O)
        self.d_xT = ap("xT_scr", [S, DC, 128, T], F32, kind=N)
        self.d_h = ap("h_scr", [FC, 128, T], BF16, kind=N)
        self.d_wq1t = ap("wq1t_scr", [DC, 128, DFF], BF16, kind=N)
        self.d_wq2t = ap("wq2t_scr", [FC, 128, D], BF16, kind=N)

    def tap_dram(self, name, shape, dtype):
        if name not in self.tap_d:
            self.tap_d[name] = self.nc.dram_tensor(
                "tap_" + name, list(shape), dtype, kind="ExternalOutput").ap()
        return self.tap_d[name]

    def _alloc_persistent(self):
        nc, E = self.nc, self.es.enter_context
        self.identf = E(nc.sbuf_tensor(_u("c_identf"), [128, 128], F32))
        self.identb = E(nc.sbuf_tensor(_u("c_identb"), [128, 128], BF16))
        self.cosT = E(nc.sbuf_tensor(_u("c_cos"), [128, T], BF16))
        self.ssinT = E(nc.sbuf_tensor(_u("c_ssin"), [128, T], BF16))
        self.maskw = E(nc.sbuf_tensor(_u("c_maskw"), [128, 896], BF16))
        self.ones_col = E(nc.sbuf_tensor(_u("c_ones_col"), [128, 1], BF16))
        self.ones_colf = E(nc.sbuf_tensor(_u("c_ones_colf"), [128, 1], F32))
        self.ones_row = E(nc.sbuf_tensor(_u("c_ones_row"), [128, 128], BF16))
        self.prl65 = E(nc.sbuf_tensor(_u("c_prl65"), [65, 65], BF16))
        self.ones_rowf = E(nc.sbuf_tensor(_u("c_ones_rowf"), [1, 128], F32))
        self.eps_ln = E(nc.sbuf_tensor(_u("c_eps_ln"), [1, 1], F32))
        self.eps_kv = E(nc.sbuf_tensor(_u("c_eps_kv"), [1, 1], F32))
        self.negc = E(nc.sbuf_tensor(_u("c_negc"), [128, 1], F32))
        self.b1l = E(nc.sbuf_tensor(_u("c_b1l"), [128, FC], F32))
        self.b2l = E(nc.sbuf_tensor(_u("c_b2l"), [128, DC], F32))
        self.gkv = E(nc.sbuf_tensor(_u("c_gkv"), [128, KVC], F32))
        self.XN = None
        self.XB = None

        self.sq = Sem(self.es, nc, "sq")
        self.sg = Sem(self.es, nc, "sg")
        self.spe = Sem(self.es, nc, "spe")
        self.sdv = Sem(self.es, nc, "sdv")
        self.sac = Sem(self.es, nc, "sac")
        self.sst = Sem(self.es, nc, "sst")
        self.sqp = [Sem(self.es, nc, "sqp0"), Sem(self.es, nc, "sqp1")]
        self.sgp2 = [Sem(self.es, nc, "sgp20"), Sem(self.es, nc, "sgp21")]
        self.sqq = [Sem(self.es, nc, "sqq0"), Sem(self.es, nc, "sqq1")]
        self.sgq = [Sem(self.es, nc, "sgq0"), Sem(self.es, nc, "sgq1")]

    @staticmethod
    def wload(e, dst_sb, src_dram, cols, sem, n=16):
        e.dma_start(out=dst_sb[:].rearrange("p (c r) -> p c r", r=cols),
                    in_=src_dram.rearrange("(c p) r -> p c r", p=128)) \
            .then_inc(sem, 16)

    def drain(self, ph, sync=False, gp=False, st=False):
        if sync:
            def fs(e, v=self.sq.n):
                e.wait_ge(self.sq.h, v)
            ph.on("sync", fs)
        if gp:
            def fg(e, v=self.sg.n, v0=self.sgp2[0].n, v1=self.sgp2[1].n,
                   q0=self.sgq[0].n, q1=self.sgq[1].n):
                e.wait_ge(self.sg.h, v)
                e.wait_ge(self.sgp2[0].h, v0)
                e.wait_ge(self.sgp2[1].h, v1)
                e.wait_ge(self.sgq[0].h, q0)
                e.wait_ge(self.sgq[1].h, q1)
            ph.on("gpsimd", fg)
        if st:
            def ft(e, v=self.sst.n):
                e.wait_ge(self.sst.h, v)
            ph.on("sync", ft)

    def p0(self):
        ph = Phase(self.nc, "p0")
        sq = self.sq

        def fsync(e):
            for dst, src in ((self.identf, self.d_idf), (self.identb, self.d_idb),
                             (self.cosT, self.d_cos), (self.ssinT, self.d_ssin),
                             (self.maskw, self.d_mask), (self.b1l, self.d_b1),
                             (self.b2l, self.d_b2), (self.gkv, self.d_gkv)):
                e.dma_start(out=dst[:], in_=src).then_inc(sq.h, 16)
        sq.n += 16 * 8
        ph.on("sync", fsync)

        def fvec(e):
            e.memset(self.ones_col[:], 1.0)
            e.memset(self.ones_colf[:], 1.0)
            e.memset(self.ones_row[:], 1.0)
            e.memset(self.ones_rowf[:], 1.0)
            e.memset(self.eps_ln[:], 1e-5)
            e.memset(self.eps_kv[:], 1e-6)
            e.memset(self.negc[:], -EXPC)
            e.memset(self.prl65[:], 0.0)
            e.memset(self.prl65[64:65, :], 1.0)
        ph.on("vector", fvec)
        self.drain(ph, sync=True)
        ph.emit()

    def quant_init(self, es):
        nc = self.nc
        E = es.enter_context
        q = self._q = {}
        q["wtmp"] = [E(nc.sbuf_tensor(_u(f"q_w{i}"), [128, 1024], F32, side="right")) for i in range(2)]
        q["acc"] = E(nc.sbuf_tensor(_u("q_acc"), [128, 2], F32, side="right"))
        q["r1"] = E(nc.sbuf_tensor(_u("q_r1"), [128, 1], F32, side="right"))
        q["thr"] = E(nc.sbuf_tensor(_u("q_thr"), [128, 2], F32, side="right"))
        q["nthr"] = E(nc.sbuf_tensor(_u("q_nthr"), [128, 2], F32, side="right"))
        q["throw"] = E(nc.sbuf_tensor(_u("q_throw"), [1, 2], F32, side="right"))
        q["bq"] = [E(nc.sbuf_tensor(_u(f"q_bq{i}"), [128, 1024], BF16, side="right")) for i in range(2)]
        q["qq"] = [E(nc.sbuf_tensor(_u(f"q_qq{i}"), [128, 1024], BF16, side="right")) for i in range(2)]
        q["stage"] = [E(nc.sbuf_tensor(_u(f"q_st{i}"), [128, 1024], BF16, side="right")) for i in range(2)]
        q["qtr"] = [E(nc.psum_tensor(_u(f"q_ptr{i}"), [128, 128], BF16, side="right")) for i in range(2)]
        units = []
        for ch in range(FC):
            units.append((0, ch, 0, D))
        for ch in range(DC):
            for qr in range(4):
                units.append((1, ch, qr * 1024, 1024))
        q["units"] = units
        q["ai"] = 0
        q["ci"] = 0
        q["thr_done"] = False
        q["amarks"] = {}
        q["cmarks"] = {}
        q["nA"] = len(units)

    def q_pump_A(self, ph, n):
        q = getattr(self, "_q", None)
        if not q or q["ai"] >= q["nA"]:
            return
        sdv = self.sdv
        i0, i1 = q["ai"], min(q["ai"] + n, q["nA"])
        q["ai"] = i1
        units, wtmp, acc, r1 = q["units"], q["wtmp"], q["acc"], q["r1"]
        loads = {}
        for i in range(i0, i1):
            self.sqq[i % 2].n += 16
            loads[i] = self.sqq[i % 2].n
            sdv.n += 1
            q["amarks"][i] = sdv.n

        def fsy(e, i0=i0, i1=i1, loads=dict(loads)):
            for i in range(i0, i1):
                mi, ch, off, cols = units[i]
                if i >= 2:
                    e.wait_ge(sdv.h, q["amarks"][i - 2])
                src = (self.d_w1 if mi == 0 else self.d_w2)[
                    ch * 128:(ch + 1) * 128, off:off + cols]
                e.dma_start(out=wtmp[i % 2][:, 0:cols], in_=src) \
                    .then_inc(self.sqq[i % 2].h, 16)
        ph.on("sync", fsy)

        def fdv(e, i0=i0, i1=i1, loads=dict(loads)):
            for i in range(i0, i1):
                mi, ch, off, cols = units[i]
                e.wait_ge(self.sqq[i % 2].h, loads[i])
                e.tensor_reduce(r1[:], wtmp[i % 2][:, 0:cols],
                                mybir.AxisListType.X, ALU.add,
                                apply_absolute_value=True)
                first = (i == 0) if mi == 0 else (i == FC)
                if first:
                    e.tensor_copy(acc[:, mi:mi + 1], r1[:]).then_inc(sdv.h, 1)
                else:
                    e.tensor_tensor(acc[:, mi:mi + 1], acc[:, mi:mi + 1], r1[:],
                                    ALU.add).then_inc(sdv.h, 1)
        ph.on("vector", fdv)

    def q_pump_thr(self, ph, es):
        q = getattr(self, "_q", None)
        if not q or q["thr_done"]:
            return
        assert q["ai"] >= q["nA"], "thr pumped before pass A complete"
        q["thr_done"] = True
        nc = self.nc
        spe, sdv = self.spe, self.sdv
        pgam = es.enter_context(nc.psum_tensor(_u("q_pgam"), [1, 2], F32))
        pthr = es.enter_context(nc.psum_tensor(_u("q_pthr"), [128, 2], F32))
        a_last = sdv.n
        spe.n += 1
        g_done = spe.n

        def fpe(e, a_last=max(q["amarks"].values())):
            e.wait_ge(sdv.h, a_last)
            e.matmul(pgam[:], self.ones_colf[:], q["acc"][:], start=True,
                     stop=True).then_inc(spe.h, 1)
        ph.on("tensor", fpe)
        sdv.n += 1
        t_done = sdv.n

        def fdv(e, g_done=g_done):
            e.wait_ge(spe.h, g_done)
            for mi, nn in ((0, FC * 128 * D), (1, DC * 128 * DFF)):
                e.tensor_scalar(q["throw"][:, mi:mi + 1], pgam[:, mi:mi + 1],
                                0.5 / nn, 0.5e-5, ALU.mult, ALU.add)
            e.last.then_inc(sdv.h, 1)
        ph.on("vector", fdv)
        spe.n += 1
        t2_done = spe.n

        def fpe2(e, t_done=t_done):
            e.wait_ge(sdv.h, t_done)
            e.matmul(pthr[:], self.ones_rowf[:], q["throw"][:], start=True,
                     stop=True).then_inc(spe.h, 1)
        ph.on("tensor", fpe2)
        sdv.n += 1
        q["thr_mark"] = sdv.n

        def fdv2(e, t2_done=t2_done):
            e.wait_ge(spe.h, t2_done)
            e.tensor_copy(q["thr"][:], pthr[:])
            e.tensor_scalar_mul(q["nthr"][:], pthr[:], -1.0).then_inc(sdv.h, 1)
        ph.on("vector", fdv2)

    def q_pump_C(self, ph, n):
        q = getattr(self, "_q", None)
        if not q or not q.get("thr_done") or q["ci"] >= q["nA"]:
            return
        spe, sdv, sg = self.spe, self.sdv, self.sg
        i0, i1 = q["ci"], min(q["ci"] + n, q["nA"])
        q["ci"] = i1
        units, wtmp = q["units"], q["wtmp"]
        bq, qq, stage, qtr = q["bq"], q["qq"], q["stage"], q["qtr"]
        loads = {}
        ktr = {}
        for i in range(i0, i1):
            mi, ch, off, cols = units[i]
            self.sqq[i % 2].n += 16
            loads[i] = self.sqq[i % 2].n
            sdv.n += 1
            q["cmarks"][("q", i)] = sdv.n
            for j in range(cols // 128):
                spe.n += 1
                q["cmarks"][("t", i, j)] = spe.n
                sdv.n += 1
                q["cmarks"][("e", i, j)] = sdv.n
            q["cmarks"][("t", i)] = spe.n
            q["cmarks"][("e", i)] = sdv.n
            self.sgq[i % 2].n += 16
            q["cmarks"][("s", i)] = self.sgq[i % 2].n

        def fsy(e, i0=i0, i1=i1, loads=dict(loads)):
            for i in range(i0, i1):
                mi, ch, off, cols = units[i]
                if i >= 2:
                    e.wait_ge(sdv.h, q["cmarks"][("q", i - 2)])
                src = (self.d_w1 if mi == 0 else self.d_w2)[
                    ch * 128:(ch + 1) * 128, off:off + cols]
                e.dma_start(out=wtmp[i % 2][:, 0:cols], in_=src) \
                    .then_inc(self.sqq[i % 2].h, 16)
        ph.on("sync", fsy)

        def fdv(e, i0=i0, i1=i1, loads=dict(loads)):
            for i in range(i0, i1):
                mi, ch, off, cols = units[i]
                e.wait_ge(self.sqq[i % 2].h, loads[i])
                if i == 0:
                    e.wait_ge(sdv.h, q["thr_mark"])
                if i >= 2:
                    e.wait_ge(self.sgq[i % 2].h,
                              q["cmarks"][("s", i - 2)])
                    e.wait_ge(spe.h, q["cmarks"][("t", i - 2)])
                e.tensor_single_scalar(bq[i % 2][:, 0:cols], wtmp[i % 2][:, 0:cols],
                                       q["nthr"][:, mi:mi + 1], ALU.is_lt)
                e.scalar_tensor_tensor(qq[i % 2][:, 0:cols], wtmp[i % 2][:, 0:cols],
                                       q["thr"][:, mi:mi + 1], bq[i % 2][:, 0:cols],
                                       ALU.is_gt, ALU.subtract).then_inc(sdv.h, 1)
                for j in range(cols // 128):
                    e.wait_ge(spe.h, q["cmarks"][("t", i, j)])
                    e.tensor_copy(stage[i % 2][:, j * 128:(j + 1) * 128],
                                  qtr[j % 2][:]).then_inc(sdv.h, 1)
        ph.on("vector", fdv)

        def fpe(e, i0=i0, i1=i1):
            flat = [(i, j) for i in range(i0, i1)
                    for j in range(units[i][3] // 128)]
            for k, (i, j) in enumerate(flat):
                if j == 0:
                    e.wait_ge(sdv.h, q["cmarks"][("q", i)])
                if k >= 2:
                    e.wait_ge(sdv.h, q["cmarks"][("e",) + flat[k - 2]])
                elif i0 > 0 and ("e", i0 - 1) in q["cmarks"]:
                    e.wait_ge(sdv.h, q["cmarks"][("e", i0 - 1)])
                e.transpose(qtr[j % 2][:], qq[i % 2][:, j * 128:(j + 1) * 128],
                            self.identb[:]).then_inc(spe.h, 1)
        ph.on("tensor", fpe)

        def fgp(e, i0=i0, i1=i1):
            for i in range(i0, i1):
                mi, ch, off, cols = units[i]
                e.wait_ge(sdv.h, q["cmarks"][("e", i)])
                if mi == 0:
                    out = self.d_wq1t[:, :, ch * 128:(ch + 1) * 128] \
                        .rearrange("c p j -> p c j")
                else:
                    f0 = off // 128
                    out = self.d_wq2t[f0:f0 + 8, :, ch * 128:(ch + 1) * 128] \
                        .rearrange("c p j -> p c j")
                e.dma_start(out=out,
                            in_=stage[i % 2][:, 0:cols]
                            .rearrange("p (c j) -> p c j", j=128)) \
                    .then_inc(self.sgq[i % 2].h, 16)
        ph.on("gpsimd", fgp)

    def ln_group(self, ph, est, tag, g, n_chunks, src_bf_slices, sq_src_slices,
                 eps_tile, repl_hilo, psig, psig2, prep, pbrep, stat, statb,
                 pe_extra_wait=None):
        nc = self.nc
        spe, sdv, sac = self.spe, self.sdv, self.sac
        nf = float(n_chunks * 128)

        dvw = sdv.n
        def fpe(e, src=list(src_bf_slices), sqs=list(sq_src_slices),
                w=pe_extra_wait, dvw=dvw):
            if w is not None:
                e.wait_ge(w[0].h, w[1])
            e.wait_ge(sdv.h, dvw)
            for c in range(len(src)):
                e.matmul(psig[:], self.ones_col[:], src[c],
                         start=(c == 0), stop=(c == len(src) - 1))
            for c in range(len(sqs)):
                e.matmul(psig2[:], self.ones_col[:], sqs[c],
                         start=(c == 0), stop=(c == len(sqs) - 1))
            e.last.then_inc(spe.h, 1)
        spe.n += 1
        stats_done = spe.n
        ph.on("tensor", fpe)

        def fdv1(e, stats_done=stats_done):
            e.wait_ge(spe.h, stats_done)
            e.tensor_scalar_mul(stat[:, 0:G], psig[:], 1.0 / nf)
            e.tensor_scalar_mul(stat[:, G:2 * G], psig2[:], 1.0 / nf)
            e.tensor_tensor(stat[:, 2 * G:3 * G], stat[:, 0:G], stat[:, 0:G], ALU.mult)
            e.tensor_tensor(stat[:, G:2 * G], stat[:, G:2 * G], stat[:, 2 * G:3 * G],
                            ALU.subtract).then_inc(sdv.h, 1)
        sdv.n += 1
        var_done = sdv.n
        ph.on("vector", fdv1)

        def fac(e, var_done=var_done):
            e.wait_ge(sdv.h, var_done)
            e.activation(stat[:, 3 * G:4 * G], stat[:, G:2 * G], AF.Ln, bias=eps_tile[:])
            e.activation(stat[:, 2 * G:3 * G], stat[:, 3 * G:4 * G], AF.Exp,
                         scale=-0.5).then_inc(sac.h, 1)
        sac.n += 1
        r_done = sac.n
        ph.on("scalar", fac)

        if not repl_hilo:
            def fdv2(e, r_done=r_done):
                e.wait_ge(sac.h, r_done)
                e.tensor_copy(statb[:, 0:G], stat[:, 2 * G:3 * G])
                e.tensor_tensor(stat[:, 3 * G:4 * G], stat[:, 0:G],
                                stat[:, 2 * G:3 * G], ALU.mult)
                e.tensor_scalar_mul(stat[:, 3 * G:4 * G], stat[:, 3 * G:4 * G], -1.0)
                e.tensor_copy(statb[:, G:2 * G], stat[:, 3 * G:4 * G]) \
                    .then_inc(sdv.h, 1)
            sdv.n += 1
            sb_done = sdv.n
            ph.on("vector", fdv2)

            def fpe2(e, sb_done=sb_done):
                e.wait_ge(sdv.h, sb_done)
                e.matmul(prep[:], self.ones_row[0:1, :], statb[:, 0:G],
                         start=True, stop=True)
                e.matmul(pbrep[:], self.ones_row[0:1, :], statb[:, G:2 * G],
                         start=True, stop=True).then_inc(spe.h, 1)
            spe.n += 1
            ph.on("tensor", fpe2)
        else:
            def fdv2(e, r_done=r_done):
                e.wait_ge(sac.h, r_done)
                e.tensor_tensor(stat[:, 3 * G:4 * G], stat[:, 0:G],
                                stat[:, 2 * G:3 * G], ALU.mult)
                e.tensor_scalar_mul(stat[:, 3 * G:4 * G], stat[:, 3 * G:4 * G], -1.0)
                e.tensor_copy(statb[:, 0:G], stat[:, 2 * G:3 * G])
                e.tensor_copy(statb[:, G:2 * G], stat[:, 3 * G:4 * G])
                e.tensor_tensor(stat[:, 4 * G:5 * G], stat[:, 2 * G:3 * G],
                                statb[:, 0:G], ALU.subtract)
                e.tensor_tensor(stat[:, 5 * G:6 * G], stat[:, 3 * G:4 * G],
                                statb[:, G:2 * G], ALU.subtract)
                e.tensor_copy(statb[:, 2 * G:3 * G], stat[:, 4 * G:5 * G])
                e.tensor_copy(statb[:, 3 * G:4 * G], stat[:, 5 * G:6 * G]) \
                    .then_inc(sdv.h, 1)
            sdv.n += 1
            sb_done = sdv.n
            ph.on("vector", fdv2)

            def fpe2(e, sb_done=sb_done):
                e.wait_ge(sdv.h, sb_done)
                e.matmul(prep[:], self.ones_row[0:1, :], statb[:, 0:G],
                         start=True, stop=False)
                e.matmul(prep[:], self.ones_row[0:1, :], statb[:, 2 * G:3 * G],
                         start=False, stop=True)
                e.matmul(pbrep[:], self.ones_row[0:1, :], statb[:, G:2 * G],
                         start=True, stop=False)
                e.matmul(pbrep[:], self.ones_row[0:1, :], statb[:, 3 * G:4 * G],
                         start=False, stop=True).then_inc(spe.h, 1)
            spe.n += 1
            ph.on("tensor", fpe2)
        return spe.n

    def ingest_ln1(self, s, XT, do_ln=True):
        nc = self.nc
        sq, sg, spe, sdv, sac = self.sq, self.sg, self.spe, self.sdv, self.sac
        with contextlib.ExitStack() as est:
            E = est.enter_context
            xtok = [E(nc.sbuf_tensor(_u(f"i_xtok{i}"), [128, D], F32)) for i in range(2)]
            pst_ = contextlib.ExitStack()
            ptr = [pst_.enter_context(nc.psum_tensor(_u(f"i_ptr{i}"), [128, 128], F32))
                   for i in range(2)]

            ph = Phase(nc, f"ingest{s}")
            NTT = NT
            for d in _DBG:
                if d.startswith("nt"):
                    NTT = int(d[2:])
            loads, trs, ev_dv, ev_ac = [], [], [], []
            for tt in range(NTT):
                self.sqp[tt % 2].n += 16
                loads.append(self.sqp[tt % 2].n)
                for c in range(DC):
                    spe.n += 1
                    trs.append(spe.n)
                    sdv.n += 1
                    ev_dv.append(sdv.n)
                    sac.n += 1
                    ev_ac.append(sac.n)

            def fsync(e):
                for tt in range(NTT):
                    if tt >= 2 and "loadsonly" not in _DBG:
                        e.wait_ge(spe.h, trs[(tt - 1) * DC - 1])
                    e.dma_start(out=xtok[tt % 2][:],
                                in_=self.d_x[s, tt * 128:(tt + 1) * 128, :]) \
                        .then_inc(self.sqp[tt % 2].h, 16)
                if "loadsonly" in _DBG:
                    e.wait_ge(self.sqp[0].h, self.sqp[0].n)
                    e.wait_ge(self.sqp[1].h, self.sqp[1].n)
            ph.on("sync", fsync)

            def fpe(e):
                if "loadsonly" in _DBG:
                    return
                k = 0
                for tt in range(NTT):
                    e.wait_ge(self.sqp[tt % 2].h, loads[tt])
                    for c in range(DC):
                        if k >= 2:
                            if "nodve" not in _DBG:
                                e.wait_ge(sdv.h, ev_dv[k - 2])
                            if "noact" not in _DBG:
                                e.wait_ge(sac.h, ev_ac[k - 2])
                        e.transpose(ptr[k % 2][:], xtok[tt % 2][:, c * 128:(c + 1) * 128],
                                    self.identf[:]).then_inc(spe.h, 1)
                        k += 1
            ph.on("tensor", fpe)

            def fdv(e):
                if "loadsonly" in _DBG or "nodve" in _DBG:
                    return
                k = 0
                for tt in range(NTT):
                    for c in range(DC):
                        e.wait_ge(spe.h, trs[k])
                        e.tensor_copy(XT[c][:, tt * 128:(tt + 1) * 128], ptr[k % 2][:]) \
                            .then_inc(sdv.h, 1)
                        k += 1
            ph.on("vector", fdv)

            def fac(e):
                if "loadsonly" in _DBG or "noact" in _DBG:
                    return
                k = 0
                for tt in range(NTT):
                    for c in range(DC):
                        e.wait_ge(sdv.h, ev_dv[k])
                        e.activation(self.XB[c][:, tt * 128:(tt + 1) * 128],
                                     ptr[k % 2][:], AF.Copy).then_inc(sac.h, 1)
                        k += 1
            ph.on("scalar", fac)

            if "nospill" not in _DBG:
                def fgp(e, w=ev_dv[-1]):
                    e.wait_ge(sdv.h, w)
                    for c in range(DC):
                        e.dma_start(out=self.d_xT[s, c], in_=XT[c][:]).then_inc(sg.h, 16)
                sg.n += 16 * DC
                ph.on("gpsimd", fgp)
            self.q_pump_A(ph, 64)
            self.drain(ph, gp=True)
            ph.emit()
            pst_.close()

            if not do_ln:
                return
            with contextlib.ExitStack() as lst:
                El = lst.enter_context
                sqb = El(nc.sbuf_tensor(_u("l1_sq"), [128, DC * G], BF16))
                stat = El(nc.sbuf_tensor(_u("l1_stat"), [1, 6 * G], F32))
                statb = El(nc.sbuf_tensor(_u("l1_statb"), [1, 4 * G], BF16))
                tmp = El(nc.sbuf_tensor(_u("l1_tmp"), [128, G], F32))
                psig = El(nc.psum_tensor(_u("l1_psig"), [1, G], F32))
                psig2 = El(nc.psum_tensor(_u("l1_psig2"), [1, G], F32))
                prep = El(nc.psum_tensor(_u("l1_prep"), [128, G], F32))
                pbrep = El(nc.psum_tensor(_u("l1_pbrep"), [128, G], F32))

                ph = Phase(nc, f"ln1_{s}")
                for g in range(NG):
                    gs = slice(g * G, (g + 1) * G)

                    def facq(e, gs=gs):
                        for c in range(DC):
                            e.activation(sqb[:, c * G:(c + 1) * G], self.XB[c][:, gs],
                                         AF.Square).then_inc(sac.h, 1)
                    sac.n += DC
                    sq_done = sac.n
                    ph.on("scalar", facq)

                    rep_done = self.ln_group(
                        ph, lst, f"ln1_{s}_{g}", g, DC,
                        [self.XB[c][:, gs] for c in range(DC)],
                        [sqb[:, c * G:(c + 1) * G] for c in range(DC)],
                        self.eps_ln, False, psig, psig2, prep, pbrep, stat, statb,
                        pe_extra_wait=(sac, sq_done))

                    def fap(e, gs=gs, rep_done=rep_done, XT=XT):
                        e.wait_ge(spe.h, rep_done)
                        for c in range(DC):
                            e.tensor_tensor(tmp[:], XT[c][:, gs], prep[:], ALU.mult)
                            e.tensor_tensor(self.XN[c][:, gs], tmp[:], pbrep[:],
                                            ALU.add)
                        e.last.then_inc(sdv.h, 1)
                    sdv.n += 1
                    ph.on("vector", fap)
                self.q_pump_thr(ph, lst)
                self.q_pump_C(ph, 14)
                ph.emit()

        self.tap_chunks(f"xn{s}", self.XN)
        self.tap_chunks(f"xt{s}", XT)

    def tap_chunks(self, name, chunks, width=T):
        if name not in self.taps:
            return
        nch = len(chunks)
        dt = chunks[0][:, 0:1].dtype
        d = self.tap_dram(name, [nch, 128, width], dt)
        ph = Phase(self.nc, f"tap_{name}")
        sst = self.sst

        def fs(e):
            for c in range(nch):
                e.dma_start(out=d[c], in_=chunks[c][:, 0:width]).then_inc(sst.h, 16)
        sst.n += 16 * nch
        ph.on("sync", fs)
        self.drain(ph, st=True)
        ph.emit()

    def tap_ap(self, name, ap_fn, shape, dtype):
        if name not in self.taps:
            return
        d = self.tap_dram(name, shape, dtype)
        ph = Phase(self.nc, f"tap_{name}")
        sst = self.sst
        pairs = ap_fn(d)

        def fs(e):
            for o, i in pairs:
                e.dma_start(out=o, in_=i).then_inc(sst.h, 16)
        sst.n += 16 * len(pairs)
        ph.on("sync", fs)
        self.drain(ph, st=True)
        ph.emit()

    def projections(self, s, est, QCb, KR, VT):
        kvst = contextlib.ExitStack()
        KVCb = [kvst.enter_context(
            self.nc.sbuf_tensor(_u(f"KVC{i}"), [128, T], BF16))
            for i in range(KVC)]
        """x_n -> kv_c (RMS+g), q_c (pre-scaled), k_rope (roped), k_content, V."""
        nc = self.nc
        sq, sg, spe, sdv, sac = self.sq, self.sg, self.spe, self.sdv, self.sac

        with contextlib.ExitStack() as wst:
            Ew = wst.enter_context
            wq = Ew(nc.sbuf_tensor(_u("p_wq"), [128, DC * QR], BF16))
            wkv = Ew(nc.sbuf_tensor(_u("p_wkv"), [128, DC * KVR], BF16))
            kvsq = Ew(nc.sbuf_tensor(_u("p_kvsq"), [128, KVC * G], BF16))
            preps = Ew(nc.sbuf_tensor(_u("p_preps"), [128, G], F32))
            stat = Ew(nc.sbuf_tensor(_u("p_stat"), [1, 4 * G], F32))
            statb = Ew(nc.sbuf_tensor(_u("p_statb"), [1, G], BF16))
            pkv = [Ew(nc.psum_tensor(_u(f"p_pkv{i}"), [128, G], F32)) for i in range(KVC)]
            pq = [Ew(nc.psum_tensor(_u(f"p_pq{i}"), [128, G], F32)) for i in range(2)]
            psig = Ew(nc.psum_tensor(_u("p_psig"), [1, G], F32))
            prep = Ew(nc.psum_tensor(_u("p_prep"), [128, G], F32))

            ph = Phase(nc, f"proj_kvq_{s}")



            def fgp(e):
                self.wload(e, wq, self.d_wq, QR, sg.h)
                self.wload(e, wkv, self.d_wkv, KVR, sg.h)
            sg.n += 32
            wkv_ld = sg.n
            ph.on("gpsimd", fgp)

            def fdvw(e, wqf_ld=wkv_ld):
                e.wait_ge(sg.h, wqf_ld)
                e.tensor_scalar_mul(wq[:], wq[:], SCALE_Q).then_inc(sdv.h, 1)
            sdv.n += 1
            wq_rdy = sdv.n
            ph.on("vector", fdvw)

            for g in range(NG):
                gs = slice(g * G, (g + 1) * G)
                def fpe(e, gs=gs, g=g, wkv_ld=wkv_ld, wq_rdy=wq_rdy, dvp=sdv.n):
                    e.wait_ge(sg.h, wkv_ld)
                    e.wait_ge(sdv.h, dvp)
                    for kc in range(KVC):
                        for k in range(DC):
                            e.matmul(pkv[kc][:],
                                     wkv[:, k * KVR + kc * 128:k * KVR + (kc + 1) * 128],
                                     self.XN[k][:, gs], start=(k == 0), stop=(k == DC - 1))
                    e.last.then_inc(spe.h, 1)
                spe.n += 1
                kv_done = spe.n
                ph.on("tensor", fpe)

                def facs(e, kv_done=kv_done):
                    e.wait_ge(spe.h, kv_done)
                    for kc in range(KVC):
                        e.activation(kvsq[:, kc * G:(kc + 1) * G], pkv[kc][:],
                                     AF.Square).then_inc(sac.h, 1)
                sac.n += KVC
                kvsq_done = sac.n
                ph.on("scalar", facs)

                def fpe_s(e, kvsq_done=kvsq_done, dvp=sdv.n):
                    e.wait_ge(sac.h, kvsq_done)
                    e.wait_ge(sdv.h, dvp)
                    for kc in range(KVC):
                        e.matmul(psig[:], self.ones_col[:], kvsq[:, kc * G:(kc + 1) * G],
                                 start=(kc == 0), stop=(kc == KVC - 1))
                    e.last.then_inc(spe.h, 1)
                spe.n += 1
                sig_done = spe.n
                ph.on("tensor", fpe_s)

                def fdv_r(e, sig_done=sig_done):
                    e.wait_ge(spe.h, sig_done)
                    e.tensor_scalar_mul(stat[:, 0:G], psig[:], 1.0 / KVR) \
                        .then_inc(sdv.h, 1)
                sdv.n += 1
                ms_done = sdv.n
                ph.on("vector", fdv_r)

                def fac_r(e, ms_done=ms_done):
                    e.wait_ge(sdv.h, ms_done)
                    e.activation(stat[:, G:2 * G], stat[:, 0:G], AF.Ln,
                                 bias=self.eps_kv[:])
                    e.activation(stat[:, 2 * G:3 * G], stat[:, G:2 * G], AF.Exp,
                                 scale=-0.5).then_inc(sac.h, 1)
                sac.n += 1
                r_done = sac.n
                ph.on("scalar", fac_r)

                def fdv_c(e, r_done=r_done):
                    e.wait_ge(sac.h, r_done)
                    e.tensor_copy(statb[:, 0:G], stat[:, 2 * G:3 * G]).then_inc(sdv.h, 1)
                sdv.n += 1
                rb_done = sdv.n
                ph.on("vector", fdv_c)

                def fpe_r(e, rb_done=rb_done):
                    e.wait_ge(sdv.h, rb_done)
                    e.matmul(prep[:], self.ones_row[0:1, :], statb[:, 0:G],
                             start=True, stop=True).then_inc(spe.h, 1)
                spe.n += 1
                rep_done = spe.n
                ph.on("tensor", fpe_r)

                def fdv_a(e, gs=gs, rep_done=rep_done):
                    e.wait_ge(spe.h, rep_done)
                    e.tensor_copy(preps[:], prep[:])
                    for kc in range(KVC):
                        e.scalar_tensor_tensor(KVCb[kc][:, gs], pkv[kc][:],
                                               self.gkv[:, kc:kc + 1], preps[:],
                                               ALU.mult, ALU.mult)
                    e.last.then_inc(sdv.h, 1)
                sdv.n += 1
                ph.on("vector", fdv_a)

                qdv_marks = [sdv.n] * 2
                for qc in range(QC):
                    def fpe_q(e, gs=gs, qc=qc, wq_rdy=wq_rdy, dvp=qdv_marks[qc]):
                        e.wait_ge(sdv.h, max(wq_rdy, dvp))
                        pb = pq[qc % 2]
                        for k in range(DC):
                            e.matmul(pb[:],
                                     wq[:, k * QR + qc * 128:k * QR + (qc + 1) * 128],
                                     self.XN[k][:, gs], start=(k == 0), stop=(k == DC - 1))
                        e.last.then_inc(spe.h, 1)
                    spe.n += 1
                    q_done = spe.n
                    ph.on("tensor", fpe_q)

                    def fdv_q(e, gs=gs, qc=qc, q_done=q_done):
                        e.wait_ge(spe.h, q_done)
                        e.tensor_copy(QCb[qc][:, gs], pq[qc % 2][:]).then_inc(sdv.h, 1)
                    sdv.n += 1
                    qdv_marks.append(sdv.n)
                    ph.on("vector", fdv_q)
            self.q_pump_C(ph, 18)
            ph.emit()

        with contextlib.ExitStack() as wst:
            Ew = wst.enter_context
            wkr = Ew(nc.sbuf_tensor(_u("p_wkr"), [128, DC * H * RD], BF16))
            pkr = [Ew(nc.psum_tensor(_u(f"p_pkr{i}"), [128, G], F32)) for i in range(2)]
            rt1 = Ew(nc.sbuf_tensor(_u("p_rt1"), [128, G], BF16))

            ph = Phase(nc, f"proj_krope_{s}")

            def fgp(e):
                self.wload(e, wkr, self.d_wkrope, H * RD, sg.h)
            sg.n += 16
            wkr_ld = sg.n
            ph.on("gpsimd", fgp)

            RC = H * RD // 128
            for g in range(NG):
                gs = slice(g * G, (g + 1) * G)
                for rc in range(RC):
                    def fpe(e, gs=gs, rc=rc, wkr_ld=wkr_ld, dvp=sdv.n):
                        e.wait_ge(sg.h, wkr_ld)
                        e.wait_ge(sdv.h, dvp)
                        for k in range(DC):
                            e.matmul(pkr[rc % 2][:],
                                     wkr[:, k * 512 + rc * 128:k * 512 + (rc + 1) * 128],
                                     self.XN[k][:, gs], start=(k == 0), stop=(k == DC - 1))
                        e.last.then_inc(spe.h, 1)
                    spe.n += 1
                    mm_done = spe.n
                    ph.on("tensor", fpe)

                    def fdv(e, gs=gs, rc=rc, mm_done=mm_done):
                        e.wait_ge(spe.h, mm_done)
                        p = pkr[rc % 2]
                        ks_ = KR[rc][:, gs]
                        e.tensor_tensor(rt1[:], p[:], self.ssinT[:, gs], ALU.mult)
                        e.stream_shuffle(ks_, rt1[:], mask=[(i + 16) % 32
                                                            for i in range(32)])
                        e.tensor_tensor(rt1[:], p[:], self.cosT[:, gs], ALU.mult)
                        e.tensor_tensor(ks_, ks_, rt1[:], ALU.add).then_inc(sdv.h, 1)
                    sdv.n += 1
                    ph.on("vector", fdv)
            self.q_pump_C(ph, 16)
            ph.emit()

        with contextlib.ExitStack() as wst:
            Ew = wst.enter_context
            wkd = Ew(nc.sbuf_tensor(_u("p_wkd"), [128, KVC * D], BF16))
            wvd = Ew(nc.sbuf_tensor(_u("p_wvd"), [128, KVC * D], BF16))
            pkd = [Ew(nc.psum_tensor(_u(f"p_pkd{i}"), [128, G], F32)) for i in range(2)]
            pv = Ew(nc.psum_tensor(_u("p_pv"), [128, D], F32))

            ph = Phase(nc, f"proj_dec_{s}")

            def fgp(e):
                self.wload(e, wkd, self.d_wkdec, D, sg.h)
                self.wload(e, wvd, self.d_wvdec, D, sg.h)
            sg.n += 32
            wd_ld = sg.n
            ph.on("gpsimd", fgp)

            for g in range(NG):
                gs = slice(g * G, (g + 1) * G)
                for c in range(DC):
                    def fpe(e, gs=gs, c=c, wd_ld=wd_ld, dvp=sdv.n):
                        e.wait_ge(sg.h, wd_ld)
                        e.wait_ge(sdv.h, dvp)
                        for kc in range(KVC):
                            e.matmul(pkd[c % 2][:],
                                     wkd[:, kc * D + c * 128:kc * D + (c + 1) * 128],
                                     KVCb[kc][:, gs], start=(kc == 0), stop=(kc == KVC - 1))
                        e.last.then_inc(spe.h, 1)
                    spe.n += 1
                    mm_done = spe.n
                    ph.on("tensor", fpe)

                    def fdv(e, gs=gs, c=c, mm_done=mm_done):
                        e.wait_ge(spe.h, mm_done)
                        e.tensor_copy(self.XB[c][:, gs], pkd[c % 2][:]).then_inc(sdv.h, 1)
                    sdv.n += 1
                    ph.on("vector", fdv)

            for tt in range(NT):
                def fpe(e, tt=tt, wd_ld=wd_ld, dvp=sdv.n):
                    e.wait_ge(sg.h, wd_ld)
                    e.wait_ge(sdv.h, dvp)
                    for half in range(2):
                        for kc in range(KVC):
                            e.matmul(pv[:, half * 512:(half + 1) * 512],
                                     KVCb[kc][:, tt * 128:(tt + 1) * 128],
                                     wvd[:, kc * D + half * 512:kc * D + (half + 1) * 512],
                                     start=(kc == 0), stop=(kc == KVC - 1))
                    e.last.then_inc(spe.h, 1)
                spe.n += 1
                mm_done = spe.n
                ph.on("tensor", fpe)

                def fdv(e, tt=tt, mm_done=mm_done):
                    e.wait_ge(spe.h, mm_done)
                    e.tensor_copy(
                        VT[:, tt * (H * 65):(tt + 1) * (H * 65)]
                        .rearrange("p (h j) -> p h j", j=65)[:, :, 0:64],
                        pv[:].rearrange("p (h j) -> p h j", j=64)) \
                        .then_inc(sdv.h, 1)
                sdv.n += 1
                ph.on("vector", fdv)

            def fdv_ones(e):
                e.memset(VT[:].rearrange("p (th j) -> p th j", j=65)[:, :, 64:65], 1.0)
                e.last.then_inc(sdv.h, 1)
            sdv.n += 1
            ph.on("vector", fdv_ones)
            self.q_pump_C(ph, 64)
            self.drain(ph, gp=True)
            ph.emit()
        kvst.close()
        if getattr(self, "_q", None) and self._q["ci"] >= self._q["nA"]:
            self._q = None
            self._qes.close()

        self.tap_chunks(f"kvc{s}", KVCb)
        self.tap_chunks(f"qc{s}", QCb)
        self.tap_chunks(f"kr{s}", KR)
        self.tap_chunks(f"kc{s}", self.XB)
        self.tap_chunks(f"vt{s}", [VT], width=NT * H * 65)

    def attention(self, s, est, QCb, KR, VT):
        nc = self.nc
        sq, sg, spe, sdv, sac = self.sq, self.sg, self.spe, self.sdv, self.sac
        E = est.enter_context
        wqd = E(nc.sbuf_tensor(_u("a_wqd"), [128, QC * D], BF16))
        wqr = E(nc.sbuf_tensor(_u("a_wqr"), [128, QC * H * RD], BF16))
        QCZ = E(nc.sbuf_tensor(_u("a_qcz"), [128, H * G], BF16))
        QRZ = E(nc.sbuf_tensor(_u("a_qrz"), [128, H * G], BF16))
        PT = [E(nc.sbuf_tensor(_u(f"a_pt{i}"), [128, 4 * G], BF16)) for i in range(2)]
        OST = E(nc.sbuf_tensor(_u("a_ost"), [128, 8 * G], BF16))
        RL = E(nc.sbuf_tensor(_u("a_rl"), [128, 2 * G], BF16))
        RLS = E(nc.sbuf_tensor(_u("a_rls"), [128, 2 * G], BF16))
        RLB = E(nc.sbuf_tensor(_u("a_rlb"), [128, 2 * G], BF16))
        SCP = E(nc.psum_tensor(_u("a_scp"), [128, 4 * G], F32))
        PO = [E(nc.psum_tensor(_u(f"a_po{i}"), [65, G], F32)) for i in range(2)]
        PRL = [E(nc.psum_tensor(_u(f"a_prl{i}"), [65, G], F32)) for i in range(2)]

        ph = Phase(nc, f"attn_{s}")

        def fz0(e):
            e.memset(QCZ[:], 0.0)
            e.memset(QRZ[:], 0.0)
            e.memset(RLB[:], 0.0)
        ph.on("vector", fz0)
        M = {}
        pe_ops, dv_ops, ac_ops, gp_ops = [], [], [], []

        def fgp0(e):
            self.wload(e, wqd, self.d_wqdec, D, sg.h)
            self.wload(e, wqr, self.d_wqrope, H * RD, sg.h)
        sg.n += 32
        wq_ld = sg.n
        ph.on("gpsimd", fgp0)

        last_oev = [0, 0]
        prev_odma = 0
        pend_prl = []
        wslice = [0, 0, 0]
        sc_slot_ev = [0] * 4
        prev_win_exp = 0
        mask_shuf = [(i + 16) % 32 for i in range(32)]

        for g in range(NG):
            gs = slice(g * G, (g + 1) * G)
            KT, NP = 4 * (g + 1), 2 * (g + 1)

            qev_start = sdv.n
            for c in range(12):
                slot = c % 4
                par, half = slot % 2, slot // 2
                spe.n += 1
                M[("qp", g, c)] = spe.n
                pe_ops.append(("qproj", g, gs, c, par, half,
                               prev_win_exp if c == 0 else 0,
                               sc_slot_ev[slot], wq_ld if (g == 0 and c == 0) else 0))
                sdv.n += 1
                M[("qe", g, c)] = sdv.n
                sc_slot_ev[slot] = sdv.n
                dv_ops.append(("qevac", g, gs, c, par, half))
            qev_all = sdv.n

            for hp in range(8):
                h0, h1 = 2 * hp, 2 * hp + 1
                for pi in range(NP):
                    sac.n += 1
                    M[("exp", g, h0, pi)] = sac.n
                    sac.n += 1
                    M[("exp", g, h1, pi)] = sac.n
                prev_win_exp = sac.n
                for pi in range(NP):
                    if pi * 2 >= 4 * g:
                        for h in (h0, h1):
                            sdv.n += 1
                            M[("mask", g, h, pi)] = sdv.n
                            dv_ops.append(("mask", g, h, pi))
                for h in (h0, h1):
                    sac.n += 2
                    M[("rlb", g, h)] = sac.n
                for h in (h0, h1):
                    sdv.n += 1
                    M[("oev", g, h)] = sdv.n
                    dv_ops.append(("oev", g, h,
                                   prev_odma if (h == 1 and g > 0) else 0))
                po_wait = (last_oev[0], last_oev[1])
                for pi in range(NP):
                    for h in (h0, h1):
                        spe.n += 1
                        M[("qk", g, h, pi)] = spe.n
                        pe_ops.append(("qk", g, gs, h, pi,
                                       qev_all if pi == 0 else 0))
                    if pi == 0 and pend_prl:
                        for op in pend_prl:
                            spe.n += 1
                            M[("prl",) + op[1:3]] = spe.n
                            pe_ops.append(op)
                        pend_prl.clear()
                    if pi >= 1:
                        for h in (h0, h1):
                            spe.n += 1
                            M[("pv", g, h, pi - 1)] = spe.n
                            pe_ops.append(("pv", g, h, pi - 1, pi - 1 == 0,
                                           False, po_wait[h % 2]))
                for h in (h0, h1):
                    spe.n += 1
                    M[("pv", g, h, NP - 1)] = spe.n
                    pe_ops.append(("pv", g, h, NP - 1, NP == 1, True,
                                   po_wait[h % 2] if NP == 1 else 0))
                for h in (h0, h1):
                    pend_prl.append(("prl", g, h, po_wait[h % 2]))
                    last_oev[h % 2] = M[("oev", g, h)]
            gp_ops.append(("odma", g, sdv.n, list(pend_prl)))
            for op in pend_prl:
                spe.n += 1
                M[("prl",) + op[1:3]] = spe.n
                pe_ops.append(op)
            pend_prl.clear()
            sg.n += 16 * 8
            prev_odma = sg.n
            ph.on("tensor", lambda e, a=wslice[0], b=len(pe_ops): _fpe(e, a, b))
            ph.on("vector", lambda e, a=wslice[1], b=len(dv_ops): _fdv(e, a, b))
            ph.on("scalar", lambda e, g=g: _fac(e, g))
            ph.on("gpsimd", lambda e, a=wslice[2], b=len(gp_ops): _fgp(e, a, b))
            wslice[0], wslice[1], wslice[2] = len(pe_ops), len(dv_ops), len(gp_ops)

        def _fpe(e, a, b):
            for op in pe_ops[a:b]:
                if op[0] == "qproj":
                    _, g, gs, c, par, half, w_pexp, w_ev, w_ld = op
                    if w_ld:
                        e.wait_ge(sg.h, w_ld)
                    if w_pexp:
                        e.wait_ge(sac.h, w_pexp)
                    if w_ev:
                        e.wait_ge(sdv.h, w_ev)
                    dst = SCP[:, (par * 2 + half) * G:(par * 2 + half + 1) * G]
                    if c < 8:
                        for qc in range(QC):
                            e.matmul(dst, wqd[:, qc * D + c * 128:qc * D + (c + 1) * 128],
                                     QCb[qc][:, gs], start=(qc == 0), stop=(qc == QC - 1))
                    else:
                        rc = c - 8
                        for qc in range(QC):
                            e.matmul(dst, wqr[:, qc * 512 + rc * 128:qc * 512 + (rc + 1) * 128],
                                     QCb[qc][:, gs], start=(qc == 0), stop=(qc == QC - 1))
                    e.last.then_inc(spe.h, 1)
                elif op[0] == "qk":
                    _, g, gs, h, pi, w_qev = op
                    hc = h // 2
                    rc4 = h // 4
                    if w_qev:
                        e.wait_ge(sdv.h, w_qev)
                    if pi >= 1:
                        e.wait_ge(sac.h, M[("exp", g, h, pi - 1)])
                    for j in range(2):
                        kt = pi * 2 + j
                        ks = slice(kt * 128, (kt + 1) * 128)
                        dst = SCP[:, (h % 2) * 2 * G + j * G:
                                  (h % 2) * 2 * G + (j + 1) * G]
                        e.matmul(dst, self.XB[hc][:, ks],
                                 QCZ[:, h * G:(h + 1) * G],
                                 start=True, stop=False)
                        e.matmul(dst, KR[rc4][:, ks],
                                 QRZ[:, h * G:(h + 1) * G],
                                 start=False, stop=True)
                    e.last.then_inc(spe.h, 1)
                elif op[0] == "pv":
                    _, g, h, pi, first, last_, w_oev = op
                    e.wait_ge(sac.h, M[("exp", g, h, pi)])
                    if first and w_oev:
                        e.wait_ge(sdv.h, w_oev)
                    pt = PT[pi % 2]
                    for j in range(2):
                        kt = pi * 2 + j
                        vs = VT[:, kt * (H * 65) + h * 65:kt * (H * 65) + (h + 1) * 65]
                        e.matmul(PO[h % 2][:], vs,
                                 pt[:, (h % 2) * 2 * G + j * G:
                                    (h % 2) * 2 * G + (j + 1) * G],
                                 start=(first and j == 0), stop=(last_ and j == 1))
                    e.last.then_inc(spe.h, 1)
                elif op[0] == "prl":
                    _, g, h, w_oev = op
                    e.wait_ge(sac.h, M[("rlb", g, h)])
                    if w_oev:
                        e.wait_ge(sdv.h, w_oev)
                    e.matmul(PRL[h % 2][:], self.prl65[:],
                             RLB[0:65, (h % 2) * G:(h % 2 + 1) * G],
                             start=True, stop=True).then_inc(spe.h, 1)
        def _fdv(e, a, b):
            for op in dv_ops[a:b]:
                if op[0] == "qevac":
                    _, g, gs, c, par, half = op
                    e.wait_ge(spe.h, M[("qp", g, c)])
                    src = SCP[:, (par * 2 + half) * G:(par * 2 + half + 1) * G]
                    if c < 8:
                        e.tensor_copy(QCZ[0:64, 2 * c * G:(2 * c + 1) * G],
                                      src[0:64, :])
                        e.tensor_copy(QCZ[64:128, (2 * c + 1) * G:(2 * c + 2) * G],
                                      src[64:128, :]).then_inc(sdv.h, 1)
                    else:
                        rc = c - 8
                        ta, tb = PT[0][:, 0:G], PT[0][:, G:2 * G]
                        e.tensor_tensor(ta, src, self.ssinT[:, gs], ALU.mult)
                        e.stream_shuffle(tb, ta, mask=mask_shuf)
                        e.tensor_tensor(ta, src, self.cosT[:, gs], ALU.mult)
                        e.tensor_tensor(tb, tb, ta, ALU.add)
                        for qb in range(4):
                            hh = 4 * rc + qb
                            e.tensor_copy(
                                QRZ[32 * qb:32 * qb + 32, hh * G:(hh + 1) * G],
                                PT[0][32 * qb:32 * qb + 32, G:2 * G])
                        e.last.then_inc(sdv.h, 1)
                elif op[0] == "mask":
                    _, g, h, pi = op
                    e.wait_ge(spe.h, M[("qk", g, h, pi)])
                    for j in range(2):
                        kt = pi * 2 + j
                        off = 384 - 128 * (kt - 4 * g)
                        sl = slice((h % 2) * 2 * G + j * G,
                                   (h % 2) * 2 * G + (j + 1) * G)
                        e.tensor_tensor(SCP[:, sl], SCP[:, sl],
                                        self.maskw[:, off:off + G], ALU.add)
                    e.last.then_inc(sdv.h, 1)
                elif op[0] == "oev":
                    _, g, h, w_odma = op
                    sel = h % 2
                    if w_odma:
                        e.wait_ge(sg.h, w_odma)
                    e.wait_ge(spe.h, M[("prl", g, h)])
                    if h % 2 == 0:
                        dst = self.XN[h // 2][0:64, g * G:(g + 1) * G]
                    else:
                        dst = OST[0:64, (h // 2) * G:(h // 2 + 1) * G]
                    ls = slice(sel * G, (sel + 1) * G)
                    e.tensor_copy(RLS[0:64, ls], PRL[sel][0:64, :])
                    e.tensor_tensor(dst, PO[sel][0:64, :], RLS[0:64, ls],
                                    ALU.mult).then_inc(sdv.h, 1)
        def _fac(e, g):
            if True:
                NP = 2 * (g + 1)
                for hp in range(8):
                    h0, h1 = 2 * hp, 2 * hp + 1
                    for pi in range(NP):
                        diag = (pi * 2 >= 4 * g)
                        for h in (h0, h1):
                            sel = h % 2
                            if diag:
                                e.wait_ge(sdv.h, M[("mask", g, h, pi)])
                            else:
                                e.wait_ge(spe.h, M[("qk", g, h, pi)])
                            if pi >= 2:
                                e.wait_ge(spe.h, M[("pv", g, h, pi - 2)])
                            e.activation(
                                PT[pi % 2][:, sel * 2 * G:(sel + 1) * 2 * G],
                                SCP[:, sel * 2 * G:(sel + 1) * 2 * G],
                                AF.Exp, bias=self.negc[:]).then_inc(sac.h, 1)
                    for h in (h0, h1):
                        sel = h % 2
                        ls = slice(sel * G, (sel + 1) * G)
                        e.wait_ge(spe.h, M[("pv", g, h, NP - 1)])
                        e.activation(RL[64:65, ls], PO[sel][64:65, :], AF.Ln) \
                            .then_inc(sac.h, 1)
                        e.activation(RLB[64:65, ls], RL[64:65, ls], AF.Exp,
                                     scale=-1.0).then_inc(sac.h, 1)
        def _fgp(e, a, b):
            for op in gp_ops[a:b]:
                _, g, w_dv = op[:3]
                e.wait_ge(sdv.h, w_dv)
                for c in range(DC):
                    e.dma_start(out=self.XN[c][64:128, g * G:(g + 1) * G],
                                in_=OST[0:64, c * G:(c + 1) * G]) \
                        .then_inc(sg.h, 16)
        self.drain(ph, gp=True)
        ph.emit()

        self.tap_chunks(f"ot{s}", self.XN)

    def ln_stage(self, M, tag, src_bf, src_sq, n_chunks, eps_tile, hilo,
                 psig, psig2, prep, pbrep, stat, statb, pe_list, dv_list, ac_list,
                 w_src_sac, w_psig_free, w_prep_free):
        spe, sdv, sac = self.spe, self.sdv, self.sac
        nf = float(n_chunks * 128)
        spe.n += 1
        M[("stats", tag)] = spe.n
        pe_list.append(("stats", tag, src_bf, src_sq, w_src_sac, w_psig_free,
                        psig, psig2))
        sdv.n += 1
        M[("fdv1", tag)] = sdv.n
        dv_list.append(("fdv1", tag, nf, psig, psig2, stat))
        sac.n += 1
        M[("rstd", tag)] = sac.n
        ac_list.append(("rstd", tag, eps_tile, stat))
        sdv.n += 1
        M[("statb", tag)] = sdv.n
        dv_list.append(("statb", tag, hilo, stat, statb))

    def ln_stage_repl(self, M, tag, hilo, statb, prep, pbrep, pe_list, w_prep_free):
        spe = self.spe
        spe.n += 1
        M[("repl", tag)] = spe.n
        pe_list.append(("repl", tag, hilo, statb, prep, pbrep, w_prep_free))

    def emit_ln_pe(self, e, op, M):
        spe, sdv, sac = self.spe, self.sdv, self.sac
        if op[0] == "stats":
            _, tag, src_bf, src_sq, w_sac, w_free, psig, psig2 = op
            e.wait_ge(sac.h, w_sac if not callable(w_sac) else w_sac())
            wf = w_free() if callable(w_free) else w_free
            if wf:
                e.wait_ge(sdv.h, wf)
            n = len(src_bf)
            for c in range(n):
                e.matmul(psig[:], self.ones_col[:], src_bf[c],
                         start=(c == 0), stop=(c == n - 1))
            for c in range(n):
                e.matmul(psig2[:], self.ones_col[:], src_sq[c],
                         start=(c == 0), stop=(c == n - 1))
            e.last.then_inc(spe.h, 1)
        elif op[0] == "repl":
            _, tag, hilo, statb, prep, pbrep, w_prep = op
            e.wait_ge(sdv.h, M[("statb", tag)])
            wp = w_prep() if callable(w_prep) else w_prep
            if wp:
                e.wait_ge(sdv.h, wp)
            if hilo:
                e.matmul(prep[:], self.ones_row[0:1, :], statb[:, 0:G],
                         start=True, stop=False)
                e.matmul(prep[:], self.ones_row[0:1, :], statb[:, 2 * G:3 * G],
                         start=False, stop=True)
                e.matmul(pbrep[:], self.ones_row[0:1, :], statb[:, G:2 * G],
                         start=True, stop=False)
                e.matmul(pbrep[:], self.ones_row[0:1, :], statb[:, 3 * G:4 * G],
                         start=False, stop=True).then_inc(spe.h, 1)
            else:
                e.matmul(prep[:], self.ones_row[0:1, :], statb[:, 0:G],
                         start=True, stop=True)
                e.matmul(pbrep[:], self.ones_row[0:1, :], statb[:, G:2 * G],
                         start=True, stop=True).then_inc(spe.h, 1)

    def emit_ln_dv(self, e, op, M):
        spe, sdv, sac = self.spe, self.sdv, self.sac
        if op[0] == "fdv1":
            _, tag, nf, psig, psig2, stat = op
            e.wait_ge(spe.h, M[("stats", tag)])
            e.tensor_scalar_mul(stat[:, 0:G], psig[:], 1.0 / nf)
            e.tensor_scalar_mul(stat[:, G:2 * G], psig2[:], 1.0 / nf)
            e.tensor_tensor(stat[:, 2 * G:3 * G], stat[:, 0:G], stat[:, 0:G],
                            ALU.mult)
            e.tensor_tensor(stat[:, G:2 * G], stat[:, G:2 * G],
                            stat[:, 2 * G:3 * G], ALU.subtract).then_inc(sdv.h, 1)
        elif op[0] == "statb":
            _, tag, hilo, stat, statb = op
            e.wait_ge(sac.h, M[("rstd", tag)])
            if hilo:
                e.tensor_tensor(stat[:, 3 * G:4 * G], stat[:, 0:G],
                                stat[:, 2 * G:3 * G], ALU.mult)
                e.tensor_scalar_mul(stat[:, 3 * G:4 * G], stat[:, 3 * G:4 * G], -1.0)
                e.tensor_copy(statb[:, 0:G], stat[:, 2 * G:3 * G])
                e.tensor_copy(statb[:, G:2 * G], stat[:, 3 * G:4 * G])
                e.tensor_tensor(stat[:, 4 * G:5 * G], stat[:, 2 * G:3 * G],
                                statb[:, 0:G], ALU.subtract)
                e.tensor_tensor(stat[:, 5 * G:6 * G], stat[:, 3 * G:4 * G],
                                statb[:, G:2 * G], ALU.subtract)
                e.tensor_copy(statb[:, 2 * G:3 * G], stat[:, 4 * G:5 * G])
                e.tensor_copy(statb[:, 3 * G:4 * G], stat[:, 5 * G:6 * G]) \
                    .then_inc(sdv.h, 1)
            else:
                e.tensor_copy(statb[:, 0:G], stat[:, 2 * G:3 * G])
                e.tensor_tensor(stat[:, 3 * G:4 * G], stat[:, 0:G],
                                stat[:, 2 * G:3 * G], ALU.mult)
                e.tensor_scalar_mul(stat[:, 3 * G:4 * G], stat[:, 3 * G:4 * G], -1.0)
                e.tensor_copy(statb[:, G:2 * G], stat[:, 3 * G:4 * G]) \
                    .then_inc(sdv.h, 1)

    def emit_ln_ac(self, e, op, M):
        sdv, sac = self.sdv, self.sac
        _, tag, eps_tile, stat = op
        e.wait_ge(sdv.h, M[("fdv1", tag)])
        e.activation(stat[:, 3 * G:4 * G], stat[:, G:2 * G], AF.Ln, bias=eps_tile[:])
        e.activation(stat[:, 2 * G:3 * G], stat[:, 3 * G:4 * G], AF.Exp,
                     scale=-0.5).then_inc(sac.h, 1)

    def wo_ln2_ln3(self, s):
        nc = self.nc
        sq, sg, spe, sdv, sac = self.sq, self.sg, self.spe, self.sdv, self.sac
        with contextlib.ExitStack() as est:
            E = est.enter_context
            wo = E(nc.sbuf_tensor(_u("o_wo"), [128, DC * D], BF16))
            OBF = [E(nc.sbuf_tensor(_u(f"o_obf{i}"), [128, DC * G], BF16))
                   for i in range(2)]
            OSQ = [E(nc.sbuf_tensor(_u(f"o_osq{i}"), [128, DC * G], BF16))
                   for i in range(2)]
            YG = [E(nc.sbuf_tensor(_u(f"o_yg{i}"), [128, DC * G], F32))
                  for i in range(2)]
            YBF = E(nc.sbuf_tensor(_u("o_ybf"), [128, DC * G], BF16))
            YSQ = E(nc.sbuf_tensor(_u("o_ysq"), [128, DC * G], BF16))
            stat = E(nc.sbuf_tensor(_u("o_stat"), [1, 6 * G], F32))
            statb = E(nc.sbuf_tensor(_u("o_statb"), [1, 4 * G], BF16))
            tmp = E(nc.sbuf_tensor(_u("o_tmp"), [128, G], F32))
            po = [E(nc.psum_tensor(_u(f"o_po{i}"), [128, G], F32)) for i in range(2)]
            psig = E(nc.psum_tensor(_u("o_psig"), [1, G], F32))
            psig2 = E(nc.psum_tensor(_u("o_psig2"), [1, G], F32))
            prep = E(nc.psum_tensor(_u("o_prep"), [128, G], F32))
            pbrep = E(nc.psum_tensor(_u("o_pbrep"), [128, G], F32))

            ph = Phase(nc, f"wo_{s}")
            M = {}
            pe_ops, dv_ops, ac_ops, gp_ops = [], [], [], []

            def fgp0(e):
                self.wload(e, wo, self.d_wo, D, sg.h)
            sg.n += 16
            wo_ld = sg.n
            ph.on("gpsimd", fgp0)

            def sched_chains(g):
                for c in range(DC):
                    spe.n += 1
                    M[("woc", g, c)] = spe.n
                    pe_ops.append(("woc", g, c,
                                   wo_ld if (g == 0 and c == 0) else 0))
                    sac.n += 1
                    M[("oev", g, c)] = sac.n
                    ac_ops.append(("oev", g, c))

            def sched_stage(kind, g):
                tag = (s, kind, g)
                if kind == 2:
                    src_bf = [OBF[g % 2][:, c * G:(c + 1) * G] for c in range(DC)]
                    src_sq = [OSQ[g % 2][:, c * G:(c + 1) * G] for c in range(DC)]
                    w_sac = M[("oev", g, DC - 1)]
                else:
                    src_bf = [YBF[:, c * G:(c + 1) * G] for c in range(DC)]
                    src_sq = [YSQ[:, c * G:(c + 1) * G] for c in range(DC)]
                    w_sac = M[("ysq", g)]
                w_psig = self._last_fdv1[0]
                w_prep = self._last_apply[0]
                self.ln_stage(M, tag, src_bf, src_sq, DC, self.eps_ln, True,
                              psig, psig2, prep, pbrep, stat, statb,
                              pe_ops_stats, dv_ops, ac_ops, w_sac, w_psig, w_prep)
                self.ln_stage_repl(M, tag, True, statb, prep, pbrep,
                                   pe_ops_stats, w_prep)
                self._last_fdv1[0] = M[("fdv1", tag)]
                if kind == 2:
                    sdv.n += 1
                    M[("applyu", g)] = sdv.n
                    dv_ops.append(("applyu", g))
                    self._last_apply[0] = sdv.n
                    gp_ops.append(("xacc", g, sdv.n))
                    self.sgp2[g % 2].n += 16 * DC
                    M[("xacc", g)] = self.sgp2[g % 2].n
                    sdv.n += 1
                    M[("ybf", g)] = sdv.n
                    dv_ops.append(("ybf", g))
                    sac.n += 1
                    M[("ysq", g)] = sac.n
                    ac_ops.append(("ysq", g))
                else:
                    sdv.n += 1
                    M[("applyz", g)] = sdv.n
                    dv_ops.append(("applyz", g))
                    self._last_apply[0] = sdv.n
                    gp_ops.append(("yspill", g, sdv.n))
                    self.sgp2[g % 2].n += 16 * DC
                    M[("yspill", g)] = self.sgp2[g % 2].n

            self._last_fdv1 = [0]
            self._last_apply = [0]
            pe_ops_stats = pe_ops
            sched_chains(0)
            sched_chains(1)
            sched_stage(2, 0)
            sched_chains(2)
            sched_stage(2, 1)
            sched_stage(3, 0)
            sched_chains(3)
            sched_stage(2, 2)
            sched_stage(3, 1)
            sched_stage(2, 3)
            sched_stage(3, 2)
            sched_stage(3, 3)

            def fpe(e, ops=tuple(pe_ops)):
                for op in ops:
                    if op[0] == "woc":
                        _, g, c, w_ld = op
                        if w_ld:
                            e.wait_ge(sg.h, w_ld)
                        gc = g * DC + c
                        if gc >= 2:
                            pg, pc = divmod(gc - 2, DC)
                            e.wait_ge(sac.h, M[("oev", pg, pc)])
                        gs = slice(g * G, (g + 1) * G)
                        for k in range(DC):
                            e.matmul(po[c % 2][:],
                                     wo[:, k * D + c * 128:k * D + (c + 1) * 128],
                                     self.XN[k][:, gs], start=(k == 0),
                                     stop=(k == DC - 1))
                        e.last.then_inc(spe.h, 1)
                    else:
                        self.emit_ln_pe(e, op, M)
            ph.on("tensor", fpe)

            def fac(e, ops=tuple(ac_ops)):
                for op in ops:
                    if op[0] == "oev":
                        _, g, c = op
                        e.wait_ge(spe.h, M[("woc", g, c)])
                        if ("stats", (s, 2, g - 2)) in M:
                            e.wait_ge(spe.h, M[("stats", (s, 2, g - 2))])
                        e.activation(OBF[g % 2][:, c * G:(c + 1) * G], po[c % 2][:],
                                     AF.Copy)
                        e.activation(OSQ[g % 2][:, c * G:(c + 1) * G], po[c % 2][:],
                                     AF.Square).then_inc(sac.h, 1)
                    elif op[0] == "ysq":
                        _, g = op
                        e.wait_ge(sdv.h, M[("ybf", g)])
                        if ("stats", (s, 3, g - 1)) in M:
                            e.wait_ge(spe.h, M[("stats", (s, 3, g - 1))])
                        e.activation(YSQ[:], YBF[:], AF.Square).then_inc(sac.h, 1)
                    else:
                        self.emit_ln_ac(e, op, M)
            ph.on("scalar", fac)

            def fdv(e, ops=tuple(dv_ops)):
                for op in ops:
                    if op[0] == "applyu":
                        _, g = op
                        e.wait_ge(spe.h, M[("repl", (s, 2, g))])
                        if ("yspill", g - 2) in M:
                            e.wait_ge(self.sgp2[g % 2].h, M[("yspill", g - 2)])
                        for c in range(DC):
                            e.tensor_tensor(tmp[:], OBF[g % 2][:, c * G:(c + 1) * G],
                                            prep[:], ALU.mult)
                            e.tensor_tensor(YG[g % 2][:, c * G:(c + 1) * G], tmp[:],
                                            pbrep[:], ALU.add)
                        e.last.then_inc(sdv.h, 1)
                    elif op[0] == "ybf":
                        _, g = op
                        e.wait_ge(self.sgp2[g % 2].h, M[("xacc", g)])
                        if ("stats", (s, 3, g - 1)) in M:
                            e.wait_ge(spe.h, M[("stats", (s, 3, g - 1))])
                        e.tensor_copy(YBF[:], YG[g % 2][:]).then_inc(sdv.h, 1)
                    elif op[0] == "applyz":
                        _, g = op
                        gs = slice(g * G, (g + 1) * G)
                        e.wait_ge(spe.h, M[("repl", (s, 3, g))])
                        for c in range(DC):
                            e.tensor_tensor(tmp[:], YG[g % 2][:, c * G:(c + 1) * G],
                                            prep[:], ALU.mult)
                            e.tensor_tensor(self.XN[c][:, gs], tmp[:], pbrep[:],
                                            ALU.add)
                        e.last.then_inc(sdv.h, 1)
                    else:
                        self.emit_ln_dv(e, op, M)
            ph.on("vector", fdv)

            def fgp(e, ops=tuple(gp_ops)):
                for op in ops:
                    kind, g, w = op
                    e.wait_ge(sdv.h, w)
                    gs = slice(g * G, (g + 1) * G)
                    for c in range(DC):
                        if kind == "xacc":
                            e.dma_start(out=YG[g % 2][:, c * G:(c + 1) * G],
                                        in_=self.d_xT[s, c][:, gs],
                                        accum_op=ALU.add) \
                                .then_inc(self.sgp2[g % 2].h, 16)
                        else:
                            e.dma_start(out=self.d_xT[s, c][:, gs],
                                        in_=YG[g % 2][:, c * G:(c + 1) * G]) \
                                .then_inc(self.sgp2[g % 2].h, 16)
            ph.on("gpsimd", fgp)
            self.drain(ph, gp=True)
            ph.emit()

        self.tap_chunks(f"zn{s}", self.XN)

    def ffn_w1(self, s):
        nc = self.nc
        sq, sg, spe, sdv, sac = self.sq, self.sg, self.spe, self.sdv, self.sac
        with contextlib.ExitStack() as est:
            E = est.enter_context
            WQT = E(nc.sbuf_tensor(_u("f_wqt1"), [128, DC * DFF], BF16))
            HB = [E(nc.sbuf_tensor(_u(f"f_hb{i}"), [128, FC * G], BF16))
                  for i in range(2)]
            ph1 = [E(nc.psum_tensor(_u(f"f_ph{i}"), [128, G], F32)) for i in range(4)]
            ph = Phase(nc, f"w1_{s}")
            M = {}
            pe_ops, ac_ops, gp_ops = [], [], []

            def fs0(e):
                e.dma_start(out=WQT[:].rearrange("p (c f) -> p c f", f=DFF),
                            in_=self.d_wq1t.rearrange("c p f -> p c f")) \
                    .then_inc(sq.h, 16)
            sq.n += 16
            w_ld = sq.n
            ph.on("sync", fs0)

            for g in range(NG):
                for f in range(FC):
                    spe.n += 1
                    M[("w1c", g, f)] = spe.n
                    pe_ops.append(("w1c", g, f, w_ld if (g == 0 and f == 0) else 0))
                    sac.n += 1
                    M[("hev", g, f)] = sac.n
                    ac_ops.append(("hev", g, f))
                gp_ops.append(("hspill", g, sac.n))
                self.sgp2[g % 2].n += 16
                M[("hspill", g)] = self.sgp2[g % 2].n

            def fpe(e, ops=tuple(pe_ops)):
                for op in ops:
                    _, g, f, w_ld_ = op
                    if w_ld_:
                        e.wait_ge(sq.h, w_ld_)
                    gf = g * FC + f
                    if gf >= 4:
                        pg, pf = divmod(gf - 4, FC)
                        e.wait_ge(sac.h, M[("hev", pg, pf)])
                    gs = slice(g * G, (g + 1) * G)
                    for k in range(DC):
                        e.matmul(ph1[f % 4][:],
                                 WQT[:, k * DFF + f * 128:k * DFF + (f + 1) * 128],
                                 self.XN[k][:, gs], start=(k == 0), stop=(k == DC - 1))
                    e.last.then_inc(spe.h, 1)
            ph.on("tensor", fpe)

            def fac(e, ops=tuple(ac_ops)):
                for op in ops:
                    _, g, f = op
                    e.wait_ge(spe.h, M[("w1c", g, f)])
                    if ("hspill", g - 2) in M and f == 0:
                        e.wait_ge(self.sgp2[g % 2].h, M[("hspill", g - 2)])
                    e.activation(HB[g % 2][:, f * G:(f + 1) * G], ph1[f % 4][:],
                                 AF.Relu, bias=self.b1l[:, f:f + 1]) \
                        .then_inc(sac.h, 1)
            ph.on("scalar", fac)

            def fgp(e, ops=tuple(gp_ops)):
                for op in ops:
                    _, g, w = op
                    e.wait_ge(sac.h, w)
                    gs = slice(g * G, (g + 1) * G)
                    e.dma_start(out=self.d_h[:, :, gs].rearrange("f p j -> p f j"),
                                in_=HB[g % 2][:].rearrange("p (f j) -> p f j", j=G)) \
                        .then_inc(self.sgp2[g % 2].h, 16)
            ph.on("gpsimd", fgp)
            self.drain(ph, gp=True)
            ph.emit()

    def ffn_w2(self, s):
        nc = self.nc
        sq, sg, spe, sdv, sac = self.sq, self.sg, self.spe, self.sdv, self.sac
        with contextlib.ExitStack() as est:
            E = est.enter_context
            WQT = E(nc.sbuf_tensor(_u("f_wqt2"), [128, FC * D], BF16))
            HB = [E(nc.sbuf_tensor(_u(f"f_hb2{i}"), [128, FC * G], BF16))
                  for i in range(2)]
            FBF = E(nc.sbuf_tensor(_u("f_fbf"), [128, DC * G], BF16))
            FSQ = E(nc.sbuf_tensor(_u("f_fsq"), [128, DC * G], BF16))
            FOUT = E(nc.sbuf_tensor(_u("f_fout"), [128, DC * G], F32))
            OUTT = E(nc.sbuf_tensor(_u("f_outt"), [128, D], F32))
            stat = E(nc.sbuf_tensor(_u("f_stat"), [1, 6 * G], F32))
            statb = E(nc.sbuf_tensor(_u("f_statb"), [1, 4 * G], BF16))
            tmp = E(nc.sbuf_tensor(_u("f_tmp"), [128, G], F32))
            pf = [E(nc.psum_tensor(_u(f"f_pf{i}"), [128, G], F32)) for i in range(2)]
            psig = E(nc.psum_tensor(_u("f_psig"), [1, G], F32))
            psig2 = E(nc.psum_tensor(_u("f_psig2"), [1, G], F32))
            prep = E(nc.psum_tensor(_u("f_prep"), [128, G], F32))
            pbrep = E(nc.psum_tensor(_u("f_pbrep"), [128, G], F32))
            ptr = [E(nc.psum_tensor(_u(f"f_ptr{i}"), [128, 128], F32)) for i in range(2)]

            ph = Phase(nc, f"w2_{s}")
            M = {}
            if not hasattr(self, "_dbgM"):
                self._dbgM = {}
            self._dbgM[s] = M
            pe_ops, dv_ops, ac_ops, gp_ops, sy_ops = [], [], [], [], []
            self._last_fdv1 = [0]
            self._last_apply = [0]

            sq.n += 16
            M[("wld",)] = sq.n
            sy_ops.append(("wld",))

            def sched_hload(g):
                self.sqp[g % 2].n += 16
                M[("hld", g)] = self.sqp[g % 2].n
                sy_ops.append(("hld", g))

            def sched_chains(g, mid=None):
                for c in range(DC):
                    spe.n += 1
                    M[("w2c", g, c)] = spe.n
                    pe_ops.append(("w2c", g, c))
                    sac.n += 1
                    M[("fev", g, c)] = sac.n
                    ac_ops.append(("fev", g, c))
                    if c == 1 and mid is not None:
                        mid()

            def sched_stats4(g):
                tag = (s, 4, g)
                src_bf = [FBF[:, c * G:(c + 1) * G] for c in range(DC)]
                src_sq = [FSQ[:, c * G:(c + 1) * G] for c in range(DC)]
                self.ln_stage(M, tag, src_bf, src_sq, DC, self.eps_ln, True,
                              psig, psig2, prep, pbrep, stat, statb,
                              pe_ops, dv_ops, ac_ops,
                              M[("fev", g, DC - 1)], self._last_fdv1[0],
                              self._last_apply[0])
                self._last_fdv1[0] = M[("fdv1", tag)]

            def sched_repl4(g):
                self.ln_stage_repl(M, (s, 4, g), True, statb, prep, pbrep,
                                   pe_ops, self._last_apply[0])

            def sched_applyf(g):
                sdv.n += 1
                M[("applyf", g)] = sdv.n
                dv_ops.append(("applyf", g))
                self._last_apply[0] = sdv.n
                gp_ops.append(("yacc", g, sdv.n))
                self.sgp2[g % 2].n += 16 * DC
                M[("yacc", g)] = self.sgp2[g % 2].n

            def sched_egress(g):
                for st in range(4):
                    for c in range(DC):
                        spe.n += 1
                        M[("etr", g, st, c)] = spe.n
                        pe_ops.append(("etr", g, st, c))
                        sdv.n += 1
                        M[("eev", g, st, c)] = sdv.n
                        dv_ops.append(("eev", g, st, c))
                    self.sst.n += 16
                    M[("store", g, st)] = self.sst.n
                    sy_ops.append(("store", g, st))

            sched_hload(0)
            sched_chains(0)
            sched_hload(1)
            sched_stats4(0)
            sched_chains(1, mid=lambda: sched_repl4(0))
            sched_applyf(0)
            sched_hload(2)
            sched_stats4(1)
            sched_chains(2, mid=lambda: (sched_repl4(1), sched_egress(0)))
            sched_applyf(1)
            sched_hload(3)
            sched_stats4(2)
            sched_chains(3, mid=lambda: (sched_repl4(2), sched_egress(1)))
            sched_applyf(2)
            sched_stats4(3)
            sched_repl4(3)
            sched_egress(2)
            sched_applyf(3)
            sched_egress(3)

            def fpe(e, ops=tuple(pe_ops)):
                for op in ops:
                    if op[0] == "w2c":
                        _, g, c = op
                        if g == 0 and c == 0:
                            e.wait_ge(sq.h, M[("wld",)])
                        e.wait_ge(self.sqp[g % 2].h, M[("hld", g)])
                        gc = g * DC + c
                        if gc >= 2:
                            pg, pc = divmod(gc - 2, DC)
                            e.wait_ge(sac.h, M[("fev", pg, pc)])
                        for f in range(FC):
                            e.matmul(pf[c % 2][:],
                                     WQT[:, f * D + c * 128:f * D + (c + 1) * 128],
                                     HB[g % 2][:, f * G:(f + 1) * G],
                                     start=(f == 0), stop=(f == FC - 1))
                        e.last.then_inc(spe.h, 1)
                    elif op[0] == "etr":
                        _, g, st, c = op
                        k = (g * 4 + st) * DC + c
                        if st == 0 and c == 0:
                            e.wait_ge(self.sgp2[g % 2].h, M[("yacc", g)])
                        if k >= 2:
                            pg2 = k - 2
                            gg, rem = divmod(pg2, 4 * DC)
                            pst_, pc = divmod(rem, DC)
                            e.wait_ge(sdv.h, M[("eev", gg, pst_, pc)])
                        e.transpose(ptr[k % 2][:],
                                    FOUT[:, c * G + st * 128:c * G + (st + 1) * 128],
                                    self.identf[:]).then_inc(spe.h, 1)
                    else:
                        self.emit_ln_pe(e, op, M)
            ph.on("tensor", fpe)

            def fac(e, ops=tuple(ac_ops)):
                for op in ops:
                    if op[0] == "fev":
                        _, g, c = op
                        e.wait_ge(spe.h, M[("w2c", g, c)])
                        if g >= 1 and c == 0:
                            e.wait_ge(sdv.h, M[("applyf", g - 1)])
                        e.activation(FBF[:, c * G:(c + 1) * G], pf[c % 2][:],
                                     AF.Identity, bias=self.b2l[:, c:c + 1])
                        e.activation(FSQ[:, c * G:(c + 1) * G], pf[c % 2][:],
                                     AF.Square, bias=self.b2l[:, c:c + 1]) \
                            .then_inc(sac.h, 1)
                    else:
                        self.emit_ln_ac(e, op, M)
            ph.on("scalar", fac)

            def fdv(e, ops=tuple(dv_ops)):
                for op in ops:
                    if op[0] == "applyf":
                        _, g = op
                        e.wait_ge(spe.h, M[("repl", (s, 4, g))])
                        for c in range(DC):
                            e.tensor_tensor(tmp[:], FBF[:, c * G:(c + 1) * G],
                                            prep[:], ALU.mult)
                            e.tensor_tensor(FOUT[:, c * G:(c + 1) * G], tmp[:],
                                            pbrep[:], ALU.add)
                        e.last.then_inc(sdv.h, 1)
                    elif op[0] == "eev":
                        _, g, st, c = op
                        e.wait_ge(spe.h, M[("etr", g, st, c)])
                        if c == 0:
                            prev = M.get(("store", g, st - 1)) if st >= 1 \
                                else M.get(("store", g - 1, 3), 0)
                            if prev:
                                e.wait_ge(self.sst.h, prev)
                        e.tensor_copy(OUTT[:, c * 128:(c + 1) * 128],
                                      ptr[((g * 4 + st) * DC + c) % 2][:]) \
                            .then_inc(sdv.h, 1)
                    else:
                        self.emit_ln_dv(e, op, M)
            ph.on("vector", fdv)

            def fgp(e, ops=tuple(gp_ops)):
                for op in ops:
                    _, g, w = op
                    e.wait_ge(sdv.h, w)
                    gs = slice(g * G, (g + 1) * G)
                    for c in range(DC):
                        e.dma_start(out=FOUT[:, c * G:(c + 1) * G],
                                    in_=self.d_xT[s, c][:, gs],
                                    accum_op=ALU.add) \
                            .then_inc(self.sgp2[g % 2].h, 16)
            ph.on("gpsimd", fgp)

            def fsy(e, ops=tuple(sy_ops)):
                for op in ops:
                    if op[0] == "wld":
                        e.dma_start(out=WQT[:].rearrange("p (c f) -> p c f", f=D),
                                    in_=self.d_wq2t.rearrange("c p f -> p c f")) \
                            .then_inc(sq.h, 16)
                    elif op[0] == "hld":
                        _, g = op
                        if g >= 2:
                            e.wait_ge(spe.h, M[("w2c", g - 2, DC - 1)])
                        gs = slice(g * G, (g + 1) * G)
                        e.dma_start(out=HB[g % 2][:]
                                    .rearrange("p (f j) -> p f j", j=G),
                                    in_=self.d_h[:, :, gs]
                                    .rearrange("f p j -> p f j")) \
                            .then_inc(self.sqp[g % 2].h, 16)
                    elif op[0] == "store":
                        _, g, st = op
                        for c in range(DC):
                            e.wait_ge(sdv.h, M[("eev", g, st, c)])
                        e.dma_start(out=self.d_out[s, g * G + st * 128:
                                                   g * G + (st + 1) * 128, :],
                                    in_=OUTT[:]).then_inc(self.sst.h, 16)
            ph.on("sync", fsy)
            self.drain(ph, st=True, gp=True)
            ph.emit()

    def finale(self):
        ph = Phase(self.nc, "fin")
        sst = self.sst

        def fs(e):
            e.wait_ge(sst.h, sst.n)
        ph.on("sync", fs)
        ph.emit()

    def build(self, upto=None):
        nc = self.nc
        self.p0()
        if upto != "p0":
            do_q = upto not in ("ingest", "ln1", "proj", "attn", "wo")
            if do_q:
                self._qes = contextlib.ExitStack()
                self.quant_init(self._qes)
            for s in range(S):
                with contextlib.ExitStack() as xbst:
                    self.XN = [xbst.enter_context(
                        nc.sbuf_tensor(_u(f"XN{c}"), [128, T], BF16))
                        for c in range(DC)]
                    self.XB = [xbst.enter_context(
                        nc.sbuf_tensor(_u(f"XB{c}"), [128, T], BF16))
                        for c in range(DC)]
                    with contextlib.ExitStack() as sst_:
                        XT = [sst_.enter_context(
                            nc.sbuf_tensor(_u(f"XT{c}"), [128, T], F32))
                            for c in range(DC)]
                        self.ingest_ln1(s, XT, do_ln=(upto != "ingest"))
                    if upto in ("ln1", "ingest"):
                        continue
                    with contextlib.ExitStack() as pst:
                        E = pst.enter_context
                        QCb = [E(nc.sbuf_tensor(_u(f"QC{i}"), [128, T], BF16))
                               for i in range(QC)]
                        KR = [E(nc.sbuf_tensor(_u(f"KR{i}"), [128, T], BF16))
                              for i in range(4)]
                        VT = E(nc.sbuf_tensor(_u("VTt"), [128, NT * H * 65], BF16))
                        self.projections(s, pst, QCb, KR, VT)
                        if upto == "proj":
                            continue
                        with contextlib.ExitStack() as ast:
                            self.attention(s, ast, QCb, KR, VT)
                    if upto in ("attn", "ln1", "ingest", "proj"):
                        continue
                    self.wo_ln2_ln3(s)
                    if upto == "wo":
                        continue
                    self.ffn_w1(s)
                if upto in ("attn", "ln1", "ingest", "proj", "wo"):
                    continue
                self.ffn_w2(s)
        self.finale()
        return nc


def _host_consts():
    inv_freq = 1.0 / 10000.0 ** (np.arange(0, RD, 2, dtype=np.float32) / RD)
    t = np.arange(T, dtype=np.float32)
    freqs = np.outer(t, inv_freq)
    emb = np.concatenate([freqs, freqs], -1)
    cos = np.cos(emb).T.astype(np.float32)
    sin = np.sin(emb).T.astype(np.float32)
    ssin = np.concatenate([-sin[:RD // 2], sin[RD // 2:]], 0)
    ssin = ssin[(np.arange(RD) + RD // 2) % RD]
    cos128 = np.tile(cos, (4, 1)).astype(ml_dtypes.bfloat16)
    ssin128 = np.tile(ssin, (4, 1)).astype(ml_dtypes.bfloat16)
    ku = np.arange(128)[:, None]
    uu = np.arange(896)[None, :]
    maskw = np.where(uu < ku + 384, np.float32(MASKNEG), np.float32(0.0))
    return {
        "identf": np.eye(128, dtype=np.float32),
        "identb": np.eye(128, dtype=ml_dtypes.bfloat16),
        "cosT": np.ascontiguousarray(cos128),
        "ssinT": np.ascontiguousarray(ssin128),
        "maskw": maskw.astype(ml_dtypes.bfloat16),
    }


def _host_inputs(L0_attn_x, L0_attn_W_kv, L0_attn_g_kv, L0_attn_W_kdec,
                 L0_attn_W_vdec, L0_attn_W_q, L0_attn_W_qdec, L0_attn_W_krope,
                 L0_attn_W_qrope, L0_attn_W_o, L0_ffn_W1, L0_ffn_b1, L0_ffn_W2,
                 L0_ffn_b2):
    x = np.ascontiguousarray(np.asarray(L0_attn_x, dtype=np.float32))
    f32 = lambda a: np.ascontiguousarray(np.asarray(a, np.float32))
    w = {
        "wkv": f32(L0_attn_W_kv),
        "gkv": np.ascontiguousarray(f32(L0_attn_g_kv).reshape(KVC, 128).T),
        "wkdec": f32(L0_attn_W_kdec),
        "wvdec": f32(L0_attn_W_vdec),
        "wq": f32(L0_attn_W_q),
        "wqdec": f32(L0_attn_W_qdec),
        "wkrope": f32(L0_attn_W_krope),
        "wqrope": f32(L0_attn_W_qrope),
        "wo": f32(L0_attn_W_o),
        "w1": f32(L0_ffn_W1),
        "b1l": np.ascontiguousarray(f32(L0_ffn_b1).reshape(FC, 128).T),
        "w2": f32(L0_ffn_W2),
        "b2l": np.ascontiguousarray(f32(L0_ffn_b2).reshape(DC, 128).T),
    }
    w.update(_host_consts())
    in_maps = []
    for c in range(8):
        m = dict(w)
        m["x"] = np.ascontiguousarray(x[2 * c:2 * c + 2])
        in_maps.append(m)
    return in_maps


def kernel(**inputs):
    in_maps = _host_inputs(**inputs)
    mk = MK(_TAPS)
    nc = mk.build()
    res = run_bass_kernel_spmd(nc, in_maps, core_ids=list(range(8)))
    mk.es.close()
    out = np.stack([res.results[c]["out"] for c in range(8)], 0)
    return out.reshape(16, T, D).astype(np.float32)


# version 14 (final)
# speedup vs baseline: 1.2130x; 1.0168x over previous
import numpy as np
import ml_dtypes
import contextlib
from collections import defaultdict

import concourse.bass as bass
import concourse.mybir as mybir
from concourse.bass_utils import run_bass_kernel_spmd

F32 = mybir.dt.float32
BF16 = mybir.dt.bfloat16
AF = mybir.ActivationFunctionType
ALU = mybir.AluOpType

S = 2
T = 2048
D = 1024
DC = D // 128
H = 16
HD = 64
RD = 32
KVR = 256
KVC = KVR // 128
QR = 512
QC = QR // 128
DFF = 4096
FC = DFF // 128
G = 512
NG = T // G
NT = T // 128
EXPC = 11.0
MASKNEG = -60.0
SCALE_Q = float((HD + RD) ** -0.5)
LSC = float(2.0 ** 20)

_TAPS = ()
_DBG = set()

_UID = [0]


def _u(name):
    _UID[0] += 1
    return f"{name}_u{_UID[0]}"



class Sem:
    def __init__(self, es, nc, name):
        self.h = es.enter_context(nc.semaphore(name))
        self.n = 0


_DATAPATH = ("matmul", "transpose", "tensor", "activation", "memset", "memzero",
             "reciprocal", "stream_shuffle", "scalar_tensor_tensor", "dma_start",
             "reduce", "iota", "select", "ldweights", "copy", "square", "sign",
             "mul", "add", "bn_", "pool", "max", "sqrt")
_WRITE_KW = ("out", "accum_out", "out_max", "out_indices", "dest")


def _ap_name(x):
    t = getattr(x, "tensor", None)
    return getattr(t, "name", None) if t is not None else None


class _EngRec:

    def __init__(self, eng):
        self._e = eng
        self.last = None
        self._track = getattr(eng, "engine", None) not in (
            mybir.EngineType.PE, mybir.EngineType.SP)
        self._w = set()
        self._r = set()

    def __getattr__(self, name):
        a = getattr(self._e, name)
        if callable(a) and any(name.startswith(p) for p in _DATAPATH):
            def call(*args, __a=a, __n=name, **kw):
                if self._track:
                    is_dma = __n.startswith("dma_start")
                    wr, rd = [], []
                    for i, x in enumerate(args):
                        nm = _ap_name(x)
                        if nm is None:
                            continue
                        (wr if (i == 0 and not is_dma) else rd).append(nm)
                    for k, x in kw.items():
                        nm = _ap_name(x)
                        if nm is None:
                            continue
                        (wr if (k in _WRITE_KW and not is_dma) else rd).append(nm)
                    conflict = (any(n in self._w for n in rd)
                                or any(n in self._w or n in self._r for n in wr))
                    if conflict:
                        self._e.drain()
                        self._w.clear()
                        self._r.clear()
                    if not is_dma:
                        self._w.update(wr)
                    self._r.update(rd)
                r = __a(*args, **kw)
                if hasattr(r, "then_inc"):
                    self.last = r
                return r
            return call
        return a


class Phase:

    def __init__(self, nc, name=""):
        self.nc = nc
        self.name = name
        self.streams = defaultdict(list)

    def on(self, engine, fn):
        self.streams[engine].append(fn)

    def emit(self):
        if not self.streams:
            return
        with self.nc.Block() as blk:
            for engine in ("sync", "gpsimd", "tensor", "scalar", "vector"):
                fns = self.streams.get(engine)
                if not fns:
                    continue

                def runner(eng, fns=tuple(fns)):
                    rec = _EngRec(eng)
                    for f in fns:
                        f(rec)

                getattr(blk, engine)(runner)


class MK:
    def __init__(self, taps=()):
        self.taps = set(taps)
        self.nc = bass.Bass("TRN2", target_bir_lowering=False)
        self.es = contextlib.ExitStack()
        self.tap_d = {}
        self._declare_dram()
        self._alloc_persistent()

    def _declare_dram(self):
        nc = self.nc
        I, O, N = "ExternalInput", "ExternalOutput", "Internal"
        ap = lambda *a, **k: nc.dram_tensor(*a, **k).ap()
        self.d_x = ap("x", [S, T, D], F32, kind=I)
        self.d_wkv = ap("wkv", [D, KVR], F32, kind=I)
        self.d_gkv = ap("gkv", [128, KVC], F32, kind=I)
        self.d_wkdec = ap("wkdec", [KVR, D], F32, kind=I)
        self.d_wvdec = ap("wvdec", [KVR, D], F32, kind=I)
        self.d_wq = ap("wq", [D, QR], F32, kind=I)
        self.d_wqdec = ap("wqdec", [QR, D], F32, kind=I)
        self.d_wkrope = ap("wkrope", [D, H * RD], F32, kind=I)
        self.d_wqrope = ap("wqrope", [QR, H * RD], F32, kind=I)
        self.d_wo = ap("wo", [D, D], F32, kind=I)
        self.d_w1 = ap("w1", [DFF, D], F32, kind=I)
        self.d_b1 = ap("b1l", [128, FC], F32, kind=I)
        self.d_w2 = ap("w2", [D, DFF], F32, kind=I)
        self.d_b2 = ap("b2l", [128, DC], F32, kind=I)
        self.d_idf = ap("identf", [128, 128], F32, kind=I)
        self.d_idb = ap("identb", [128, 128], BF16, kind=I)
        self.d_cos = ap("cosT", [128, T], BF16, kind=I)
        self.d_ssin = ap("ssinT", [128, T], BF16, kind=I)
        self.d_mask = ap("maskw", [128, 896], BF16, kind=I)
        self.d_out = ap("out", [S, T, D], F32, kind=O)
        self.d_xT = ap("xT_scr", [S, DC, 128, T], F32, kind=N)
        self.d_h = ap("h_scr", [FC, 128, T], BF16, kind=N)
        self.d_wq1t = ap("wq1t_scr", [DC, 128, DFF], BF16, kind=N)
        self.d_wq2t = ap("wq2t_scr", [FC, 128, D], BF16, kind=N)

    def tap_dram(self, name, shape, dtype):
        if name not in self.tap_d:
            self.tap_d[name] = self.nc.dram_tensor(
                "tap_" + name, list(shape), dtype, kind="ExternalOutput").ap()
        return self.tap_d[name]

    def _alloc_persistent(self):
        nc, E = self.nc, self.es.enter_context
        self.identf = E(nc.sbuf_tensor(_u("c_identf"), [128, 128], F32))
        self.identb = E(nc.sbuf_tensor(_u("c_identb"), [128, 128], BF16))
        self.cosT = E(nc.sbuf_tensor(_u("c_cos"), [128, T], BF16))
        self.ssinT = E(nc.sbuf_tensor(_u("c_ssin"), [128, T], BF16))
        self.maskw = E(nc.sbuf_tensor(_u("c_maskw"), [128, 896], BF16))
        self.ones_col = E(nc.sbuf_tensor(_u("c_ones_col"), [128, 1], BF16))
        self.ones_colf = E(nc.sbuf_tensor(_u("c_ones_colf"), [128, 1], F32))
        self.ones_row = E(nc.sbuf_tensor(_u("c_ones_row"), [128, 128], BF16))
        self.prl65 = E(nc.sbuf_tensor(_u("c_prl65"), [65, 65], BF16))
        self.ones_rowf = E(nc.sbuf_tensor(_u("c_ones_rowf"), [1, 128], F32))
        self.eps_ln = E(nc.sbuf_tensor(_u("c_eps_ln"), [1, 1], F32))
        self.eps_kv = E(nc.sbuf_tensor(_u("c_eps_kv"), [1, 1], F32))
        self.negc = E(nc.sbuf_tensor(_u("c_negc"), [128, 1], F32))
        self.b1l = E(nc.sbuf_tensor(_u("c_b1l"), [128, FC], F32))
        self.b2l = E(nc.sbuf_tensor(_u("c_b2l"), [128, DC], F32))
        self.gkv = E(nc.sbuf_tensor(_u("c_gkv"), [128, KVC], F32))
        self.XN = None
        self.XB = None

        self.sq = Sem(self.es, nc, "sq")
        self.sg = Sem(self.es, nc, "sg")
        self.spe = Sem(self.es, nc, "spe")
        self.sdv = Sem(self.es, nc, "sdv")
        self.sac = Sem(self.es, nc, "sac")
        self.sst = Sem(self.es, nc, "sst")
        self.sqp = [Sem(self.es, nc, "sqp0"), Sem(self.es, nc, "sqp1")]
        self.sgp2 = [Sem(self.es, nc, "sgp20"), Sem(self.es, nc, "sgp21")]
        self.sqq = [Sem(self.es, nc, "sqq0"), Sem(self.es, nc, "sqq1")]
        self.sgq = [Sem(self.es, nc, "sgq0"), Sem(self.es, nc, "sgq1")]

    @staticmethod
    def wload(e, dst_sb, src_dram, cols, sem, n=16):
        e.dma_start(out=dst_sb[:].rearrange("p (c r) -> p c r", r=cols),
                    in_=src_dram.rearrange("(c p) r -> p c r", p=128)) \
            .then_inc(sem, 16)

    def drain(self, ph, sync=False, gp=False, st=False):
        if sync:
            def fs(e, v=self.sq.n):
                e.wait_ge(self.sq.h, v)
            ph.on("sync", fs)
        if gp:
            def fg(e, v=self.sg.n, v0=self.sgp2[0].n, v1=self.sgp2[1].n,
                   q0=self.sgq[0].n, q1=self.sgq[1].n):
                e.wait_ge(self.sg.h, v)
                e.wait_ge(self.sgp2[0].h, v0)
                e.wait_ge(self.sgp2[1].h, v1)
                e.wait_ge(self.sgq[0].h, q0)
                e.wait_ge(self.sgq[1].h, q1)
            ph.on("gpsimd", fg)
        if st:
            def ft(e, v=self.sst.n):
                e.wait_ge(self.sst.h, v)
            ph.on("sync", ft)

    def p0(self):
        ph = Phase(self.nc, "p0")
        sq = self.sq

        def fsync(e):
            for dst, src in ((self.identf, self.d_idf), (self.identb, self.d_idb),
                             (self.cosT, self.d_cos), (self.ssinT, self.d_ssin),
                             (self.maskw, self.d_mask), (self.b1l, self.d_b1),
                             (self.b2l, self.d_b2), (self.gkv, self.d_gkv)):
                e.dma_start(out=dst[:], in_=src).then_inc(sq.h, 16)
        sq.n += 16 * 8
        ph.on("sync", fsync)

        def fvec(e):
            e.memset(self.ones_col[:], 1.0)
            e.memset(self.ones_colf[:], 1.0)
            e.memset(self.ones_row[:], 1.0)
            e.memset(self.ones_rowf[:], 1.0)
            e.memset(self.eps_ln[:], 1e-5)
            e.memset(self.eps_kv[:], 1e-6)
            e.memset(self.negc[:], -EXPC)
            e.memset(self.prl65[:], 0.0)
            e.memset(self.prl65[64:65, :], 1.0)
        ph.on("vector", fvec)
        self.drain(ph, sync=True)
        ph.emit()

    def quant_init(self, es):
        nc = self.nc
        E = es.enter_context
        q = self._q = {}
        q["wtmp"] = [E(nc.sbuf_tensor(_u(f"q_w{i}"), [128, 1024], F32, side="right")) for i in range(2)]
        q["acc"] = E(nc.sbuf_tensor(_u("q_acc"), [128, 2], F32, side="right"))
        q["r1"] = E(nc.sbuf_tensor(_u("q_r1"), [128, 1], F32, side="right"))
        q["thr"] = E(nc.sbuf_tensor(_u("q_thr"), [128, 2], F32, side="right"))
        q["nthr"] = E(nc.sbuf_tensor(_u("q_nthr"), [128, 2], F32, side="right"))
        q["throw"] = E(nc.sbuf_tensor(_u("q_throw"), [1, 2], F32, side="right"))
        q["bq"] = [E(nc.sbuf_tensor(_u(f"q_bq{i}"), [128, 1024], BF16, side="right")) for i in range(2)]
        q["qq"] = [E(nc.sbuf_tensor(_u(f"q_qq{i}"), [128, 1024], BF16, side="right")) for i in range(2)]
        q["stage"] = [E(nc.sbuf_tensor(_u(f"q_st{i}"), [128, 1024], BF16, side="right")) for i in range(2)]
        q["qtr"] = [E(nc.psum_tensor(_u(f"q_ptr{i}"), [128, 128], BF16, side="right")) for i in range(2)]
        units = []
        for ch in range(FC):
            units.append((0, ch, 0, D))
        for ch in range(DC):
            for qr in range(4):
                units.append((1, ch, qr * 1024, 1024))
        q["units"] = units
        q["ai"] = 0
        q["ci"] = 0
        q["thr_done"] = False
        q["amarks"] = {}
        q["cmarks"] = {}
        q["nA"] = len(units)

    def q_pump_A(self, ph, n):
        q = getattr(self, "_q", None)
        if not q or q["ai"] >= q["nA"]:
            return
        sdv = self.sdv
        i0, i1 = q["ai"], min(q["ai"] + n, q["nA"])
        q["ai"] = i1
        units, wtmp, acc, r1 = q["units"], q["wtmp"], q["acc"], q["r1"]
        loads = {}
        for i in range(i0, i1):
            self.sqq[i % 2].n += 16
            loads[i] = self.sqq[i % 2].n
            sdv.n += 1
            q["amarks"][i] = sdv.n

        def fsy(e, i0=i0, i1=i1, loads=dict(loads)):
            for i in range(i0, i1):
                mi, ch, off, cols = units[i]
                if i >= 2:
                    e.wait_ge(sdv.h, q["amarks"][i - 2])
                src = (self.d_w1 if mi == 0 else self.d_w2)[
                    ch * 128:(ch + 1) * 128, off:off + cols]
                e.dma_start(out=wtmp[i % 2][:, 0:cols], in_=src) \
                    .then_inc(self.sqq[i % 2].h, 16)
        ph.on("sync", fsy)

        def fdv(e, i0=i0, i1=i1, loads=dict(loads)):
            for i in range(i0, i1):
                mi, ch, off, cols = units[i]
                e.wait_ge(self.sqq[i % 2].h, loads[i])
                e.tensor_reduce(r1[:], wtmp[i % 2][:, 0:cols],
                                mybir.AxisListType.X, ALU.add,
                                apply_absolute_value=True)
                first = (i == 0) if mi == 0 else (i == FC)
                if first:
                    e.tensor_copy(acc[:, mi:mi + 1], r1[:]).then_inc(sdv.h, 1)
                else:
                    e.tensor_tensor(acc[:, mi:mi + 1], acc[:, mi:mi + 1], r1[:],
                                    ALU.add).then_inc(sdv.h, 1)
        ph.on("vector", fdv)

    def q_pump_thr(self, ph, es):
        q = getattr(self, "_q", None)
        if not q or q["thr_done"]:
            return
        assert q["ai"] >= q["nA"], "thr pumped before pass A complete"
        q["thr_done"] = True
        nc = self.nc
        spe, sdv = self.spe, self.sdv
        pgam = es.enter_context(nc.psum_tensor(_u("q_pgam"), [1, 2], F32))
        pthr = es.enter_context(nc.psum_tensor(_u("q_pthr"), [128, 2], F32))
        a_last = sdv.n
        spe.n += 1
        g_done = spe.n

        def fpe(e, a_last=max(q["amarks"].values())):
            e.wait_ge(sdv.h, a_last)
            e.matmul(pgam[:], self.ones_colf[:], q["acc"][:], start=True,
                     stop=True).then_inc(spe.h, 1)
        ph.on("tensor", fpe)
        sdv.n += 1
        t_done = sdv.n

        def fdv(e, g_done=g_done):
            e.wait_ge(spe.h, g_done)
            for mi, nn in ((0, FC * 128 * D), (1, DC * 128 * DFF)):
                e.tensor_scalar(q["throw"][:, mi:mi + 1], pgam[:, mi:mi + 1],
                                0.5 / nn, 0.5e-5, ALU.mult, ALU.add)
            e.last.then_inc(sdv.h, 1)
        ph.on("vector", fdv)
        spe.n += 1
        t2_done = spe.n

        def fpe2(e, t_done=t_done):
            e.wait_ge(sdv.h, t_done)
            e.matmul(pthr[:], self.ones_rowf[:], q["throw"][:], start=True,
                     stop=True).then_inc(spe.h, 1)
        ph.on("tensor", fpe2)
        sdv.n += 1
        q["thr_mark"] = sdv.n

        def fdv2(e, t2_done=t2_done):
            e.wait_ge(spe.h, t2_done)
            e.tensor_copy(q["thr"][:], pthr[:])
            e.tensor_scalar_mul(q["nthr"][:], pthr[:], -1.0).then_inc(sdv.h, 1)
        ph.on("vector", fdv2)

    def q_pump_C(self, ph, n):
        q = getattr(self, "_q", None)
        if not q or not q.get("thr_done") or q["ci"] >= q["nA"]:
            return
        spe, sdv, sg = self.spe, self.sdv, self.sg
        i0, i1 = q["ci"], min(q["ci"] + n, q["nA"])
        q["ci"] = i1
        units, wtmp = q["units"], q["wtmp"]
        bq, qq, stage, qtr = q["bq"], q["qq"], q["stage"], q["qtr"]
        loads = {}
        ktr = {}
        for i in range(i0, i1):
            mi, ch, off, cols = units[i]
            self.sqq[i % 2].n += 16
            loads[i] = self.sqq[i % 2].n
            sdv.n += 1
            q["cmarks"][("q", i)] = sdv.n
            for j in range(cols // 128):
                spe.n += 1
                q["cmarks"][("t", i, j)] = spe.n
                sdv.n += 1
                q["cmarks"][("e", i, j)] = sdv.n
            q["cmarks"][("t", i)] = spe.n
            q["cmarks"][("e", i)] = sdv.n
            self.sgq[i % 2].n += 16
            q["cmarks"][("s", i)] = self.sgq[i % 2].n

        def fsy(e, i0=i0, i1=i1, loads=dict(loads)):
            for i in range(i0, i1):
                mi, ch, off, cols = units[i]
                if i >= 2:
                    e.wait_ge(sdv.h, q["cmarks"][("q", i - 2)])
                src = (self.d_w1 if mi == 0 else self.d_w2)[
                    ch * 128:(ch + 1) * 128, off:off + cols]
                e.dma_start(out=wtmp[i % 2][:, 0:cols], in_=src) \
                    .then_inc(self.sqq[i % 2].h, 16)
        ph.on("sync", fsy)

        def fdv(e, i0=i0, i1=i1, loads=dict(loads)):
            for i in range(i0, i1):
                mi, ch, off, cols = units[i]
                e.wait_ge(self.sqq[i % 2].h, loads[i])
                if i == 0:
                    e.wait_ge(sdv.h, q["thr_mark"])
                if i >= 2:
                    e.wait_ge(self.sgq[i % 2].h,
                              q["cmarks"][("s", i - 2)])
                    e.wait_ge(spe.h, q["cmarks"][("t", i - 2)])
                e.tensor_single_scalar(bq[i % 2][:, 0:cols], wtmp[i % 2][:, 0:cols],
                                       q["nthr"][:, mi:mi + 1], ALU.is_lt)
                e.scalar_tensor_tensor(qq[i % 2][:, 0:cols], wtmp[i % 2][:, 0:cols],
                                       q["thr"][:, mi:mi + 1], bq[i % 2][:, 0:cols],
                                       ALU.is_gt, ALU.subtract).then_inc(sdv.h, 1)
                for j in range(cols // 128):
                    e.wait_ge(spe.h, q["cmarks"][("t", i, j)])
                    e.tensor_copy(stage[i % 2][:, j * 128:(j + 1) * 128],
                                  qtr[j % 2][:]).then_inc(sdv.h, 1)
        ph.on("vector", fdv)

        def fpe(e, i0=i0, i1=i1):
            flat = [(i, j) for i in range(i0, i1)
                    for j in range(units[i][3] // 128)]
            for k, (i, j) in enumerate(flat):
                if j == 0:
                    e.wait_ge(sdv.h, q["cmarks"][("q", i)])
                if k >= 2:
                    e.wait_ge(sdv.h, q["cmarks"][("e",) + flat[k - 2]])
                elif i0 > 0 and ("e", i0 - 1) in q["cmarks"]:
                    e.wait_ge(sdv.h, q["cmarks"][("e", i0 - 1)])
                e.transpose(qtr[j % 2][:], qq[i % 2][:, j * 128:(j + 1) * 128],
                            self.identb[:]).then_inc(spe.h, 1)
        ph.on("tensor", fpe)

        def fgp(e, i0=i0, i1=i1):
            for i in range(i0, i1):
                mi, ch, off, cols = units[i]
                e.wait_ge(sdv.h, q["cmarks"][("e", i)])
                if mi == 0:
                    out = self.d_wq1t[:, :, ch * 128:(ch + 1) * 128] \
                        .rearrange("c p j -> p c j")
                else:
                    f0 = off // 128
                    out = self.d_wq2t[f0:f0 + 8, :, ch * 128:(ch + 1) * 128] \
                        .rearrange("c p j -> p c j")
                e.dma_start(out=out,
                            in_=stage[i % 2][:, 0:cols]
                            .rearrange("p (c j) -> p c j", j=128)) \
                    .then_inc(self.sgq[i % 2].h, 16)
        ph.on("gpsimd", fgp)

    def ln_group(self, ph, est, tag, g, n_chunks, src_bf_slices, sq_src_slices,
                 eps_tile, repl_hilo, psig, psig2, prep, pbrep, stat, statb,
                 pe_extra_wait=None):
        nc = self.nc
        spe, sdv, sac = self.spe, self.sdv, self.sac
        nf = float(n_chunks * 128)

        dvw = sdv.n
        def fpe(e, src=list(src_bf_slices), sqs=list(sq_src_slices),
                w=pe_extra_wait, dvw=dvw):
            if w is not None:
                e.wait_ge(w[0].h, w[1])
            e.wait_ge(sdv.h, dvw)
            for c in range(len(src)):
                e.matmul(psig[:], self.ones_col[:], src[c],
                         start=(c == 0), stop=(c == len(src) - 1))
            for c in range(len(sqs)):
                e.matmul(psig2[:], self.ones_col[:], sqs[c],
                         start=(c == 0), stop=(c == len(sqs) - 1))
            e.last.then_inc(spe.h, 1)
        spe.n += 1
        stats_done = spe.n
        ph.on("tensor", fpe)

        def fdv1(e, stats_done=stats_done):
            e.wait_ge(spe.h, stats_done)
            e.tensor_scalar_mul(stat[:, 0:G], psig[:], 1.0 / nf)
            e.tensor_scalar_mul(stat[:, G:2 * G], psig2[:], 1.0 / nf)
            e.tensor_tensor(stat[:, 2 * G:3 * G], stat[:, 0:G], stat[:, 0:G], ALU.mult)
            e.tensor_tensor(stat[:, G:2 * G], stat[:, G:2 * G], stat[:, 2 * G:3 * G],
                            ALU.subtract).then_inc(sdv.h, 1)
        sdv.n += 1
        var_done = sdv.n
        ph.on("vector", fdv1)

        def fac(e, var_done=var_done):
            e.wait_ge(sdv.h, var_done)
            e.activation(stat[:, 3 * G:4 * G], stat[:, G:2 * G], AF.Ln, bias=eps_tile[:])
            e.activation(stat[:, 2 * G:3 * G], stat[:, 3 * G:4 * G], AF.Exp,
                         scale=-0.5).then_inc(sac.h, 1)
        sac.n += 1
        r_done = sac.n
        ph.on("scalar", fac)

        if not repl_hilo:
            def fdv2(e, r_done=r_done):
                e.wait_ge(sac.h, r_done)
                e.tensor_copy(statb[:, 0:G], stat[:, 2 * G:3 * G])
                e.tensor_tensor(stat[:, 3 * G:4 * G], stat[:, 0:G],
                                stat[:, 2 * G:3 * G], ALU.mult)
                e.tensor_scalar_mul(stat[:, 3 * G:4 * G], stat[:, 3 * G:4 * G], -1.0)
                e.tensor_copy(statb[:, G:2 * G], stat[:, 3 * G:4 * G]) \
                    .then_inc(sdv.h, 1)
            sdv.n += 1
            sb_done = sdv.n
            ph.on("vector", fdv2)

            def fpe2(e, sb_done=sb_done):
                e.wait_ge(sdv.h, sb_done)
                e.matmul(prep[:], self.ones_row[0:1, :], statb[:, 0:G],
                         start=True, stop=True)
                e.matmul(pbrep[:], self.ones_row[0:1, :], statb[:, G:2 * G],
                         start=True, stop=True).then_inc(spe.h, 1)
            spe.n += 1
            ph.on("tensor", fpe2)
        else:
            def fdv2(e, r_done=r_done):
                e.wait_ge(sac.h, r_done)
                e.tensor_tensor(stat[:, 3 * G:4 * G], stat[:, 0:G],
                                stat[:, 2 * G:3 * G], ALU.mult)
                e.tensor_scalar_mul(stat[:, 3 * G:4 * G], stat[:, 3 * G:4 * G], -1.0)
                e.tensor_copy(statb[:, 0:G], stat[:, 2 * G:3 * G])
                e.tensor_copy(statb[:, G:2 * G], stat[:, 3 * G:4 * G])
                e.tensor_tensor(stat[:, 4 * G:5 * G], stat[:, 2 * G:3 * G],
                                statb[:, 0:G], ALU.subtract)
                e.tensor_tensor(stat[:, 5 * G:6 * G], stat[:, 3 * G:4 * G],
                                statb[:, G:2 * G], ALU.subtract)
                e.tensor_copy(statb[:, 2 * G:3 * G], stat[:, 4 * G:5 * G])
                e.tensor_copy(statb[:, 3 * G:4 * G], stat[:, 5 * G:6 * G]) \
                    .then_inc(sdv.h, 1)
            sdv.n += 1
            sb_done = sdv.n
            ph.on("vector", fdv2)

            def fpe2(e, sb_done=sb_done):
                e.wait_ge(sdv.h, sb_done)
                e.matmul(prep[:], self.ones_row[0:1, :], statb[:, 0:G],
                         start=True, stop=False)
                e.matmul(prep[:], self.ones_row[0:1, :], statb[:, 2 * G:3 * G],
                         start=False, stop=True)
                e.matmul(pbrep[:], self.ones_row[0:1, :], statb[:, G:2 * G],
                         start=True, stop=False)
                e.matmul(pbrep[:], self.ones_row[0:1, :], statb[:, 3 * G:4 * G],
                         start=False, stop=True).then_inc(spe.h, 1)
            spe.n += 1
            ph.on("tensor", fpe2)
        return spe.n

    def ingest_ln1(self, s, XT, do_ln=True):
        nc = self.nc
        sq, sg, spe, sdv, sac = self.sq, self.sg, self.spe, self.sdv, self.sac
        with contextlib.ExitStack() as est:
            E = est.enter_context
            xtok = [E(nc.sbuf_tensor(_u(f"i_xtok{i}"), [128, D], F32)) for i in range(4)]
            pst_ = contextlib.ExitStack()
            ptr = [pst_.enter_context(nc.psum_tensor(_u(f"i_ptr{i}"), [128, 128], F32))
                   for i in range(4)]

            ph = Phase(nc, f"ingest{s}")
            NTT = NT
            for d in _DBG:
                if d.startswith("nt"):
                    NTT = int(d[2:])
            loads, trs, ev_dv, ev_ac = [], [], [], []
            for tt in range(NTT):
                self.sqp[tt % 2].n += 16
                loads.append(self.sqp[tt % 2].n)
                for c in range(DC):
                    spe.n += 1
                    trs.append(spe.n)
                    sdv.n += 1
                    ev_dv.append(sdv.n)
                    sac.n += 1
                    ev_ac.append(sac.n)

            def fsync(e):
                for tt in range(NTT):
                    if tt >= 4 and "loadsonly" not in _DBG:
                        e.wait_ge(spe.h, trs[(tt - 3) * DC - 1])
                    e.dma_start(out=xtok[tt % 4][:],
                                in_=self.d_x[s, tt * 128:(tt + 1) * 128, :]) \
                        .then_inc(self.sqp[tt % 2].h, 16)
                if "loadsonly" in _DBG:
                    e.wait_ge(self.sqp[0].h, self.sqp[0].n)
                    e.wait_ge(self.sqp[1].h, self.sqp[1].n)
            ph.on("sync", fsync)

            def fpe(e):
                if "loadsonly" in _DBG:
                    return
                k = 0
                for tt in range(NTT):
                    e.wait_ge(self.sqp[tt % 2].h, loads[tt])
                    for c in range(DC):
                        if k >= 4:
                            if "nodve" not in _DBG:
                                e.wait_ge(sdv.h, ev_dv[k - 4])
                            if "noact" not in _DBG:
                                e.wait_ge(sac.h, ev_ac[k - 4])
                        e.transpose(ptr[k % 4][:], xtok[tt % 4][:, c * 128:(c + 1) * 128],
                                    self.identf[:]).then_inc(spe.h, 1)
                        k += 1
            ph.on("tensor", fpe)

            def fdv(e):
                if "loadsonly" in _DBG or "nodve" in _DBG:
                    return
                k = 0
                for tt in range(NTT):
                    for c in range(DC):
                        e.wait_ge(spe.h, trs[k])
                        e.tensor_copy(XT[c][:, tt * 128:(tt + 1) * 128], ptr[k % 4][:]) \
                            .then_inc(sdv.h, 1)
                        k += 1
            ph.on("vector", fdv)

            def fac(e):
                if "loadsonly" in _DBG or "noact" in _DBG:
                    return
                k = 0
                for tt in range(NTT):
                    for c in range(DC):
                        e.wait_ge(sdv.h, ev_dv[k])
                        e.activation(self.XB[c][:, tt * 128:(tt + 1) * 128],
                                     ptr[k % 4][:], AF.Copy).then_inc(sac.h, 1)
                        k += 1
            ph.on("scalar", fac)

            if "nospill" not in _DBG:
                def fgp(e, w=ev_dv[-1]):
                    e.wait_ge(sdv.h, w)
                    for c in range(DC):
                        e.dma_start(out=self.d_xT[s, c], in_=XT[c][:]).then_inc(sg.h, 16)
                sg.n += 16 * DC
                ph.on("gpsimd", fgp)
            self.q_pump_A(ph, 64)
            self.drain(ph, gp=True)
            ph.emit()
            pst_.close()

            if not do_ln:
                return
            with contextlib.ExitStack() as lst:
                El = lst.enter_context
                sqb = El(nc.sbuf_tensor(_u("l1_sq"), [128, DC * G], BF16))
                stat = El(nc.sbuf_tensor(_u("l1_stat"), [1, 6 * G], F32))
                statb = El(nc.sbuf_tensor(_u("l1_statb"), [1, 4 * G], BF16))
                tmp = El(nc.sbuf_tensor(_u("l1_tmp"), [128, G], F32))
                psig = El(nc.psum_tensor(_u("l1_psig"), [1, G], F32))
                psig2 = El(nc.psum_tensor(_u("l1_psig2"), [1, G], F32))
                prep = El(nc.psum_tensor(_u("l1_prep"), [128, G], F32))
                pbrep = El(nc.psum_tensor(_u("l1_pbrep"), [128, G], F32))

                ph = Phase(nc, f"ln1_{s}")
                for g in range(NG):
                    gs = slice(g * G, (g + 1) * G)

                    def facq(e, gs=gs):
                        for c in range(DC):
                            e.activation(sqb[:, c * G:(c + 1) * G], self.XB[c][:, gs],
                                         AF.Square).then_inc(sac.h, 1)
                    sac.n += DC
                    sq_done = sac.n
                    ph.on("scalar", facq)

                    rep_done = self.ln_group(
                        ph, lst, f"ln1_{s}_{g}", g, DC,
                        [self.XB[c][:, gs] for c in range(DC)],
                        [sqb[:, c * G:(c + 1) * G] for c in range(DC)],
                        self.eps_ln, False, psig, psig2, prep, pbrep, stat, statb,
                        pe_extra_wait=(sac, sq_done))

                    def fap(e, gs=gs, rep_done=rep_done, XT=XT):
                        e.wait_ge(spe.h, rep_done)
                        for c in range(DC):
                            e.tensor_tensor(tmp[:], XT[c][:, gs], prep[:], ALU.mult)
                            e.tensor_tensor(self.XN[c][:, gs], tmp[:], pbrep[:],
                                            ALU.add)
                        e.last.then_inc(sdv.h, 1)
                    sdv.n += 1
                    ph.on("vector", fap)
                self.q_pump_thr(ph, lst)
                self.q_pump_C(ph, 14)
                ph.emit()

        self.tap_chunks(f"xn{s}", self.XN)
        self.tap_chunks(f"xt{s}", XT)

    def tap_chunks(self, name, chunks, width=T):
        if name not in self.taps:
            return
        nch = len(chunks)
        dt = chunks[0][:, 0:1].dtype
        d = self.tap_dram(name, [nch, 128, width], dt)
        ph = Phase(self.nc, f"tap_{name}")
        sst = self.sst

        def fs(e):
            for c in range(nch):
                e.dma_start(out=d[c], in_=chunks[c][:, 0:width]).then_inc(sst.h, 16)
        sst.n += 16 * nch
        ph.on("sync", fs)
        self.drain(ph, st=True)
        ph.emit()

    def tap_ap(self, name, ap_fn, shape, dtype):
        if name not in self.taps:
            return
        d = self.tap_dram(name, shape, dtype)
        ph = Phase(self.nc, f"tap_{name}")
        sst = self.sst
        pairs = ap_fn(d)

        def fs(e):
            for o, i in pairs:
                e.dma_start(out=o, in_=i).then_inc(sst.h, 16)
        sst.n += 16 * len(pairs)
        ph.on("sync", fs)
        self.drain(ph, st=True)
        ph.emit()

    def projections(self, s, est, QCb, KR, VT):
        kvst = contextlib.ExitStack()
        KVCb = [kvst.enter_context(
            self.nc.sbuf_tensor(_u(f"KVC{i}"), [128, T], BF16))
            for i in range(KVC)]
        """x_n -> kv_c (RMS+g), q_c (pre-scaled), k_rope (roped), k_content, V."""
        nc = self.nc
        sq, sg, spe, sdv, sac = self.sq, self.sg, self.spe, self.sdv, self.sac

        with contextlib.ExitStack() as wst:
            Ew = wst.enter_context
            wq = Ew(nc.sbuf_tensor(_u("p_wq"), [128, DC * QR], BF16))
            wkv = Ew(nc.sbuf_tensor(_u("p_wkv"), [128, DC * KVR], BF16))
            kvsq = Ew(nc.sbuf_tensor(_u("p_kvsq"), [128, KVC * G], BF16))
            preps = Ew(nc.sbuf_tensor(_u("p_preps"), [128, G], F32))
            stat = Ew(nc.sbuf_tensor(_u("p_stat"), [1, 4 * G], F32))
            statb = Ew(nc.sbuf_tensor(_u("p_statb"), [1, G], BF16))
            pkv = [Ew(nc.psum_tensor(_u(f"p_pkv{i}"), [128, G], F32)) for i in range(KVC)]
            pq = [Ew(nc.psum_tensor(_u(f"p_pq{i}"), [128, G], F32)) for i in range(2)]
            psig = Ew(nc.psum_tensor(_u("p_psig"), [1, G], F32))
            prep = Ew(nc.psum_tensor(_u("p_prep"), [128, G], F32))

            ph = Phase(nc, f"proj_kvq_{s}")



            def fgp(e):
                self.wload(e, wq, self.d_wq, QR, sg.h)
                self.wload(e, wkv, self.d_wkv, KVR, sg.h)
            sg.n += 32
            wkv_ld = sg.n
            ph.on("gpsimd", fgp)

            def fdvw(e, wqf_ld=wkv_ld):
                e.wait_ge(sg.h, wqf_ld)
                e.tensor_scalar_mul(wq[:], wq[:], SCALE_Q).then_inc(sdv.h, 1)
            sdv.n += 1
            wq_rdy = sdv.n
            ph.on("vector", fdvw)

            for g in range(NG):
                gs = slice(g * G, (g + 1) * G)
                def fpe(e, gs=gs, g=g, wkv_ld=wkv_ld, wq_rdy=wq_rdy, dvp=sdv.n):
                    e.wait_ge(sg.h, wkv_ld)
                    e.wait_ge(sdv.h, dvp)
                    for kc in range(KVC):
                        for k in range(DC):
                            e.matmul(pkv[kc][:],
                                     wkv[:, k * KVR + kc * 128:k * KVR + (kc + 1) * 128],
                                     self.XN[k][:, gs], start=(k == 0), stop=(k == DC - 1))
                    e.last.then_inc(spe.h, 1)
                spe.n += 1
                kv_done = spe.n
                ph.on("tensor", fpe)

                def facs(e, kv_done=kv_done):
                    e.wait_ge(spe.h, kv_done)
                    for kc in range(KVC):
                        e.activation(kvsq[:, kc * G:(kc + 1) * G], pkv[kc][:],
                                     AF.Square).then_inc(sac.h, 1)
                sac.n += KVC
                kvsq_done = sac.n
                ph.on("scalar", facs)

                def fpe_s(e, kvsq_done=kvsq_done, dvp=sdv.n):
                    e.wait_ge(sac.h, kvsq_done)
                    e.wait_ge(sdv.h, dvp)
                    for kc in range(KVC):
                        e.matmul(psig[:], self.ones_col[:], kvsq[:, kc * G:(kc + 1) * G],
                                 start=(kc == 0), stop=(kc == KVC - 1))
                    e.last.then_inc(spe.h, 1)
                spe.n += 1
                sig_done = spe.n
                ph.on("tensor", fpe_s)

                def fdv_r(e, sig_done=sig_done):
                    e.wait_ge(spe.h, sig_done)
                    e.tensor_scalar_mul(stat[:, 0:G], psig[:], 1.0 / KVR) \
                        .then_inc(sdv.h, 1)
                sdv.n += 1
                ms_done = sdv.n
                ph.on("vector", fdv_r)

                def fac_r(e, ms_done=ms_done):
                    e.wait_ge(sdv.h, ms_done)
                    e.activation(stat[:, G:2 * G], stat[:, 0:G], AF.Ln,
                                 bias=self.eps_kv[:])
                    e.activation(stat[:, 2 * G:3 * G], stat[:, G:2 * G], AF.Exp,
                                 scale=-0.5).then_inc(sac.h, 1)
                sac.n += 1
                r_done = sac.n
                ph.on("scalar", fac_r)

                def fdv_c(e, r_done=r_done):
                    e.wait_ge(sac.h, r_done)
                    e.tensor_copy(statb[:, 0:G], stat[:, 2 * G:3 * G]).then_inc(sdv.h, 1)
                sdv.n += 1
                rb_done = sdv.n
                ph.on("vector", fdv_c)

                def fpe_r(e, rb_done=rb_done):
                    e.wait_ge(sdv.h, rb_done)
                    e.matmul(prep[:], self.ones_row[0:1, :], statb[:, 0:G],
                             start=True, stop=True).then_inc(spe.h, 1)
                spe.n += 1
                rep_done = spe.n
                ph.on("tensor", fpe_r)

                def fdv_a(e, gs=gs, rep_done=rep_done):
                    e.wait_ge(spe.h, rep_done)
                    e.tensor_copy(preps[:], prep[:])
                    for kc in range(KVC):
                        e.scalar_tensor_tensor(KVCb[kc][:, gs], pkv[kc][:],
                                               self.gkv[:, kc:kc + 1], preps[:],
                                               ALU.mult, ALU.mult)
                    e.last.then_inc(sdv.h, 1)
                sdv.n += 1
                ph.on("vector", fdv_a)

                qdv_marks = [sdv.n] * 2
                for qc in range(QC):
                    def fpe_q(e, gs=gs, qc=qc, wq_rdy=wq_rdy, dvp=qdv_marks[qc]):
                        e.wait_ge(sdv.h, max(wq_rdy, dvp))
                        pb = pq[qc % 2]
                        for k in range(DC):
                            e.matmul(pb[:],
                                     wq[:, k * QR + qc * 128:k * QR + (qc + 1) * 128],
                                     self.XN[k][:, gs], start=(k == 0), stop=(k == DC - 1))
                        e.last.then_inc(spe.h, 1)
                    spe.n += 1
                    q_done = spe.n
                    ph.on("tensor", fpe_q)

                    def fdv_q(e, gs=gs, qc=qc, q_done=q_done):
                        e.wait_ge(spe.h, q_done)
                        e.tensor_copy(QCb[qc][:, gs], pq[qc % 2][:]).then_inc(sdv.h, 1)
                    sdv.n += 1
                    qdv_marks.append(sdv.n)
                    ph.on("vector", fdv_q)
            self.q_pump_C(ph, 18)
            ph.emit()

        with contextlib.ExitStack() as wst:
            Ew = wst.enter_context
            wkr = Ew(nc.sbuf_tensor(_u("p_wkr"), [128, DC * H * RD], BF16))
            pkr = [Ew(nc.psum_tensor(_u(f"p_pkr{i}"), [128, G], F32)) for i in range(2)]
            rt1 = Ew(nc.sbuf_tensor(_u("p_rt1"), [128, G], BF16))

            ph = Phase(nc, f"proj_krope_{s}")

            def fgp(e):
                self.wload(e, wkr, self.d_wkrope, H * RD, sg.h)
            sg.n += 16
            wkr_ld = sg.n
            ph.on("gpsimd", fgp)

            RC = H * RD // 128
            for g in range(NG):
                gs = slice(g * G, (g + 1) * G)
                for rc in range(RC):
                    def fpe(e, gs=gs, rc=rc, wkr_ld=wkr_ld, dvp=sdv.n):
                        e.wait_ge(sg.h, wkr_ld)
                        e.wait_ge(sdv.h, dvp)
                        for k in range(DC):
                            e.matmul(pkr[rc % 2][:],
                                     wkr[:, k * 512 + rc * 128:k * 512 + (rc + 1) * 128],
                                     self.XN[k][:, gs], start=(k == 0), stop=(k == DC - 1))
                        e.last.then_inc(spe.h, 1)
                    spe.n += 1
                    mm_done = spe.n
                    ph.on("tensor", fpe)

                    def fdv(e, gs=gs, rc=rc, mm_done=mm_done):
                        e.wait_ge(spe.h, mm_done)
                        p = pkr[rc % 2]
                        ks_ = KR[rc][:, gs]
                        e.tensor_tensor(rt1[:], p[:], self.ssinT[:, gs], ALU.mult)
                        e.stream_shuffle(ks_, rt1[:], mask=[(i + 16) % 32
                                                            for i in range(32)])
                        e.tensor_tensor(rt1[:], p[:], self.cosT[:, gs], ALU.mult)
                        e.tensor_tensor(ks_, ks_, rt1[:], ALU.add).then_inc(sdv.h, 1)
                    sdv.n += 1
                    ph.on("vector", fdv)
            self.q_pump_C(ph, 16)
            ph.emit()

        with contextlib.ExitStack() as wst:
            Ew = wst.enter_context
            wkd = Ew(nc.sbuf_tensor(_u("p_wkd"), [128, KVC * D], BF16))
            wvd = Ew(nc.sbuf_tensor(_u("p_wvd"), [128, KVC * D], BF16))
            pkd = [Ew(nc.psum_tensor(_u(f"p_pkd{i}"), [128, G], F32)) for i in range(2)]
            pv = Ew(nc.psum_tensor(_u("p_pv"), [128, D], F32))

            ph = Phase(nc, f"proj_dec_{s}")

            def fgp(e):
                self.wload(e, wkd, self.d_wkdec, D, sg.h)
                self.wload(e, wvd, self.d_wvdec, D, sg.h)
            sg.n += 32
            wd_ld = sg.n
            ph.on("gpsimd", fgp)

            for g in range(NG):
                gs = slice(g * G, (g + 1) * G)
                for c in range(DC):
                    def fpe(e, gs=gs, c=c, wd_ld=wd_ld, dvp=sdv.n):
                        e.wait_ge(sg.h, wd_ld)
                        e.wait_ge(sdv.h, dvp)
                        for kc in range(KVC):
                            e.matmul(pkd[c % 2][:],
                                     wkd[:, kc * D + c * 128:kc * D + (c + 1) * 128],
                                     KVCb[kc][:, gs], start=(kc == 0), stop=(kc == KVC - 1))
                        e.last.then_inc(spe.h, 1)
                    spe.n += 1
                    mm_done = spe.n
                    ph.on("tensor", fpe)

                    def fdv(e, gs=gs, c=c, mm_done=mm_done):
                        e.wait_ge(spe.h, mm_done)
                        e.tensor_copy(self.XB[c][:, gs], pkd[c % 2][:]).then_inc(sdv.h, 1)
                    sdv.n += 1
                    ph.on("vector", fdv)

            for tt in range(NT):
                def fpe(e, tt=tt, wd_ld=wd_ld, dvp=sdv.n):
                    e.wait_ge(sg.h, wd_ld)
                    e.wait_ge(sdv.h, dvp)
                    for half in range(2):
                        for kc in range(KVC):
                            e.matmul(pv[:, half * 512:(half + 1) * 512],
                                     KVCb[kc][:, tt * 128:(tt + 1) * 128],
                                     wvd[:, kc * D + half * 512:kc * D + (half + 1) * 512],
                                     start=(kc == 0), stop=(kc == KVC - 1))
                    e.last.then_inc(spe.h, 1)
                spe.n += 1
                mm_done = spe.n
                ph.on("tensor", fpe)

                def fdv(e, tt=tt, mm_done=mm_done):
                    e.wait_ge(spe.h, mm_done)
                    e.tensor_copy(
                        VT[:, tt * (H * 65):(tt + 1) * (H * 65)]
                        .rearrange("p (h j) -> p h j", j=65)[:, :, 0:64],
                        pv[:].rearrange("p (h j) -> p h j", j=64)) \
                        .then_inc(sdv.h, 1)
                sdv.n += 1
                ph.on("vector", fdv)

            def fdv_ones(e):
                e.memset(VT[:].rearrange("p (th j) -> p th j", j=65)[:, :, 64:65], 1.0)
                e.last.then_inc(sdv.h, 1)
            sdv.n += 1
            ph.on("vector", fdv_ones)
            self.q_pump_C(ph, 64)
            self.drain(ph, gp=True)
            ph.emit()
        kvst.close()
        if getattr(self, "_q", None) and self._q["ci"] >= self._q["nA"]:
            self._q = None
            self._qes.close()

        self.tap_chunks(f"kvc{s}", KVCb)
        self.tap_chunks(f"qc{s}", QCb)
        self.tap_chunks(f"kr{s}", KR)
        self.tap_chunks(f"kc{s}", self.XB)
        self.tap_chunks(f"vt{s}", [VT], width=NT * H * 65)

    def attention(self, s, est, QCb, KR, VT):
        nc = self.nc
        sq, sg, spe, sdv, sac = self.sq, self.sg, self.spe, self.sdv, self.sac
        E = est.enter_context
        wqd = E(nc.sbuf_tensor(_u("a_wqd"), [128, QC * D], BF16))
        wqr = E(nc.sbuf_tensor(_u("a_wqr"), [128, QC * H * RD], BF16))
        QCZ = E(nc.sbuf_tensor(_u("a_qcz"), [128, H * G], BF16))
        QRZ = E(nc.sbuf_tensor(_u("a_qrz"), [128, H * G], BF16))
        PT = [E(nc.sbuf_tensor(_u(f"a_pt{i}"), [128, 4 * G], BF16)) for i in range(2)]
        OST = E(nc.sbuf_tensor(_u("a_ost"), [128, 8 * G], BF16))
        RL = E(nc.sbuf_tensor(_u("a_rl"), [128, 2 * G], BF16))
        RLS = E(nc.sbuf_tensor(_u("a_rls"), [128, 2 * G], BF16))
        RLB = E(nc.sbuf_tensor(_u("a_rlb"), [128, 2 * G], BF16))
        SCP = E(nc.psum_tensor(_u("a_scp"), [128, 4 * G], F32))
        PO = [E(nc.psum_tensor(_u(f"a_po{i}"), [65, G], F32)) for i in range(2)]
        PRL = [E(nc.psum_tensor(_u(f"a_prl{i}"), [65, G], F32)) for i in range(2)]

        ph = Phase(nc, f"attn_{s}")

        def fz0(e):
            e.memset(QCZ[:], 0.0)
            e.memset(QRZ[:], 0.0)
            e.memset(RLB[:], 0.0)
        ph.on("vector", fz0)
        M = {}
        pe_ops, dv_ops, ac_ops, gp_ops = [], [], [], []

        def fgp0(e):
            self.wload(e, wqd, self.d_wqdec, D, sg.h)
            self.wload(e, wqr, self.d_wqrope, H * RD, sg.h)
        sg.n += 32
        wq_ld = sg.n
        ph.on("gpsimd", fgp0)

        last_oev = [0, 0]
        prev_odma = 0
        pend_prl = []
        wslice = [0, 0, 0]
        sc_slot_ev = [0] * 4
        prev_win_exp = 0
        mask_shuf = [(i + 16) % 32 for i in range(32)]

        for g in range(NG):
            gs = slice(g * G, (g + 1) * G)
            KT, NP = 4 * (g + 1), 2 * (g + 1)

            qev_start = sdv.n
            for c in range(12):
                slot = c % 4
                par, half = slot % 2, slot // 2
                spe.n += 1
                M[("qp", g, c)] = spe.n
                pe_ops.append(("qproj", g, gs, c, par, half,
                               prev_win_exp if c == 0 else 0,
                               sc_slot_ev[slot], wq_ld if (g == 0 and c == 0) else 0))
                sdv.n += 1
                M[("qe", g, c)] = sdv.n
                sc_slot_ev[slot] = sdv.n
                dv_ops.append(("qevac", g, gs, c, par, half))
            qev_all = sdv.n

            for hp in range(8):
                h0, h1 = 2 * hp, 2 * hp + 1
                for pi in range(NP):
                    sac.n += 1
                    M[("exp", g, h0, pi)] = sac.n
                    sac.n += 1
                    M[("exp", g, h1, pi)] = sac.n
                prev_win_exp = sac.n
                for pi in range(NP):
                    if pi * 2 >= 4 * g:
                        for h in (h0, h1):
                            sdv.n += 1
                            M[("mask", g, h, pi)] = sdv.n
                            dv_ops.append(("mask", g, h, pi))
                for h in (h0, h1):
                    sac.n += 2
                    M[("rlb", g, h)] = sac.n
                for h in (h0, h1):
                    sdv.n += 1
                    M[("oev", g, h)] = sdv.n
                    dv_ops.append(("oev", g, h,
                                   prev_odma if (h == 1 and g > 0) else 0))
                po_wait = (last_oev[0], last_oev[1])
                for pi in range(NP):
                    for h in (h0, h1):
                        spe.n += 1
                        M[("qk", g, h, pi)] = spe.n
                        pe_ops.append(("qk", g, gs, h, pi,
                                       qev_all if pi == 0 else 0))
                    if pi == 0 and pend_prl:
                        for op in pend_prl:
                            spe.n += 1
                            M[("prl",) + op[1:3]] = spe.n
                            pe_ops.append(op)
                        pend_prl.clear()
                    if pi >= 1:
                        for h in (h0, h1):
                            spe.n += 1
                            M[("pv", g, h, pi - 1)] = spe.n
                            pe_ops.append(("pv", g, h, pi - 1, pi - 1 == 0,
                                           False, po_wait[h % 2]))
                for h in (h0, h1):
                    spe.n += 1
                    M[("pv", g, h, NP - 1)] = spe.n
                    pe_ops.append(("pv", g, h, NP - 1, NP == 1, True,
                                   po_wait[h % 2] if NP == 1 else 0))
                for h in (h0, h1):
                    pend_prl.append(("prl", g, h, po_wait[h % 2]))
                    last_oev[h % 2] = M[("oev", g, h)]
            gp_ops.append(("odma", g, sdv.n, list(pend_prl)))
            for op in pend_prl:
                spe.n += 1
                M[("prl",) + op[1:3]] = spe.n
                pe_ops.append(op)
            pend_prl.clear()
            sg.n += 16 * 8
            prev_odma = sg.n
            ph.on("tensor", lambda e, a=wslice[0], b=len(pe_ops): _fpe(e, a, b))
            ph.on("vector", lambda e, a=wslice[1], b=len(dv_ops): _fdv(e, a, b))
            ph.on("scalar", lambda e, g=g: _fac(e, g))
            ph.on("gpsimd", lambda e, a=wslice[2], b=len(gp_ops): _fgp(e, a, b))
            wslice[0], wslice[1], wslice[2] = len(pe_ops), len(dv_ops), len(gp_ops)

        def _fpe(e, a, b):
            for op in pe_ops[a:b]:
                if op[0] == "qproj":
                    _, g, gs, c, par, half, w_pexp, w_ev, w_ld = op
                    if w_ld:
                        e.wait_ge(sg.h, w_ld)
                    if w_pexp:
                        e.wait_ge(sac.h, w_pexp)
                    if w_ev:
                        e.wait_ge(sdv.h, w_ev)
                    dst = SCP[:, (par * 2 + half) * G:(par * 2 + half + 1) * G]
                    if c < 8:
                        for qc in range(QC):
                            e.matmul(dst, wqd[:, qc * D + c * 128:qc * D + (c + 1) * 128],
                                     QCb[qc][:, gs], start=(qc == 0), stop=(qc == QC - 1))
                    else:
                        rc = c - 8
                        for qc in range(QC):
                            e.matmul(dst, wqr[:, qc * 512 + rc * 128:qc * 512 + (rc + 1) * 128],
                                     QCb[qc][:, gs], start=(qc == 0), stop=(qc == QC - 1))
                    e.last.then_inc(spe.h, 1)
                elif op[0] == "qk":
                    _, g, gs, h, pi, w_qev = op
                    hc = h // 2
                    rc4 = h // 4
                    if w_qev:
                        e.wait_ge(sdv.h, w_qev)
                    if pi >= 1:
                        e.wait_ge(sac.h, M[("exp", g, h, pi - 1)])
                    for j in range(2):
                        kt = pi * 2 + j
                        ks = slice(kt * 128, (kt + 1) * 128)
                        dst = SCP[:, (h % 2) * 2 * G + j * G:
                                  (h % 2) * 2 * G + (j + 1) * G]
                        e.matmul(dst, self.XB[hc][:, ks],
                                 QCZ[:, h * G:(h + 1) * G],
                                 start=True, stop=False)
                        e.matmul(dst, KR[rc4][:, ks],
                                 QRZ[:, h * G:(h + 1) * G],
                                 start=False, stop=True)
                    e.last.then_inc(spe.h, 1)
                elif op[0] == "pv":
                    _, g, h, pi, first, last_, w_oev = op
                    e.wait_ge(sac.h, M[("exp", g, h, pi)])
                    if first and w_oev:
                        e.wait_ge(sdv.h, w_oev)
                    pt = PT[pi % 2]
                    for j in range(2):
                        kt = pi * 2 + j
                        vs = VT[:, kt * (H * 65) + h * 65:kt * (H * 65) + (h + 1) * 65]
                        e.matmul(PO[h % 2][:], vs,
                                 pt[:, (h % 2) * 2 * G + j * G:
                                    (h % 2) * 2 * G + (j + 1) * G],
                                 start=(first and j == 0), stop=(last_ and j == 1))
                    e.last.then_inc(spe.h, 1)
                elif op[0] == "prl":
                    _, g, h, w_oev = op
                    e.wait_ge(sac.h, M[("rlb", g, h)])
                    if w_oev:
                        e.wait_ge(sdv.h, w_oev)
                    e.matmul(PRL[h % 2][:], self.prl65[:],
                             RLB[0:65, (h % 2) * G:(h % 2 + 1) * G],
                             start=True, stop=True).then_inc(spe.h, 1)
        def _fdv(e, a, b):
            for op in dv_ops[a:b]:
                if op[0] == "qevac":
                    _, g, gs, c, par, half = op
                    e.wait_ge(spe.h, M[("qp", g, c)])
                    src = SCP[:, (par * 2 + half) * G:(par * 2 + half + 1) * G]
                    if c < 8:
                        e.tensor_copy(QCZ[0:64, 2 * c * G:(2 * c + 1) * G],
                                      src[0:64, :])
                        e.tensor_copy(QCZ[64:128, (2 * c + 1) * G:(2 * c + 2) * G],
                                      src[64:128, :]).then_inc(sdv.h, 1)
                    else:
                        rc = c - 8
                        ta, tb = PT[0][:, 0:G], PT[0][:, G:2 * G]
                        e.tensor_tensor(ta, src, self.ssinT[:, gs], ALU.mult)
                        e.stream_shuffle(tb, ta, mask=mask_shuf)
                        e.tensor_tensor(ta, src, self.cosT[:, gs], ALU.mult)
                        e.tensor_tensor(tb, tb, ta, ALU.add)
                        for qb in range(4):
                            hh = 4 * rc + qb
                            e.tensor_copy(
                                QRZ[32 * qb:32 * qb + 32, hh * G:(hh + 1) * G],
                                PT[0][32 * qb:32 * qb + 32, G:2 * G])
                        e.last.then_inc(sdv.h, 1)
                elif op[0] == "mask":
                    _, g, h, pi = op
                    e.wait_ge(spe.h, M[("qk", g, h, pi)])
                    for j in range(2):
                        kt = pi * 2 + j
                        off = 384 - 128 * (kt - 4 * g)
                        sl = slice((h % 2) * 2 * G + j * G,
                                   (h % 2) * 2 * G + (j + 1) * G)
                        e.tensor_tensor(SCP[:, sl], SCP[:, sl],
                                        self.maskw[:, off:off + G], ALU.add)
                    e.last.then_inc(sdv.h, 1)
                elif op[0] == "oev":
                    _, g, h, w_odma = op
                    sel = h % 2
                    if w_odma:
                        e.wait_ge(sg.h, w_odma)
                    e.wait_ge(spe.h, M[("prl", g, h)])
                    if h % 2 == 0:
                        dst = self.XN[h // 2][0:64, g * G:(g + 1) * G]
                    else:
                        dst = OST[0:64, (h // 2) * G:(h // 2 + 1) * G]
                    ls = slice(sel * G, (sel + 1) * G)
                    e.tensor_copy(RLS[0:64, ls], PRL[sel][0:64, :])
                    e.tensor_tensor(dst, PO[sel][0:64, :], RLS[0:64, ls],
                                    ALU.mult).then_inc(sdv.h, 1)
        def _fac(e, g):
            if True:
                NP = 2 * (g + 1)
                for hp in range(8):
                    h0, h1 = 2 * hp, 2 * hp + 1
                    for pi in range(NP):
                        diag = (pi * 2 >= 4 * g)
                        for h in (h0, h1):
                            sel = h % 2
                            if diag:
                                e.wait_ge(sdv.h, M[("mask", g, h, pi)])
                            else:
                                e.wait_ge(spe.h, M[("qk", g, h, pi)])
                            if pi >= 2:
                                e.wait_ge(spe.h, M[("pv", g, h, pi - 2)])
                            e.activation(
                                PT[pi % 2][:, sel * 2 * G:(sel + 1) * 2 * G],
                                SCP[:, sel * 2 * G:(sel + 1) * 2 * G],
                                AF.Exp, bias=self.negc[:]).then_inc(sac.h, 1)
                    for h in (h0, h1):
                        sel = h % 2
                        ls = slice(sel * G, (sel + 1) * G)
                        e.wait_ge(spe.h, M[("pv", g, h, NP - 1)])
                        e.activation(RL[64:65, ls], PO[sel][64:65, :], AF.Ln) \
                            .then_inc(sac.h, 1)
                        e.activation(RLB[64:65, ls], RL[64:65, ls], AF.Exp,
                                     scale=-1.0).then_inc(sac.h, 1)
        def _fgp(e, a, b):
            for op in gp_ops[a:b]:
                _, g, w_dv = op[:3]
                e.wait_ge(sdv.h, w_dv)
                for c in range(DC):
                    e.dma_start(out=self.XN[c][64:128, g * G:(g + 1) * G],
                                in_=OST[0:64, c * G:(c + 1) * G]) \
                        .then_inc(sg.h, 16)
        self.drain(ph, gp=True)
        ph.emit()

        self.tap_chunks(f"ot{s}", self.XN)

    def ln_stage(self, M, tag, src_bf, src_sq, n_chunks, eps_tile, hilo,
                 psig, psig2, prep, pbrep, stat, statb, pe_list, dv_list, ac_list,
                 w_src_sac, w_psig_free, w_prep_free):
        spe, sdv, sac = self.spe, self.sdv, self.sac
        nf = float(n_chunks * 128)
        spe.n += 1
        M[("stats", tag)] = spe.n
        pe_list.append(("stats", tag, src_bf, src_sq, w_src_sac, w_psig_free,
                        psig, psig2))
        sdv.n += 1
        M[("fdv1", tag)] = sdv.n
        dv_list.append(("fdv1", tag, nf, psig, psig2, stat))
        sac.n += 1
        M[("rstd", tag)] = sac.n
        ac_list.append(("rstd", tag, eps_tile, stat))
        sdv.n += 1
        M[("statb", tag)] = sdv.n
        dv_list.append(("statb", tag, hilo, stat, statb))

    def ln_stage_repl(self, M, tag, hilo, statb, prep, pbrep, pe_list, w_prep_free):
        spe = self.spe
        spe.n += 1
        M[("repl", tag)] = spe.n
        pe_list.append(("repl", tag, hilo, statb, prep, pbrep, w_prep_free))

    def emit_ln_pe(self, e, op, M):
        spe, sdv, sac = self.spe, self.sdv, self.sac
        if op[0] == "stats":
            _, tag, src_bf, src_sq, w_sac, w_free, psig, psig2 = op
            e.wait_ge(sac.h, w_sac if not callable(w_sac) else w_sac())
            wf = w_free() if callable(w_free) else w_free
            if wf:
                e.wait_ge(sdv.h, wf)
            n = len(src_bf)
            for c in range(n):
                e.matmul(psig[:], self.ones_col[:], src_bf[c],
                         start=(c == 0), stop=(c == n - 1))
            for c in range(n):
                e.matmul(psig2[:], self.ones_col[:], src_sq[c],
                         start=(c == 0), stop=(c == n - 1))
            e.last.then_inc(spe.h, 1)
        elif op[0] == "repl":
            _, tag, hilo, statb, prep, pbrep, w_prep = op
            e.wait_ge(sdv.h, M[("statb", tag)])
            wp = w_prep() if callable(w_prep) else w_prep
            if wp:
                e.wait_ge(sdv.h, wp)
            if hilo:
                e.matmul(prep[:], self.ones_row[0:1, :], statb[:, 0:G],
                         start=True, stop=False)
                e.matmul(prep[:], self.ones_row[0:1, :], statb[:, 2 * G:3 * G],
                         start=False, stop=True)
                e.matmul(pbrep[:], self.ones_row[0:1, :], statb[:, G:2 * G],
                         start=True, stop=False)
                e.matmul(pbrep[:], self.ones_row[0:1, :], statb[:, 3 * G:4 * G],
                         start=False, stop=True).then_inc(spe.h, 1)
            else:
                e.matmul(prep[:], self.ones_row[0:1, :], statb[:, 0:G],
                         start=True, stop=True)
                e.matmul(pbrep[:], self.ones_row[0:1, :], statb[:, G:2 * G],
                         start=True, stop=True).then_inc(spe.h, 1)

    def emit_ln_dv(self, e, op, M):
        spe, sdv, sac = self.spe, self.sdv, self.sac
        if op[0] == "fdv1":
            _, tag, nf, psig, psig2, stat = op
            e.wait_ge(spe.h, M[("stats", tag)])
            e.tensor_scalar_mul(stat[:, 0:G], psig[:], 1.0 / nf)
            e.tensor_scalar_mul(stat[:, G:2 * G], psig2[:], 1.0 / nf)
            e.tensor_tensor(stat[:, 2 * G:3 * G], stat[:, 0:G], stat[:, 0:G],
                            ALU.mult)
            e.tensor_tensor(stat[:, G:2 * G], stat[:, G:2 * G],
                            stat[:, 2 * G:3 * G], ALU.subtract).then_inc(sdv.h, 1)
        elif op[0] == "statb":
            _, tag, hilo, stat, statb = op
            e.wait_ge(sac.h, M[("rstd", tag)])
            if hilo:
                e.tensor_tensor(stat[:, 3 * G:4 * G], stat[:, 0:G],
                                stat[:, 2 * G:3 * G], ALU.mult)
                e.tensor_scalar_mul(stat[:, 3 * G:4 * G], stat[:, 3 * G:4 * G], -1.0)
                e.tensor_copy(statb[:, 0:G], stat[:, 2 * G:3 * G])
                e.tensor_copy(statb[:, G:2 * G], stat[:, 3 * G:4 * G])
                e.tensor_tensor(stat[:, 4 * G:5 * G], stat[:, 2 * G:3 * G],
                                statb[:, 0:G], ALU.subtract)
                e.tensor_tensor(stat[:, 5 * G:6 * G], stat[:, 3 * G:4 * G],
                                statb[:, G:2 * G], ALU.subtract)
                e.tensor_copy(statb[:, 2 * G:3 * G], stat[:, 4 * G:5 * G])
                e.tensor_copy(statb[:, 3 * G:4 * G], stat[:, 5 * G:6 * G]) \
                    .then_inc(sdv.h, 1)
            else:
                e.tensor_copy(statb[:, 0:G], stat[:, 2 * G:3 * G])
                e.tensor_tensor(stat[:, 3 * G:4 * G], stat[:, 0:G],
                                stat[:, 2 * G:3 * G], ALU.mult)
                e.tensor_scalar_mul(stat[:, 3 * G:4 * G], stat[:, 3 * G:4 * G], -1.0)
                e.tensor_copy(statb[:, G:2 * G], stat[:, 3 * G:4 * G]) \
                    .then_inc(sdv.h, 1)

    def emit_ln_ac(self, e, op, M):
        sdv, sac = self.sdv, self.sac
        _, tag, eps_tile, stat = op
        e.wait_ge(sdv.h, M[("fdv1", tag)])
        e.activation(stat[:, 3 * G:4 * G], stat[:, G:2 * G], AF.Ln, bias=eps_tile[:])
        e.activation(stat[:, 2 * G:3 * G], stat[:, 3 * G:4 * G], AF.Exp,
                     scale=-0.5).then_inc(sac.h, 1)

    def wo_ln2_ln3(self, s):
        nc = self.nc
        sq, sg, spe, sdv, sac = self.sq, self.sg, self.spe, self.sdv, self.sac
        with contextlib.ExitStack() as est:
            E = est.enter_context
            wo = E(nc.sbuf_tensor(_u("o_wo"), [128, DC * D], BF16))
            OBF = [E(nc.sbuf_tensor(_u(f"o_obf{i}"), [128, DC * G], BF16))
                   for i in range(2)]
            OSQ = [E(nc.sbuf_tensor(_u(f"o_osq{i}"), [128, DC * G], BF16))
                   for i in range(2)]
            YG = [E(nc.sbuf_tensor(_u(f"o_yg{i}"), [128, DC * G], F32))
                  for i in range(2)]
            YBF = E(nc.sbuf_tensor(_u("o_ybf"), [128, DC * G], BF16))
            YSQ = E(nc.sbuf_tensor(_u("o_ysq"), [128, DC * G], BF16))
            stat = E(nc.sbuf_tensor(_u("o_stat"), [1, 6 * G], F32))
            statb = E(nc.sbuf_tensor(_u("o_statb"), [1, 4 * G], BF16))
            tmp = E(nc.sbuf_tensor(_u("o_tmp"), [128, G], F32))
            po = [E(nc.psum_tensor(_u(f"o_po{i}"), [128, G], F32)) for i in range(2)]
            psig = E(nc.psum_tensor(_u("o_psig"), [1, G], F32))
            psig2 = E(nc.psum_tensor(_u("o_psig2"), [1, G], F32))
            prep = E(nc.psum_tensor(_u("o_prep"), [128, G], F32))
            pbrep = E(nc.psum_tensor(_u("o_pbrep"), [128, G], F32))

            ph = Phase(nc, f"wo_{s}")
            M = {}
            pe_ops, dv_ops, ac_ops, gp_ops = [], [], [], []

            def fgp0(e):
                self.wload(e, wo, self.d_wo, D, sg.h)
            sg.n += 16
            wo_ld = sg.n
            ph.on("gpsimd", fgp0)

            def sched_chains(g):
                for c in range(DC):
                    spe.n += 1
                    M[("woc", g, c)] = spe.n
                    pe_ops.append(("woc", g, c,
                                   wo_ld if (g == 0 and c == 0) else 0))
                    sac.n += 1
                    M[("oev", g, c)] = sac.n
                    ac_ops.append(("oev", g, c))

            def sched_stage(kind, g):
                tag = (s, kind, g)
                if kind == 2:
                    src_bf = [OBF[g % 2][:, c * G:(c + 1) * G] for c in range(DC)]
                    src_sq = [OSQ[g % 2][:, c * G:(c + 1) * G] for c in range(DC)]
                    w_sac = M[("oev", g, DC - 1)]
                else:
                    src_bf = [YBF[:, c * G:(c + 1) * G] for c in range(DC)]
                    src_sq = [YSQ[:, c * G:(c + 1) * G] for c in range(DC)]
                    w_sac = M[("ysq", g)]
                w_psig = self._last_fdv1[0]
                w_prep = self._last_apply[0]
                self.ln_stage(M, tag, src_bf, src_sq, DC, self.eps_ln, True,
                              psig, psig2, prep, pbrep, stat, statb,
                              pe_ops_stats, dv_ops, ac_ops, w_sac, w_psig, w_prep)
                self.ln_stage_repl(M, tag, True, statb, prep, pbrep,
                                   pe_ops_stats, w_prep)
                self._last_fdv1[0] = M[("fdv1", tag)]
                if kind == 2:
                    sdv.n += 1
                    M[("applyu", g)] = sdv.n
                    dv_ops.append(("applyu", g))
                    self._last_apply[0] = sdv.n
                    gp_ops.append(("xacc", g, sdv.n))
                    self.sgp2[g % 2].n += 16 * DC
                    M[("xacc", g)] = self.sgp2[g % 2].n
                    sdv.n += 1
                    M[("ybf", g)] = sdv.n
                    dv_ops.append(("ybf", g))
                    sac.n += 1
                    M[("ysq", g)] = sac.n
                    ac_ops.append(("ysq", g))
                else:
                    sdv.n += 1
                    M[("applyz", g)] = sdv.n
                    dv_ops.append(("applyz", g))
                    self._last_apply[0] = sdv.n
                    gp_ops.append(("yspill", g, sdv.n))
                    self.sgp2[g % 2].n += 16 * DC
                    M[("yspill", g)] = self.sgp2[g % 2].n

            self._last_fdv1 = [0]
            self._last_apply = [0]
            pe_ops_stats = pe_ops
            sched_chains(0)
            sched_chains(1)
            sched_stage(2, 0)
            sched_chains(2)
            sched_stage(2, 1)
            sched_stage(3, 0)
            sched_chains(3)
            sched_stage(2, 2)
            sched_stage(3, 1)
            sched_stage(2, 3)
            sched_stage(3, 2)
            sched_stage(3, 3)

            def fpe(e, ops=tuple(pe_ops)):
                for op in ops:
                    if op[0] == "woc":
                        _, g, c, w_ld = op
                        if w_ld:
                            e.wait_ge(sg.h, w_ld)
                        gc = g * DC + c
                        if gc >= 2:
                            pg, pc = divmod(gc - 2, DC)
                            e.wait_ge(sac.h, M[("oev", pg, pc)])
                        gs = slice(g * G, (g + 1) * G)
                        for k in range(DC):
                            e.matmul(po[c % 2][:],
                                     wo[:, k * D + c * 128:k * D + (c + 1) * 128],
                                     self.XN[k][:, gs], start=(k == 0),
                                     stop=(k == DC - 1))
                        e.last.then_inc(spe.h, 1)
                    else:
                        self.emit_ln_pe(e, op, M)
            ph.on("tensor", fpe)

            def fac(e, ops=tuple(ac_ops)):
                for op in ops:
                    if op[0] == "oev":
                        _, g, c = op
                        e.wait_ge(spe.h, M[("woc", g, c)])
                        if ("stats", (s, 2, g - 2)) in M:
                            e.wait_ge(spe.h, M[("stats", (s, 2, g - 2))])
                        e.activation(OBF[g % 2][:, c * G:(c + 1) * G], po[c % 2][:],
                                     AF.Copy)
                        e.activation(OSQ[g % 2][:, c * G:(c + 1) * G], po[c % 2][:],
                                     AF.Square).then_inc(sac.h, 1)
                    elif op[0] == "ysq":
                        _, g = op
                        e.wait_ge(sdv.h, M[("ybf", g)])
                        if ("stats", (s, 3, g - 1)) in M:
                            e.wait_ge(spe.h, M[("stats", (s, 3, g - 1))])
                        e.activation(YSQ[:], YBF[:], AF.Square).then_inc(sac.h, 1)
                    else:
                        self.emit_ln_ac(e, op, M)
            ph.on("scalar", fac)

            def fdv(e, ops=tuple(dv_ops)):
                for op in ops:
                    if op[0] == "applyu":
                        _, g = op
                        e.wait_ge(spe.h, M[("repl", (s, 2, g))])
                        if ("yspill", g - 2) in M:
                            e.wait_ge(self.sgp2[g % 2].h, M[("yspill", g - 2)])
                        for c in range(DC):
                            e.tensor_tensor(tmp[:], OBF[g % 2][:, c * G:(c + 1) * G],
                                            prep[:], ALU.mult)
                            e.tensor_tensor(YG[g % 2][:, c * G:(c + 1) * G], tmp[:],
                                            pbrep[:], ALU.add)
                        e.last.then_inc(sdv.h, 1)
                    elif op[0] == "ybf":
                        _, g = op
                        e.wait_ge(self.sgp2[g % 2].h, M[("xacc", g)])
                        if ("stats", (s, 3, g - 1)) in M:
                            e.wait_ge(spe.h, M[("stats", (s, 3, g - 1))])
                        e.tensor_copy(YBF[:], YG[g % 2][:]).then_inc(sdv.h, 1)
                    elif op[0] == "applyz":
                        _, g = op
                        gs = slice(g * G, (g + 1) * G)
                        e.wait_ge(spe.h, M[("repl", (s, 3, g))])
                        for c in range(DC):
                            e.tensor_tensor(tmp[:], YG[g % 2][:, c * G:(c + 1) * G],
                                            prep[:], ALU.mult)
                            e.tensor_tensor(self.XN[c][:, gs], tmp[:], pbrep[:],
                                            ALU.add)
                        e.last.then_inc(sdv.h, 1)
                    else:
                        self.emit_ln_dv(e, op, M)
            ph.on("vector", fdv)

            def fgp(e, ops=tuple(gp_ops)):
                for op in ops:
                    kind, g, w = op
                    e.wait_ge(sdv.h, w)
                    gs = slice(g * G, (g + 1) * G)
                    for c in range(DC):
                        if kind == "xacc":
                            e.dma_start(out=YG[g % 2][:, c * G:(c + 1) * G],
                                        in_=self.d_xT[s, c][:, gs],
                                        accum_op=ALU.add) \
                                .then_inc(self.sgp2[g % 2].h, 16)
                        else:
                            e.dma_start(out=self.d_xT[s, c][:, gs],
                                        in_=YG[g % 2][:, c * G:(c + 1) * G]) \
                                .then_inc(self.sgp2[g % 2].h, 16)
            ph.on("gpsimd", fgp)
            self.drain(ph, gp=True)
            ph.emit()

        self.tap_chunks(f"zn{s}", self.XN)

    def ffn_w1(self, s):
        nc = self.nc
        sq, sg, spe, sdv, sac = self.sq, self.sg, self.spe, self.sdv, self.sac
        with contextlib.ExitStack() as est:
            E = est.enter_context
            WQT = E(nc.sbuf_tensor(_u("f_wqt1"), [128, DC * DFF], BF16))
            HB = [E(nc.sbuf_tensor(_u(f"f_hb{i}"), [128, FC * G], BF16))
                  for i in range(2)]
            ph1 = [E(nc.psum_tensor(_u(f"f_ph{i}"), [128, G], F32)) for i in range(4)]
            ph = Phase(nc, f"w1_{s}")
            M = {}
            pe_ops, ac_ops, gp_ops = [], [], []

            def fs0(e):
                e.dma_start(out=WQT[:].rearrange("p (c f) -> p c f", f=DFF),
                            in_=self.d_wq1t.rearrange("c p f -> p c f")) \
                    .then_inc(sq.h, 16)
            sq.n += 16
            w_ld = sq.n
            ph.on("sync", fs0)

            for g in range(NG):
                for f in range(FC):
                    spe.n += 1
                    M[("w1c", g, f)] = spe.n
                    pe_ops.append(("w1c", g, f, w_ld if (g == 0 and f == 0) else 0))
                    sac.n += 1
                    M[("hev", g, f)] = sac.n
                    ac_ops.append(("hev", g, f))
                gp_ops.append(("hspill", g, sac.n))
                self.sgp2[g % 2].n += 16
                M[("hspill", g)] = self.sgp2[g % 2].n

            def fpe(e, ops=tuple(pe_ops)):
                for op in ops:
                    _, g, f, w_ld_ = op
                    if w_ld_:
                        e.wait_ge(sq.h, w_ld_)
                    gf = g * FC + f
                    if gf >= 4:
                        pg, pf = divmod(gf - 4, FC)
                        e.wait_ge(sac.h, M[("hev", pg, pf)])
                    gs = slice(g * G, (g + 1) * G)
                    for k in range(DC):
                        e.matmul(ph1[f % 4][:],
                                 WQT[:, k * DFF + f * 128:k * DFF + (f + 1) * 128],
                                 self.XN[k][:, gs], start=(k == 0), stop=(k == DC - 1))
                    e.last.then_inc(spe.h, 1)
            ph.on("tensor", fpe)

            def fac(e, ops=tuple(ac_ops)):
                for op in ops:
                    _, g, f = op
                    e.wait_ge(spe.h, M[("w1c", g, f)])
                    if ("hspill", g - 2) in M and f == 0:
                        e.wait_ge(self.sgp2[g % 2].h, M[("hspill", g - 2)])
                    e.activation(HB[g % 2][:, f * G:(f + 1) * G], ph1[f % 4][:],
                                 AF.Relu, bias=self.b1l[:, f:f + 1]) \
                        .then_inc(sac.h, 1)
            ph.on("scalar", fac)

            def fgp(e, ops=tuple(gp_ops)):
                for op in ops:
                    _, g, w = op
                    e.wait_ge(sac.h, w)
                    gs = slice(g * G, (g + 1) * G)
                    e.dma_start(out=self.d_h[:, :, gs].rearrange("f p j -> p f j"),
                                in_=HB[g % 2][:].rearrange("p (f j) -> p f j", j=G)) \
                        .then_inc(self.sgp2[g % 2].h, 16)
            ph.on("gpsimd", fgp)
            self.drain(ph, gp=True)
            ph.emit()

    def ffn_w2(self, s):
        nc = self.nc
        sq, sg, spe, sdv, sac = self.sq, self.sg, self.spe, self.sdv, self.sac
        with contextlib.ExitStack() as est:
            E = est.enter_context
            WQT = E(nc.sbuf_tensor(_u("f_wqt2"), [128, FC * D], BF16))
            HB = [E(nc.sbuf_tensor(_u(f"f_hb2{i}"), [128, FC * G], BF16))
                  for i in range(2)]
            FBF = E(nc.sbuf_tensor(_u("f_fbf"), [128, DC * G], BF16))
            FSQ = E(nc.sbuf_tensor(_u("f_fsq"), [128, DC * G], BF16))
            FOUT = E(nc.sbuf_tensor(_u("f_fout"), [128, DC * G], F32))
            OUTT = E(nc.sbuf_tensor(_u("f_outt"), [128, D], F32))
            stat = E(nc.sbuf_tensor(_u("f_stat"), [1, 6 * G], F32))
            statb = E(nc.sbuf_tensor(_u("f_statb"), [1, 4 * G], BF16))
            tmp = E(nc.sbuf_tensor(_u("f_tmp"), [128, G], F32))
            pf = [E(nc.psum_tensor(_u(f"f_pf{i}"), [128, G], F32)) for i in range(2)]
            psig = E(nc.psum_tensor(_u("f_psig"), [1, G], F32))
            psig2 = E(nc.psum_tensor(_u("f_psig2"), [1, G], F32))
            prep = E(nc.psum_tensor(_u("f_prep"), [128, G], F32))
            pbrep = E(nc.psum_tensor(_u("f_pbrep"), [128, G], F32))
            ptr = [E(nc.psum_tensor(_u(f"f_ptr{i}"), [128, 128], F32)) for i in range(2)]

            ph = Phase(nc, f"w2_{s}")
            M = {}
            if not hasattr(self, "_dbgM"):
                self._dbgM = {}
            self._dbgM[s] = M
            pe_ops, dv_ops, ac_ops, gp_ops, sy_ops = [], [], [], [], []
            self._last_fdv1 = [0]
            self._last_apply = [0]

            sq.n += 16
            M[("wld",)] = sq.n
            sy_ops.append(("wld",))

            def sched_hload(g):
                self.sqp[g % 2].n += 16
                M[("hld", g)] = self.sqp[g % 2].n
                sy_ops.append(("hld", g))

            def sched_chains(g, mid=None):
                for c in range(DC):
                    spe.n += 1
                    M[("w2c", g, c)] = spe.n
                    pe_ops.append(("w2c", g, c))
                    sac.n += 1
                    M[("fev", g, c)] = sac.n
                    ac_ops.append(("fev", g, c))
                    if c == 1 and mid is not None:
                        mid()

            def sched_stats4(g):
                tag = (s, 4, g)
                src_bf = [FBF[:, c * G:(c + 1) * G] for c in range(DC)]
                src_sq = [FSQ[:, c * G:(c + 1) * G] for c in range(DC)]
                self.ln_stage(M, tag, src_bf, src_sq, DC, self.eps_ln, True,
                              psig, psig2, prep, pbrep, stat, statb,
                              pe_ops, dv_ops, ac_ops,
                              M[("fev", g, DC - 1)], self._last_fdv1[0],
                              self._last_apply[0])
                self._last_fdv1[0] = M[("fdv1", tag)]

            def sched_repl4(g):
                self.ln_stage_repl(M, (s, 4, g), True, statb, prep, pbrep,
                                   pe_ops, self._last_apply[0])

            def sched_applyf(g):
                sdv.n += 1
                M[("applyf", g)] = sdv.n
                dv_ops.append(("applyf", g))
                self._last_apply[0] = sdv.n
                gp_ops.append(("yacc", g, sdv.n))
                self.sgp2[g % 2].n += 16 * DC
                M[("yacc", g)] = self.sgp2[g % 2].n

            def sched_egress(g):
                for st in range(4):
                    for c in range(DC):
                        spe.n += 1
                        M[("etr", g, st, c)] = spe.n
                        pe_ops.append(("etr", g, st, c))
                        sdv.n += 1
                        M[("eev", g, st, c)] = sdv.n
                        dv_ops.append(("eev", g, st, c))
                    self.sst.n += 16
                    M[("store", g, st)] = self.sst.n
                    sy_ops.append(("store", g, st))

            sched_hload(0)
            sched_chains(0)
            sched_hload(1)
            sched_stats4(0)
            sched_chains(1, mid=lambda: sched_repl4(0))
            sched_applyf(0)
            sched_hload(2)
            sched_stats4(1)
            sched_chains(2, mid=lambda: (sched_repl4(1), sched_egress(0)))
            sched_applyf(1)
            sched_hload(3)
            sched_stats4(2)
            sched_chains(3, mid=lambda: (sched_repl4(2), sched_egress(1)))
            sched_applyf(2)
            sched_stats4(3)
            sched_repl4(3)
            sched_egress(2)
            sched_applyf(3)
            sched_egress(3)

            def fpe(e, ops=tuple(pe_ops)):
                for op in ops:
                    if op[0] == "w2c":
                        _, g, c = op
                        if g == 0 and c == 0:
                            e.wait_ge(sq.h, M[("wld",)])
                        e.wait_ge(self.sqp[g % 2].h, M[("hld", g)])
                        gc = g * DC + c
                        if gc >= 2:
                            pg, pc = divmod(gc - 2, DC)
                            e.wait_ge(sac.h, M[("fev", pg, pc)])
                        for f in range(FC):
                            e.matmul(pf[c % 2][:],
                                     WQT[:, f * D + c * 128:f * D + (c + 1) * 128],
                                     HB[g % 2][:, f * G:(f + 1) * G],
                                     start=(f == 0), stop=(f == FC - 1))
                        e.last.then_inc(spe.h, 1)
                    elif op[0] == "etr":
                        _, g, st, c = op
                        k = (g * 4 + st) * DC + c
                        if st == 0 and c == 0:
                            e.wait_ge(self.sgp2[g % 2].h, M[("yacc", g)])
                        if k >= 2:
                            pg2 = k - 2
                            gg, rem = divmod(pg2, 4 * DC)
                            pst_, pc = divmod(rem, DC)
                            e.wait_ge(sdv.h, M[("eev", gg, pst_, pc)])
                        e.transpose(ptr[k % 2][:],
                                    FOUT[:, c * G + st * 128:c * G + (st + 1) * 128],
                                    self.identf[:]).then_inc(spe.h, 1)
                    else:
                        self.emit_ln_pe(e, op, M)
            ph.on("tensor", fpe)

            def fac(e, ops=tuple(ac_ops)):
                for op in ops:
                    if op[0] == "fev":
                        _, g, c = op
                        e.wait_ge(spe.h, M[("w2c", g, c)])
                        if g >= 1 and c == 0:
                            e.wait_ge(sdv.h, M[("applyf", g - 1)])
                        e.activation(FBF[:, c * G:(c + 1) * G], pf[c % 2][:],
                                     AF.Identity, bias=self.b2l[:, c:c + 1])
                        e.activation(FSQ[:, c * G:(c + 1) * G], pf[c % 2][:],
                                     AF.Square, bias=self.b2l[:, c:c + 1]) \
                            .then_inc(sac.h, 1)
                    else:
                        self.emit_ln_ac(e, op, M)
            ph.on("scalar", fac)

            def fdv(e, ops=tuple(dv_ops)):
                for op in ops:
                    if op[0] == "applyf":
                        _, g = op
                        e.wait_ge(spe.h, M[("repl", (s, 4, g))])
                        for c in range(DC):
                            e.tensor_tensor(tmp[:], FBF[:, c * G:(c + 1) * G],
                                            prep[:], ALU.mult)
                            e.tensor_tensor(FOUT[:, c * G:(c + 1) * G], tmp[:],
                                            pbrep[:], ALU.add)
                        e.last.then_inc(sdv.h, 1)
                    elif op[0] == "eev":
                        _, g, st, c = op
                        e.wait_ge(spe.h, M[("etr", g, st, c)])
                        if c == 0:
                            prev = M.get(("store", g, st - 1)) if st >= 1 \
                                else M.get(("store", g - 1, 3), 0)
                            if prev:
                                e.wait_ge(self.sst.h, prev)
                        e.tensor_copy(OUTT[:, c * 128:(c + 1) * 128],
                                      ptr[((g * 4 + st) * DC + c) % 2][:]) \
                            .then_inc(sdv.h, 1)
                    else:
                        self.emit_ln_dv(e, op, M)
            ph.on("vector", fdv)

            def fgp(e, ops=tuple(gp_ops)):
                for op in ops:
                    _, g, w = op
                    e.wait_ge(sdv.h, w)
                    gs = slice(g * G, (g + 1) * G)
                    for c in range(DC):
                        e.dma_start(out=FOUT[:, c * G:(c + 1) * G],
                                    in_=self.d_xT[s, c][:, gs],
                                    accum_op=ALU.add) \
                            .then_inc(self.sgp2[g % 2].h, 16)
            ph.on("gpsimd", fgp)

            def fsy(e, ops=tuple(sy_ops)):
                for op in ops:
                    if op[0] == "wld":
                        e.dma_start(out=WQT[:].rearrange("p (c f) -> p c f", f=D),
                                    in_=self.d_wq2t.rearrange("c p f -> p c f")) \
                            .then_inc(sq.h, 16)
                    elif op[0] == "hld":
                        _, g = op
                        if g >= 2:
                            e.wait_ge(spe.h, M[("w2c", g - 2, DC - 1)])
                        gs = slice(g * G, (g + 1) * G)
                        e.dma_start(out=HB[g % 2][:]
                                    .rearrange("p (f j) -> p f j", j=G),
                                    in_=self.d_h[:, :, gs]
                                    .rearrange("f p j -> p f j")) \
                            .then_inc(self.sqp[g % 2].h, 16)
                    elif op[0] == "store":
                        _, g, st = op
                        for c in range(DC):
                            e.wait_ge(sdv.h, M[("eev", g, st, c)])
                        e.dma_start(out=self.d_out[s, g * G + st * 128:
                                                   g * G + (st + 1) * 128, :],
                                    in_=OUTT[:]).then_inc(self.sst.h, 16)
            ph.on("sync", fsy)
            self.drain(ph, st=True, gp=True)
            ph.emit()

    def finale(self):
        ph = Phase(self.nc, "fin")
        sst = self.sst

        def fs(e):
            e.wait_ge(sst.h, sst.n)
        ph.on("sync", fs)
        ph.emit()

    def build(self, upto=None):
        nc = self.nc
        self.p0()
        if upto != "p0":
            do_q = upto not in ("ingest", "ln1", "proj", "attn", "wo")
            if do_q:
                self._qes = contextlib.ExitStack()
                self.quant_init(self._qes)
            for s in range(S):
                with contextlib.ExitStack() as xbst:
                    self.XN = [xbst.enter_context(
                        nc.sbuf_tensor(_u(f"XN{c}"), [128, T], BF16))
                        for c in range(DC)]
                    self.XB = [xbst.enter_context(
                        nc.sbuf_tensor(_u(f"XB{c}"), [128, T], BF16))
                        for c in range(DC)]
                    with contextlib.ExitStack() as sst_:
                        XT = [sst_.enter_context(
                            nc.sbuf_tensor(_u(f"XT{c}"), [128, T], F32))
                            for c in range(DC)]
                        self.ingest_ln1(s, XT, do_ln=(upto != "ingest"))
                    if upto in ("ln1", "ingest"):
                        continue
                    with contextlib.ExitStack() as pst:
                        E = pst.enter_context
                        QCb = [E(nc.sbuf_tensor(_u(f"QC{i}"), [128, T], BF16))
                               for i in range(QC)]
                        KR = [E(nc.sbuf_tensor(_u(f"KR{i}"), [128, T], BF16))
                              for i in range(4)]
                        VT = E(nc.sbuf_tensor(_u("VTt"), [128, NT * H * 65], BF16))
                        self.projections(s, pst, QCb, KR, VT)
                        if upto == "proj":
                            continue
                        with contextlib.ExitStack() as ast:
                            self.attention(s, ast, QCb, KR, VT)
                    if upto in ("attn", "ln1", "ingest", "proj"):
                        continue
                    self.wo_ln2_ln3(s)
                    if upto == "wo":
                        continue
                    self.ffn_w1(s)
                if upto in ("attn", "ln1", "ingest", "proj", "wo"):
                    continue
                self.ffn_w2(s)
        self.finale()
        return nc


def _host_consts():
    inv_freq = 1.0 / 10000.0 ** (np.arange(0, RD, 2, dtype=np.float32) / RD)
    t = np.arange(T, dtype=np.float32)
    freqs = np.outer(t, inv_freq)
    emb = np.concatenate([freqs, freqs], -1)
    cos = np.cos(emb).T.astype(np.float32)
    sin = np.sin(emb).T.astype(np.float32)
    ssin = np.concatenate([-sin[:RD // 2], sin[RD // 2:]], 0)
    ssin = ssin[(np.arange(RD) + RD // 2) % RD]
    cos128 = np.tile(cos, (4, 1)).astype(ml_dtypes.bfloat16)
    ssin128 = np.tile(ssin, (4, 1)).astype(ml_dtypes.bfloat16)
    ku = np.arange(128)[:, None]
    uu = np.arange(896)[None, :]
    maskw = np.where(uu < ku + 384, np.float32(MASKNEG), np.float32(0.0))
    return {
        "identf": np.eye(128, dtype=np.float32),
        "identb": np.eye(128, dtype=ml_dtypes.bfloat16),
        "cosT": np.ascontiguousarray(cos128),
        "ssinT": np.ascontiguousarray(ssin128),
        "maskw": maskw.astype(ml_dtypes.bfloat16),
    }


def _host_inputs(L0_attn_x, L0_attn_W_kv, L0_attn_g_kv, L0_attn_W_kdec,
                 L0_attn_W_vdec, L0_attn_W_q, L0_attn_W_qdec, L0_attn_W_krope,
                 L0_attn_W_qrope, L0_attn_W_o, L0_ffn_W1, L0_ffn_b1, L0_ffn_W2,
                 L0_ffn_b2):
    x = np.ascontiguousarray(np.asarray(L0_attn_x, dtype=np.float32))
    f32 = lambda a: np.ascontiguousarray(np.asarray(a, np.float32))
    w = {
        "wkv": f32(L0_attn_W_kv),
        "gkv": np.ascontiguousarray(f32(L0_attn_g_kv).reshape(KVC, 128).T),
        "wkdec": f32(L0_attn_W_kdec),
        "wvdec": f32(L0_attn_W_vdec),
        "wq": f32(L0_attn_W_q),
        "wqdec": f32(L0_attn_W_qdec),
        "wkrope": f32(L0_attn_W_krope),
        "wqrope": f32(L0_attn_W_qrope),
        "wo": f32(L0_attn_W_o),
        "w1": f32(L0_ffn_W1),
        "b1l": np.ascontiguousarray(f32(L0_ffn_b1).reshape(FC, 128).T),
        "w2": f32(L0_ffn_W2),
        "b2l": np.ascontiguousarray(f32(L0_ffn_b2).reshape(DC, 128).T),
    }
    w.update(_host_consts())
    in_maps = []
    for c in range(8):
        m = dict(w)
        m["x"] = np.ascontiguousarray(x[2 * c:2 * c + 2])
        in_maps.append(m)
    return in_maps


def kernel(**inputs):
    in_maps = _host_inputs(**inputs)
    mk = MK(_TAPS)
    nc = mk.build()
    res = run_bass_kernel_spmd(nc, in_maps, core_ids=list(range(8)))
    mk.es.close()
    out = np.stack([res.results[c]["out"] for c in range(8)], 0)
    return out.reshape(16, T, D).astype(np.float32)
